# version 03
# baseline (speedup 1.0000x reference)
import numpy as np
import ml_dtypes
import concourse.bass as bass
import concourse.mybir as mybir
from concourse.bass_utils import run_bass_kernel_spmd

F32 = mybir.dt.float32
BF16 = mybir.dt.bfloat16
I32 = mybir.dt.int32
AF = mybir.ActivationFunctionType
ALU = mybir.AluOpType
AX = mybir.AxisListType

B, L, C = 4, 4096, 1024
DI, NST, KC, RNK = 256, 16, 4, 8
NHPC = 4
HID = 3072
TC1 = 1024
NCH = L // TC1
THALF = L // 2
EPS = 1e-5
QM15 = 32767.0
RC = 12582912.0

RG8 = [[0, 1, 2, 3, 4, 5, 6, 7]]
DEBUG = False
CUT = None
P1HCK = 16
P1U = 96
P1SCAN = True
P1PART = 9
P1VAR = 0
P1SKIP = False
P0PART = 3
P0MASK = 15

FA_COLS = 21016
BF_COLS = 47616

P0_XC = 0
P0_SQ = 8192
P0_RW = 9216
P0_LR = 10240
P0_WST = 12352
P0_HH = 15616

P1_U = 0
P1_UC = 2054
P1_SZ = 4102
P1_DL = 6150
P1_DB = 8198
P1_T0 = 10246
P1_T1 = 10758
P1_SS = 11270
P1_DA = 12294
P1_EP = 14342
P1_F0 = 15366

P2_X = 0
P2_S = 4096
P2_X1 = 8192
P2_G = 12288
P2_LW = 12800
P2_LS = 15872

P3_X1 = 0
P3_XN = 4104
P3_K = 8208
P3_V = 10256
P3_SR = 12304
P3_AB = 14352
P3_BB = 16404
P3_Y = 18456
P3_LR = 18968
P3_RW = 19992

P4_X1 = 0
P4_XN = 4096
P4_LR = 8192
P4_RW = 9216
P4_SG = 10240
P4_S1 = 10752
P4_G1 = 11776
P4_SC = 15872
P4_CF = 19968

B1_XNH = 0
B1_XNT = 16384
B1_DU = 32768
B1_X = 34816
B1_H = 35840
B1_PR = 36864
B1_BR = 40960
B1_CR = 43008
B1_YG = 45056
B1_HF = 47104
B2_WD = 0
B2_HF = 24576
B3_XK = 0
B3_XV = 4096
B3_XR = 8192
B3_WK = 12288
B3_WV = 16384
B3_WR = 20480
B3_WO = 24576
B3_Y1 = 28672
B3_ST = 30720
B4_WQ = 0
B4_XQ = 24576
B4_OQ = 24576
B4_GU = 40960
B4_ST = 43008


def _f(x):
    return np.ascontiguousarray(np.asarray(x), dtype=np.float32)


def make_in_maps(inp):
    eye = np.ascontiguousarray(np.eye(128, dtype=ml_dtypes.bfloat16))
    ones = np.ones((128, 1), np.float32)
    maps = []
    for c in range(8):
        b, g = c % 4, c // 4
        mh = slice(4 * g, 4 * g + 4)
        ch = slice(512 * g, 512 * g + 512)
        m = {"identb": eye, "onesf": ones}
        m["xbT"] = _f(np.asarray(inp["L0_attn_x"])[b].T)
        w = np.asarray(inp["L0_attn_in_proj_w"])[mh]
        m["ipwT"] = _f(np.transpose(w, (2, 0, 1)).reshape(128, 2048))
        cw = np.asarray(inp["L0_attn_conv_w"])[mh].reshape(4, 2, 128, KC)
        m["convw"] = _f(np.transpose(cw, (2, 0, 1, 3)).reshape(128, 32))
        cb = np.asarray(inp["L0_attn_conv_b"])[mh].reshape(4, 2, 128)
        m["convb"] = _f(np.transpose(cb, (2, 0, 1)).reshape(128, 8))
        xp = np.asarray(inp["L0_attn_x_proj_w"])[mh]
        xpT = np.transpose(xp, (2, 0, 1)).reshape(2, 128, 4, 40)
        m["xpwT"] = _f(np.transpose(xpT, (1, 2, 0, 3)).reshape(128, 320))
        dw = np.asarray(inp["L0_attn_dt_proj_w"])[mh].reshape(4, 2, 128, RNK)
        m["dtwT"] = _f(np.transpose(dw, (3, 0, 1, 2)).reshape(RNK, 1024))
        db = np.asarray(inp["L0_attn_dt_proj_b"])[mh].reshape(4, 2, 128)
        m["dtb"] = _f(np.transpose(db, (2, 0, 1)).reshape(128, 8))
        al = np.asarray(inp["L0_attn_A_log"])[mh].reshape(4, 2, 128, NST)
        m["alog"] = _f(np.transpose(al, (2, 0, 1, 3)).reshape(128, 128))
        dd = np.asarray(inp["L0_attn_D"])[mh].reshape(4, 2, 128)
        m["ddd"] = _f(np.transpose(dd, (2, 0, 1)).reshape(128, 8))
        ow = np.asarray(inp["L0_attn_out_proj_w"])[mh]
        owT = np.transpose(ow, (2, 0, 1)).reshape(2, 128, 4, 128)
        m["opwT"] = _f(np.transpose(owT, (1, 2, 0, 3)).reshape(128, 1024))
        wg = np.asarray(inp["L0_ffn_w_gate"]).reshape(24, 128, 8, 128)
        m["wg0"] = np.ascontiguousarray(
            np.transpose(wg, (0, 3, 2, 1)), dtype=np.int32).reshape(24, 128, 1024)
        wu = np.asarray(inp["L0_ffn_w_up"]).reshape(24, 128, 8, 128)
        m["wu0"] = np.ascontiguousarray(
            np.transpose(wu, (0, 3, 2, 1)), dtype=np.int32).reshape(24, 128, 1024)
        wd = np.asarray(inp["L0_ffn_w_down"]).reshape(8, 128, 24, 128)
        m["wd0"] = np.ascontiguousarray(
            np.transpose(wd, (2, 3, 0, 1)), dtype=np.int32).reshape(24, 128, 1024)
        m["wkT"] = _f(np.asarray(inp["L1_attn_Wk"]).T[:, ch])
        m["wvT"] = _f(np.asarray(inp["L1_attn_Wv"]).T[:, ch])
        m["wrT"] = _f(np.asarray(inp["L1_attn_Wr"]).T[:, ch])
        m["woT"] = _f(np.asarray(inp["L1_attn_Wo"]).T[ch, :])
        for nm, key in [("tdec", "L1_attn_time_decay"), ("tfir", "L1_attn_time_first"),
                        ("tmk", "L1_attn_time_mix_k"), ("tmv", "L1_attn_time_mix_v"),
                        ("tmr", "L1_attn_time_mix_r")]:
            m[nm] = _f(np.asarray(inp[key]).reshape(8, 128).T)
        m["wg1"] = _f(inp["L1_ffn_w_gate"])
        m["wu1"] = _f(inp["L1_ffn_w_up"])
        m["wd1"] = _f(inp["L1_ffn_w_down"])
        maps.append(m)
    return maps



class Ch:
    __slots__ = ("name", "n")

    def __init__(self, name):
        self.name = name
        self.n = 0


class Em:
    def __init__(self, nc, ename, eng, chans, sems):
        self.nc = nc
        self.ename = ename
        self.eng = eng
        self.chans = chans
        self.sems = sems
        self.lastw = {}
        self._pid = None

    def pid(self):
        if self._pid is None:
            self._pid = self.eng.partition_id()
        return self._pid

    def _wait(self, waits):
        for w in waits:
            if isinstance(w, tuple):
                name, val = w
            else:
                name, val = w, self.chans[w].n
            if val > 0 and self.lastw.get(name, -1) < val:
                self.eng.wait_ge(self.sems[name], val)
                self.lastw[name] = val

    def val(self, chname):
        return self.chans[chname].n

    def op(self, ename, fn, waits=(), inc=None, k=1):
        if ename == self.ename:
            self._wait(waits)
            ins = fn(self.eng)
            if inc is not None and ins is not None:
                ins.then_inc(self.sems[inc], k)
        if inc is not None:
            self.chans[inc].n += k

    def dma(self, ename, out, in_, waits=(), inc=None, transpose=False):
        if transpose:
            self.op(ename, lambda e: e.dma_start(out=out, in_=in_, transpose=True),
                    waits, inc, 16)
        else:
            self.op(ename, lambda e: e.dma_start(out=out, in_=in_), waits, inc, 16)

    def switch(self, ename, body, waits=(), inc=None, nincs=1, kper=16):
        if ename == self.ename:
            self._wait(waits)
            idx = self.pid()
            sem = self.sems[inc] if inc is not None else None
            for i in self.nc.Switch(engines=[self.eng], index=[idx], n=8):
                def then(instr, sem=sem):
                    if sem is not None:
                        instr.then_inc(sem, kper)
                body(self.eng, i, then)
        if inc is not None:
            self.chans[inc].n += kper * nincs

    def coll(self, kind, op, groups, ins, outs, waits=(), inc=None):
        self.op("gpsimd",
                lambda e: e.collective_compute(kind, op, replica_groups=groups,
                                               ins=ins, outs=outs),
                waits, inc, 1)


CHANNELS = [
    "w_ld", "x_ld", "x_ldb", "loc_wr", "pm_s", "pm_v",
    "st_t", "st_v", "st_s", "row_wr", "rep0", "rep0b",
    "pre_t", "pre_s", "pre_v", "brc_wr",
    "sc_s", "sc_v", "sc_t", "rep_ld", "rep_ldb",
    "epi_v", "epi_t", "epi_s", "s_wr",
    "f0_ld", "f0_t", "f0_s", "f0_v", "hf_wr",
    "cc", "g_t", "g_s", "g_v", "hf_ld",
    "c_ld", "c_v", "c_wr", "q2_v", "q2_s", "q2_ld", "q2_wr",
    "p3_ld", "p3_t", "p3_v", "p3_s", "p3_wr", "row3",
    "q_ld", "q_gt0", "q_gt1", "q_t", "q_v", "q_s", "q_wr", "row4",
    "out_wr", "dbg_wr",
]


def build_program():
    nc = bass.Bass("TRN2", target_bir_lowering=False, detect_race_conditions=False)

    din = {}
    for name, shape, dt in [
        ("xbT", [C, L], F32), ("ipwT", [128, 2048], F32), ("convw", [128, 32], F32),
        ("convb", [128, 8], F32), ("xpwT", [128, 320], F32), ("dtwT", [RNK, 1024], F32),
        ("dtb", [128, 8], F32), ("alog", [128, 128], F32), ("ddd", [128, 8], F32),
        ("opwT", [128, 1024], F32),
        ("wg0", [24, 128, 1024], I32), ("wu0", [24, 128, 1024], I32),
        ("wd0", [24, 128, 1024], I32),
        ("wkT", [C, 512], F32), ("wvT", [C, 512], F32), ("wrT", [C, 512], F32),
        ("woT", [512, C], F32),
        ("tdec", [128, 8], F32), ("tfir", [128, 8], F32), ("tmk", [128, 8], F32),
        ("tmv", [128, 8], F32), ("tmr", [128, 8], F32),
        ("wg1", [HID, C], F32), ("wu1", [HID, C], F32), ("wd1", [C, HID], F32),
        ("identb", [128, 128], BF16), ("onesf", [128, 1], F32),
    ]:
        din[name] = nc.dram_tensor(name, shape, dt, kind="ExternalInput")
    dout = nc.dram_tensor("out", [C, THALF], F32, kind="ExternalOutput")

    dsc = {}
    for name, shape, dt in [
        ("brc", [NHPC, 2 * NST, L], BF16),
        ("lnrow", [2, L], F32), ("lnrow3", [2, L], F32), ("lnrow4", [2, THALF], F32),
        ("hfd", [24, 128, THALF], BF16),
        ("gd", [8, 128, THALF], F32),
        ("wqg", [HID, C], BF16), ("wqu", [HID, C], BF16), ("wqd", [C, HID], BF16),
        ("upd", [24, 128, THALF], BF16), ("gtd", [24, 128, THALF], BF16),
        ("pmax", [4, 128], F32),
        ("sagi", [512, L], F32), ("sago", [8 * 512, L], F32),
        ("x1agi", [C, THALF], F32), ("x1ago", [8 * C, THALF], F32),
        ("s1agi", [C, L], BF16), ("s1ago", [8 * C, L], BF16),
        ("ar1i", [1, 2], F32), ("ar1o", [1, 2], F32),
        ("ar2i", [1, 2], F32), ("ar2o", [1, 2], F32),
        ("ar3i", [1, 2], F32), ("ar3o", [1, 2], F32),
        ("ar4i", [1, 2], F32), ("ar4o", [1, 2], F32),
        ("dbg", [8, 128, 2048], F32) if DEBUG else ("dbg", [1, 1, 1], F32),
        ("dbgb", [8, 128, 2048], BF16) if DEBUG else ("dbgb", [1, 1, 1], BF16),
        ("xhh", [512, L], F32), ("xth", [C, THALF], F32),
        ("lnrowth", [2, THALF], F32),
        ("sloc", [C, THALF], F32), ("x1loc", [C, L], F32),
        ("s1loc", [2 * C, THALF], BF16),
    ]:
        dbg_fetch = {"dbg", "dbgb", "lnrow", "brc", "hfd", "gd",
                     "wqg", "upd", "lnrow3"}
        kind = "ExternalOutput" if (DEBUG and name in dbg_fetch) else "Internal"
        dsc[name] = nc.dram_tensor(name, shape, dt, kind=kind)
    if DEBUG:
        for name, src in [("dsagi", [512, L]), ("dx1", [C, THALF]),
                          ("dar", [1, 8])]:
            dsc[name] = nc.dram_tensor(name, src, F32, kind="ExternalOutput")
        dsc["ds1"] = nc.dram_tensor("ds1", [C, L], BF16, kind="ExternalOutput")

    small_specs = [
        ("ipw", [128, 2048], BF16), ("opw", [128, 1024], BF16),
        ("xpw", [128, 320], F32), ("dtw", [RNK, 1024], F32),
        ("convw", [128, 32], F32), ("convb", [128, 8], F32),
        ("dtb", [128, 8], F32), ("acoef", [128, 128], F32), ("ddd", [128, 8], F32),
        ("tdec", [128, 8], F32), ("tfir", [128, 8], F32), ("tmk", [128, 8], F32),
        ("tmv", [128, 8], F32), ("tmr", [128, 8], F32),
        ("wdec", [128, 8], F32), ("ewc", [128, 8], F32),
        ("identb", [128, 128], BF16), ("onesf", [128, 1], F32),
        ("carry", [128, 32], F32),
        ("pcarry", [128, 12], F32),
        ("wkvp", [128, 8], F32),
        ("xhalo", [128, 16], F32),
        ("w0slab", [128, 1024], I32), ("w0bf", [128, 2048], BF16),
        ("i0", [128, 512], I32), ("i1", [128, 512], I32), ("i2", [128, 512], I32),
        ("mxi", [128, 2], I32),
        ("colz", [128, 16], F32),
        ("mxc", [128, 8], F32),
        ("rowz", [1, 128], F32),
        ("bcst", [128, 1024], BF16),
        ("cvals", [128, 4], F32),
        ("jnk", [128, 96], F32),
    ]

    with nc.cleanup_on_exit():
        ctxs = []
        sb = {}
        for name, shape, dt in small_specs:
            t = nc.sbuf_tensor("sb_" + name, shape, dt)
            sb[name] = t.__enter__()
            ctxs.append(t)
        t = nc.sbuf_tensor("FA", [128, FA_COLS], F32)
        sb["FA"] = t.__enter__()
        ctxs.append(t)
        t = nc.sbuf_tensor("BFA", [128, BF_COLS], BF16)
        sb["BFA"] = t.__enter__()
        ctxs.append(t)

        nc.const_aps.register(F32, EPS, sb["cvals"][:, 0:1])
        nc.const_aps.register(F32, -0.5, sb["cvals"][:, 1:2])
        nc.const_aps.register(F32, 1.0 / 127.0, sb["cvals"][:, 2:3])

        ps = {}
        for name, shape in [("pa", [128, 512]), ("pb", [128, 512]),
                            ("pc", [128, 512]), ("pd", [128, 512]),
                            ("py0", [128, 1024]), ("py1", [128, 1024])]:
            t = nc.psum_tensor(name, shape, F32)
            ps[name] = t.__enter__()
            ctxs.append(t)

        sems = {}
        for name in CHANNELS:
            t = nc.semaphore(name)
            sems[name] = t.__enter__()
            ctxs.append(t)

        blk = nc.Block()
        blk.__enter__()
        chans = {}

        def mk(ename, eng):
            for cn in CHANNELS:
                chans[cn] = Ch(cn)
            em = Em(nc, ename, eng, chans, sems)
            program(em, nc, din, dout, dsc, sb, ps)

        @blk.sync
        def _(e):
            mk("sync", e)

        @blk.tensor
        def _(e):
            mk("tensor", e)

        @blk.scalar
        def _(e):
            mk("scalar", e)

        @blk.vector
        def _(e):
            mk("vector", e)

        @blk.gpsimd
        def _(e):
            mk("gpsimd", e)

        blk.__exit__(None, None, None)
        for t in reversed(ctxs):
            t.__exit__(None, None, None)
    return nc



def program(em, nc, din, dout, dsc, sb, ps):
    FA = sb["FA"]
    BFA = sb["BFA"]

    def fa(c0, n):
        return FA[:, c0:c0 + n]

    def bfa(c0, n):
        return BFA[:, c0:c0 + n]

    ctx = dict(em=em, nc=nc, din=din, dout=dout, dsc=dsc, sb=sb, ps=ps, fa=fa, bfa=bfa)
    phases = [phase0, phase1, phase2, phase3, phase4]
    upto = len(phases) if CUT is None else CUT + 1
    for ph in phases[:upto]:
        ph(**ctx)
    if DEBUG and upto == len(phases):
        em.dma("gpsimd", dsc["dsagi"].ap(), dsc["sagi"].ap(),
               waits=["s_wr"], inc="dbg_wr")
        em.dma("gpsimd", dsc["dx1"].ap(), dsc["x1agi"].ap(),
               waits=["c_wr"], inc="dbg_wr")
        em.dma("gpsimd", dsc["ds1"].ap(), dsc["s1agi"].ap(),
               waits=["p3_wr"], inc="dbg_wr")
        em.dma("gpsimd", dsc["dar"].ap()[0:1, 0:2], dsc["ar1o"].ap(),
               waits=["cc"], inc="dbg_wr")
        em.dma("gpsimd", dsc["dar"].ap()[0:1, 2:4], dsc["ar2o"].ap(),
               inc="dbg_wr")
        em.op("gpsimd", lambda e: e.nop(), waits=["dbg_wr"], inc=None)


def vp(em, sb, ap, ch, p, c):
    em.op("vector", lambda e, ap=ap, p=p, c=c: e.tensor_copy(
        sb["jnk"][0:p, 0:c], ap), inc=ch)


def ln_rows(em, sb, fa, psrow, rw0, chS, chV, chT, extraw=()):
    rrow = fa(rw0, 512)[0:1, :]
    mrow = fa(rw0 + 512, 512)[0:1, :]
    em.op("vector", lambda e: e.tensor_scalar(mrow, psrow[0:1, :], 1.0 / C, None,
                                              ALU.mult),
          waits=[chT] + list(extraw), inc=chV)
    em.op("vector", lambda e: e.scalar_tensor_tensor(rrow, mrow, -1.0, mrow,
                                                     ALU.mult, ALU.mult), inc=chV)
    em.op("vector", lambda e: e.scalar_tensor_tensor(rrow, psrow[32:33, :], 1.0 / C,
                                                     rrow, ALU.mult, ALU.add),
          inc=chV)
    em.op("scalar", lambda e: e.activation(rrow, rrow, AF.Ln, bias=EPS),
          waits=[chV], inc=chS)
    em.op("scalar", lambda e: e.activation(rrow, rrow, AF.Exp, scale=-0.5), inc=chS)
    em.op("vector", lambda e: e.scalar_tensor_tensor(mrow, mrow, -1.0, rrow,
                                                     ALU.mult, ALU.mult),
          waits=[chS], inc=chV)


def phase0(em, nc, din, dout, dsc, sb, ps, fa, bfa):
    em.op("vector", lambda e: e.memset(sb["cvals"][:, 0:1], EPS), inc="pm_v")
    em.op("vector", lambda e: e.memset(sb["cvals"][:, 1:2], -0.5), inc="pm_v")
    em.op("vector", lambda e: e.memset(sb["cvals"][:, 2:3], 1.0 / 127.0), inc="pm_v")
    if P0MASK & 1:
        for nm in ["convw", "convb", "dtb", "ddd", "tdec", "tfir", "tmk", "tmv",
                   "tmr", "identb", "onesf"]:
            em.dma("sync", sb[nm][:, :], din[nm].ap(), inc="w_ld")
        em.dma("sync", sb["xpw"][:, :], din["xpwT"].ap(), inc="w_ld")
        em.dma("sync", sb["dtw"][:, :], din["dtwT"].ap(), inc="w_ld")
        em.dma("sync", fa(P0_WST, 2048), din["ipwT"].ap(), inc="w_ld")
        em.dma("sync", fa(P0_WST + 2048, 1024), din["opwT"].ap(), inc="w_ld")
        em.dma("sync", fa(P0_WST + 3072, 128), din["alog"].ap(), inc="w_ld")
    if P0MASK & 2:
        em.op("scalar", lambda e: e.activation(sb["ipw"][:, :], fa(P0_WST, 2048),
                                               AF.Copy),
              waits=["w_ld"], inc="pm_s")
        em.op("scalar", lambda e: e.activation(sb["opw"][:, :], fa(P0_WST + 2048, 1024),
                                               AF.Copy), inc="pm_s")
    if P0MASK & 4:
        em.op("scalar", lambda e: e.activation(sb["acoef"][:, :], fa(P0_WST + 3072, 128),
                                               AF.Exp),
              waits=["w_ld"], inc="pm_s")
        em.op("vector", lambda e: e.tensor_scalar(sb["acoef"][:, :], sb["acoef"][:, :],
                                                  -1.0, None, ALU.mult),
              waits=["pm_s"], inc="pm_v")
        em.op("scalar", lambda e: e.activation(sb["wdec"][:, :], sb["tdec"][:, :],
                                               AF.Exp),
              inc="pm_s")
        em.op("vector", lambda e: e.tensor_scalar(sb["wdec"][:, :], sb["wdec"][:, :],
                                                  -1.0, None, ALU.mult),
              waits=["pm_s"], inc="pm_v")
        em.op("scalar", lambda e: e.activation(sb["ewc"][:, :], sb["wdec"][:, :],
                                               AF.Exp),
              waits=["pm_v"], inc="pm_s")
    if P0MASK & 8:
        em.op("vector", lambda e: e.memset(sb["carry"][:, :], 0.0), inc="pm_v")
        em.op("vector", lambda e: e.memset(sb["mxc"][:, :], 0.0), inc="pm_v")
        em.op("vector", lambda e: e.memset(sb["colz"][:, :], 0.0), inc="pm_v")
        em.op("vector", lambda e: e.memset(sb["xhalo"][:, :], 0.0), inc="pm_v")
    if P0PART < 1:
        return

    def pick(e, core, then):
        g = core // 4
        for ct in range(4):
            e.tensor_copy(sb["wkvp"][:, ct:ct + 1], sb["tfir"][:, 4 * g + ct:4 * g + ct + 1])
        for ct in range(4):
            ins = e.tensor_copy(sb["wkvp"][:, 4 + ct:5 + ct],
                                sb["ewc"][:, 4 * g + ct:4 * g + ct + 1])
        then(ins)

    em.switch("vector", pick, waits=["pm_s"], inc="pm_v", nincs=1, kper=1)
    vp(em, sb, sb["wkvp"][:, 0:8], "pm_v", 128, 8)

    def locbody(e, core, then):
        g = core // 4
        then(e.dma_start(out=dsc["xhh"].ap(),
                         in_=din["xbT"].ap()[g * 512:(g + 1) * 512, :]))
        then(e.dma_start(out=dsc["xth"].ap(),
                         in_=din["xbT"].ap()[:, g * THALF:(g + 1) * THALF]))

    em.switch("gpsimd", locbody, waits=(), inc="loc_wr", nincs=2)

    if P0PART < 2:
        return
    hist_v = {}
    for tc in range(8):
        xch = "x_ld" if tc % 2 == 0 else "x_ldb"
        xoff = P0_XC + (tc % 2) * 4096
        for i in range(8):
            dst = fa(xoff + i * 512, 512)
            em.dma("sync", dst,
                   din["xbT"].ap()[i * 128:(i + 1) * 128, tc * 512:(tc + 1) * 512],
                   waits=[("st_v", hist_v.get(tc - 2, 0))], inc=xch)
        hh0 = P0_HH + (tc % 2) * 2048
        for j in range(4):
            em.dma("sync", fa(hh0 + j * 512, 512),
                   dsc["xhh"].ap()[j * 128:(j + 1) * 128, tc * 512:(tc + 1) * 512],
                   waits=[("loc_wr", 32), ("st_v", hist_v.get(tc - 2, 0))], inc=xch)
        sqt = {}
        for i in range(8):
            dst = fa(xoff + i * 512, 512)
            sq = fa(P0_SQ + (i % 2) * 512, 512)
            em.op("scalar", lambda e, dst=dst, sq=sq: e.activation(sq, dst, AF.Square),
                  waits=[xch, ("st_t", sqt.get(i - 2, 0))], inc="st_s")
            em.op("tensor", lambda e, dst=dst, i=i: e.matmul(
                ps["pa"][0:1, :], sb["onesf"][:, :], dst, start=(i == 0), stop=(i == 7)),
                waits=[xch, "w_ld", ("st_v", em.val("st_v"))], inc="st_t")
            em.op("tensor", lambda e, sq=sq, i=i: e.matmul(
                ps["pa"][32:33, :], sb["onesf"][:, :], sq, start=(i == 0), stop=(i == 7)),
                waits=["st_s"], inc="st_t")
            sqt[i] = em.val("st_t")
        ln_rows(em, sb, fa, ps["pa"], P0_RW, "st_s", "st_v", "st_t",
                extraw=[("row_wr", em.val("row_wr"))])
        em.dma("gpsimd", dsc["lnrow"].ap()[0:1, tc * 512:(tc + 1) * 512],
               fa(P0_RW, 512)[0:1, :], waits=["st_v"], inc="row_wr")
        em.dma("gpsimd", dsc["lnrow"].ap()[1:2, tc * 512:(tc + 1) * 512],
               fa(P0_RW + 512, 512)[0:1, :], inc="row_wr")
        lr = P0_LR + (tc % 2) * 1024
        rch = "rep0" if tc % 2 == 0 else "rep0b"
        em.dma("gpsimd", fa(lr, 512),
               dsc["lnrow"].ap()[0:1, tc * 512:(tc + 1) * 512].to_broadcast([128, 512]),
               waits=["row_wr", ("st_v", hist_v.get(tc - 2, 0))], inc=rch)
        em.dma("gpsimd", fa(lr + 512, 512),
               dsc["lnrow"].ap()[1:2, tc * 512:(tc + 1) * 512].to_broadcast([128, 512]),
               inc=rch)
        for j in range(4):
            src = fa(hh0 + j * 512, 512)
            dst = bfa(B1_XNH + j * 4096 + tc * 512, 512)
            em.op("vector", lambda e, src=src, dst=dst, lr=lr: e.tensor_tensor(
                dst, src, fa(lr, 512), ALU.mult),
                waits=[rch, xch], inc="st_v")
            em.op("vector", lambda e, dst=dst, lr=lr: e.tensor_tensor(
                dst, dst, fa(lr + 512, 512), ALU.add), inc="st_v")
        hist_v[tc] = em.val("st_v")

    if P0PART < 3:
        return
    def rowbody(e, core, then):
        g = core // 4
        then(e.dma_start(out=dsc["lnrowth"].ap(),
                         in_=dsc["lnrow"].ap()[:, g * THALF:(g + 1) * THALF]))

    em.switch("gpsimd", rowbody, waits=["row_wr"], inc="row_wr", nincs=1)
    for tc in range(4):
        xch = "x_ld" if tc % 2 == 0 else "x_ldb"
        rch = "rep0" if tc % 2 == 0 else "rep0b"
        xoff = P0_XC + (tc % 2) * 4096
        for i in range(8):
            em.dma("sync", fa(xoff + i * 512, 512),
                   dsc["xth"].ap()[i * 128:(i + 1) * 128, tc * 512:(tc + 1) * 512],
                   waits=[("loc_wr", 32), ("st_v", hist_v.get(8 + tc - 2, hist_v[7]))],
                   inc=xch)
        lr = P0_LR + (tc % 2) * 1024
        em.dma("gpsimd", fa(lr, 512),
               dsc["lnrowth"].ap()[0:1, tc * 512:(tc + 1) * 512].to_broadcast([128, 512]),
               waits=["row_wr", ("st_v", hist_v.get(8 + tc - 2, hist_v[7]))],
               inc=rch)
        em.dma("gpsimd", fa(lr + 512, 512),
               dsc["lnrowth"].ap()[1:2, tc * 512:(tc + 1) * 512].to_broadcast([128, 512]),
               inc=rch)
        for i in range(8):
            src = fa(xoff + i * 512, 512)
            th = bfa(B1_XNT + i * 2048 + tc * 512, 512)
            em.op("vector", lambda e, src=src, th=th, lr=lr: e.tensor_tensor(
                th, src, fa(lr, 512), ALU.mult), waits=[rch, xch], inc="st_v")
            em.op("vector", lambda e, th=th, lr=lr: e.tensor_tensor(
                th, th, fa(lr + 512, 512), ALU.add), inc="st_v")
        hist_v[8 + tc] = em.val("st_v")
    if DEBUG:
        em.dma("gpsimd", dsc["dbgb"].ap()[5], bfa(B1_XNT, 2048), waits=["st_v"],
               inc="dbg_wr")
        em.dma("gpsimd", dsc["dbgb"].ap()[6], bfa(B1_XNH, 2048), waits=["st_v"],
               inc="dbg_wr")


def phase1(em, nc, din, dout, dsc, sb, ps, fa, bfa):
    unit = [0]
    it = 0
    for h in range(NHPC):
        for ck in range(NCH):
            if it < P1HCK:
                l0_pre(em, din, dsc, sb, ps, fa, bfa, h, ck)
                if P1SCAN:
                    l0_scan(em, dsc, sb, ps, fa, bfa, h, ck)
                it += 1
            for _ in range(6):
                if unit[0] < min(96, P1U):
                    ffn0_unit(em, din, dsc, sb, ps, fa, bfa, unit[0])
                    unit[0] += 1
    while unit[0] < min(96, P1U):
        ffn0_unit(em, din, dsc, sb, ps, fa, bfa, unit[0])
        unit[0] += 1


def l0_pre(em, din, dsc, sb, ps, fa, bfa, h, ck):
    t0 = ck * TC1
    war = [("sc_v", em.val("sc_v")), ("epi_v", em.val("epi_v")),
           ("sc_s", em.val("sc_s"))]
    if P1SKIP:
        if P1VAR >= 6:
            dtbcol = sb["dtb"][:, 0:1]
            dsl = fa(P1_DL, 512)
            if P1VAR == 6:
                em.op("vector", lambda e, dsl=dsl, dtbcol=dtbcol: e.tensor_scalar(
                    dsl, ps["pa"][:, :], dtbcol, 0.0, ALU.add, ALU.max),
                    waits=[("pm_v", em.val("pm_v"))], inc="pre_v")
            elif P1VAR == 7:
                em.op("vector", lambda e, dsl=dsl, dtbcol=dtbcol: e.tensor_scalar(
                    dsl, fa(P1_DB, 512), dtbcol, 0.0, ALU.add, ALU.max),
                    waits=[("pm_v", em.val("pm_v"))], inc="pre_v")
            elif P1VAR == 8:
                em.op("vector", lambda e, dsl=dsl: e.tensor_copy(
                    dsl, ps["pa"][:, :]),
                    waits=[("pm_v", em.val("pm_v"))], inc="pre_v")
            return
        nit = 1 if P1VAR >= 3 else 2
        for dh in range(nit):
            for tc in range(nit):
                if P1VAR != 5:
                    em.op("tensor", lambda e, dh=dh, tc=tc: e.matmul(
                        ps["pa"][:, :],
                        sb["dtw"][:, (h * 2 + dh) * 128:(h * 2 + dh) * 128 + 128],
                        fa(P1_DB + tc * 512, 512)[0:RNK, :], start=True, stop=True),
                        waits=["pre_s", ("pre_v", em.val("pre_v"))], inc="pre_t")
                dtbcol = sb["dtb"][:, h * 2 + dh:h * 2 + dh + 1]
                dsl = fa(P1_DL + dh * 1024 + tc * 512, 512)
                if P1VAR != 4:
                    em.op("vector", lambda e, dsl=dsl, dtbcol=dtbcol: e.tensor_scalar(
                        dsl, ps["pa"][:, :], dtbcol, 0.0, ALU.add, ALU.max),
                        waits=["pre_t", ("sc_s", em.val("sc_s"))], inc="pre_v")
                if P1PART >= 6:
                    em.op("scalar", lambda e, dtbcol=dtbcol: e.activation(
                        fa(P1_T0, 512), ps["pa"][:, :], AF.Abs, bias=dtbcol),
                        waits=["pre_t"], inc="pre_s")
        return
    for e4 in range(4):
        for nh in range(2):
            pst = ps["pa"] if (e4 * 2 + nh) % 2 == 0 else ps["pb"]
            em.op("tensor", lambda e, e4=e4, nh=nh, pst=pst: e.matmul(
                pst[:, :], sb["ipw"][:, h * 512 + e4 * 128:h * 512 + (e4 + 1) * 128],
                bfa(B1_XNH + h * 4096 + t0 + nh * 512, 512), start=True, stop=True),
                waits=["pm_v", ("st_v", em.val("st_v")), ("pre_s", em.val("pre_s")),
                       ("pre_v", em.val("pre_v"))],
                inc="pre_t")
            if e4 < 2:
                dst = fa(P1_U + e4 * 1027 + 3 + nh * 512, 512)
                em.op("scalar", lambda e, dst=dst, pst=pst: e.activation(
                    dst, pst[:, :], AF.Copy), waits=["pre_t"] + war, inc="pre_s")
            else:
                dst = fa(P1_SZ + (e4 - 2) * 1024 + nh * 512, 512)
                em.op("scalar", lambda e, dst=dst, pst=pst: e.activation(
                    dst, pst[:, :], AF.Silu), waits=["pre_t"] + war, inc="pre_s")
    if P1PART < 2:
        return
    for dh in range(2):
        if ck == 0:
            em.op("vector", lambda e, dh=dh: e.memset(fa(P1_U + dh * 1027, 3), 0.0),
                  waits=[("pre_s", em.val("pre_s"))], inc="pre_v")
        else:
            em.op("vector", lambda e, dh=dh: e.tensor_copy(
                fa(P1_U + dh * 1027, 3), sb["colz"][:, 8 + dh * 3:11 + dh * 3]),
                waits=[("pre_s", em.val("pre_s"))], inc="pre_v")
    for dh in range(2):
        ub = P1_U + dh * 1027
        ucd = fa(P1_UC + dh * 1024, 1024)
        cwcols = (h * 2 + dh) * 4
        cbcol = sb["convb"][:, h * 2 + dh:h * 2 + dh + 1]
        em.op("vector", lambda e, ub=ub, ucd=ucd, cwcols=cwcols, cbcol=cbcol:
              e.tensor_scalar(ucd, fa(ub, 1024), sb["convw"][:, cwcols:cwcols + 1],
                              cbcol, ALU.mult, ALU.add),
              waits=["pre_s"], inc="pre_v")
        for k in range(1, KC):
            em.op("vector", lambda e, k=k, ub=ub, ucd=ucd, cwcols=cwcols:
                  e.scalar_tensor_tensor(ucd, fa(ub + k, 1024),
                                         sb["convw"][:, cwcols + k:cwcols + k + 1],
                                         ucd, ALU.mult, ALU.add), inc="pre_v")
        if ck < NCH - 1:
            em.op("vector", lambda e, ub=ub, dh=dh: e.tensor_copy(
                sb["colz"][:, 8 + dh * 3:11 + dh * 3], fa(ub + 1024, 3)), inc="pre_v")
        em.op("scalar", lambda e, ucd=ucd: e.activation(ucd, ucd, AF.Silu),
              waits=["pre_v"], inc="pre_s")
    if P1PART < 3:
        return
    for tc in range(2):
        for dh in range(2):
            em.op("tensor", lambda e, tc=tc, dh=dh: e.matmul(
                ps["pb"][0:40, :], sb["xpw"][:, (h * 2 + dh) * 40:(h * 2 + dh) * 40 + 40],
                fa(P1_UC + dh * 1024 + tc * 512, 512), start=(dh == 0), stop=(dh == 1)),
                waits=["pre_s"], inc="pre_t")
        em.op("scalar", lambda e, tc=tc: e.activation(
            fa(P1_DB + tc * 512, 512)[0:40, :], ps["pb"][0:40, :], AF.Copy),
            waits=["pre_t"], inc="pre_s")
    if P1PART < 4:
        return
    em.op("scalar", lambda e: e.activation(sb["bcst"][0:40, :],
                                           fa(P1_DB, 1024)[0:40, :], AF.Copy),
          waits=["pre_s", ("brc_wr", em.val("brc_wr"))], inc="pre_s")
    em.dma("gpsimd", dsc["brc"].ap()[h, :, t0:t0 + TC1], sb["bcst"][8:40, :],
           waits=["pre_s"], inc="brc_wr")
    if P1PART < 5:
        return
    for dh in range(2):
        for tc in range(2):
            em.op("tensor", lambda e, dh=dh, tc=tc: e.matmul(
                ps["pa"][:, :], sb["dtw"][:, (h * 2 + dh) * 128:(h * 2 + dh) * 128 + 128],
                fa(P1_DB + tc * 512, 512)[0:RNK, :], start=True, stop=True),
                waits=["pre_s", ("pre_v", em.val("pre_v"))], inc="pre_t")
            dtbcol = sb["dtb"][:, h * 2 + dh:h * 2 + dh + 1]
            dsl = fa(P1_DL + dh * 1024 + tc * 512, 512)
            em.op("vector", lambda e, dtbcol=dtbcol: e.tensor_scalar(
                fa(P1_T1, 512), ps["pa"][:, :], dtbcol, None, ALU.add),
                waits=["pre_t", ("pre_s", em.val("pre_s")),
                       ("sc_s", em.val("sc_s"))], inc="pre_v")
            em.op("vector", lambda e, dsl=dsl: e.tensor_scalar(
                dsl, fa(P1_T1, 512), 0.0, None, ALU.max), inc="pre_v")
            if P1PART < 6:
                continue
            em.op("scalar", lambda e: e.activation(
                fa(P1_T0, 512), fa(P1_T1, 512), AF.Abs),
                waits=[("pre_v", em.val("pre_v"))], inc="pre_s")
            if P1PART >= 7:
                em.op("scalar", lambda e: e.activation(fa(P1_T1, 512), fa(P1_T0, 512),
                                                       AF.Exp, scale=-1.0),
                      waits=[("pre_v", em.val("pre_v"))], inc="pre_s")
            if P1PART >= 8:
                em.op("scalar", lambda e: e.activation(fa(P1_T1, 512), fa(P1_T1, 512),
                                                       AF.Ln, bias=1.0), inc="pre_s")
            if P1PART < 9:
                continue
            em.op("vector", lambda e, dsl=dsl: e.tensor_tensor(
                dsl, dsl, fa(P1_T1, 512), ALU.add), waits=["pre_s"], inc="pre_v")
        if P1PART < 9:
            continue
        em.op("vector", lambda e, dh=dh: e.tensor_tensor(
            bfa(B1_DU + dh * 1024, 1024), fa(P1_DL + dh * 1024, 1024),
            fa(P1_UC + dh * 1024, 1024), ALU.mult), inc="pre_v")
    if DEBUG and h == 0 and ck == 0:
        em.dma("gpsimd", dsc["dbg"].ap()[5][:, 0:128], sb["acoef"][:, :],
               waits=["pre_v"], inc="dbg_wr")
        em.dma("gpsimd", dsc["dbg"].ap()[0], fa(P1_DL, 2048), waits=["pre_v"],
               inc="dbg_wr")
        em.dma("gpsimd", dsc["dbg"].ap()[1], fa(P1_UC, 2048), waits=["pre_v"],
               inc="dbg_wr")
        em.dma("gpsimd", dsc["dbg"].ap()[2], fa(P1_SZ, 2048), waits=["pre_v"],
               inc="dbg_wr")
        em.dma("gpsimd", dsc["dbgb"].ap()[0], bfa(B1_DU, 2048), waits=["pre_v"],
               inc="dbg_wr")


def l0_scan(em, dsc, sb, ps, fa, bfa, h, ck):
    t0 = ck * TC1
    tbase = em.val("sc_t")
    da_guard = {0: 0, 1: 0}
    rep_guard = {0: 0, 1: 0}
    for n in range(NST):
        par = n % 2
        rch = "rep_ld" if par == 0 else "rep_ldb"
        em.dma("gpsimd", bfa(B1_BR + par * 1024, 1024),
               dsc["brc"].ap()[h, n:n + 1, t0:t0 + TC1].to_broadcast([128, TC1]),
               waits=["brc_wr", ("sc_v", rep_guard[par])],
               inc=rch)
        em.dma("gpsimd", bfa(B1_CR + par * 1024, 1024),
               dsc["brc"].ap()[h, NST + n:NST + n + 1, t0:t0 + TC1]
               .to_broadcast([128, TC1]),
               inc=rch)
        for dh in range(2):
            yps = ps["py0"] if dh == 0 else ps["py1"]
            acol = sb["acoef"][:, (h * 2 + dh) * NST + n:(h * 2 + dh) * NST + n + 1]
            da = fa(P1_DA + dh * 1024, 1024)
            em.op("scalar", lambda e, da=da, acol=acol, dh=dh: e.activation(
                da, fa(P1_DL + dh * 1024, 1024), AF.Exp, scale=acol),
                waits=["pre_v", ("sc_v", da_guard[dh])], inc="sc_s")
            xt = bfa(B1_X, 1024)
            ht = bfa(B1_H, 1024)
            pr = bfa(B1_PR + (par * 2 + dh) * 1024, 1024)
            em.op("vector", lambda e, xt=xt, dh=dh, par=par: e.tensor_tensor(
                xt, bfa(B1_DU + dh * 1024, 1024), bfa(B1_BR + par * 1024, 1024),
                ALU.mult),
                waits=[rch, "sc_s", "pre_v"], inc="sc_v")
            init = 0.0 if ck == 0 else sb["carry"][:, dh * NST + n:dh * NST + n + 1]
            em.op("vector", lambda e, ht=ht, da=da, xt=xt, init=init:
                  e.tensor_tensor_scan(ht, da, xt, init, ALU.mult, ALU.add),
                  inc="sc_v")
            da_guard[dh] = em.val("sc_v")
            prw = [("sc_t", tbase + 4 * (n - 1))] if n >= 2 else []
            em.op("vector", lambda e, pr=pr, ht=ht, par=par: e.tensor_tensor(
                pr, ht, bfa(B1_CR + par * 1024, 1024), ALU.mult),
                waits=prw, inc="sc_v")
            if ck < NCH - 1:
                em.op("vector", lambda e, ht=ht, dh=dh, n=n: e.tensor_copy(
                    sb["carry"][:, dh * NST + n:dh * NST + n + 1], ht[:, TC1 - 1:TC1]),
                    inc="sc_v")
            if DEBUG and h == 0 and ck == 0 and dh == 0 and n == 0:
                em.dma("gpsimd", dsc["dbg"].ap()[3][:, 0:1024], da,
                       waits=["sc_v"], inc="dbg_wr")
                em.dma("gpsimd", dsc["dbgb"].ap()[1][:, 0:1024], xt,
                       waits=["sc_v"], inc="dbg_wr")
                em.dma("gpsimd", dsc["dbgb"].ap()[2][:, 0:1024], ht,
                       waits=["sc_v"], inc="dbg_wr")
                em.dma("gpsimd", dsc["dbgb"].ap()[3][:, 0:1024], pr,
                       waits=["sc_v"], inc="dbg_wr")
            if DEBUG and h == 0 and ck == 0 and n <= 2:
                em.op("vector", lambda e: e.engine_nop(), waits=["dbg_wr"],
                      inc="sc_v")
            for q in range(2):
                em.op("tensor", lambda e, yps=yps, pr=pr, q=q, n=n: e.matmul(
                    yps[:, q * 512:(q + 1) * 512], sb["identb"][:, :],
                    pr[:, q * 512:(q + 1) * 512], start=(n == 0), stop=(n == NST - 1)),
                    waits=["sc_v"], inc="sc_t")
        rep_guard[par] = em.val("sc_v")
    for dh in range(2):
        yps = ps["py0"] if dh == 0 else ps["py1"]
        dcol = sb["ddd"][:, h * 2 + dh:h * 2 + dh + 1]
        yg = bfa(B1_YG + dh * 1024, 1024)
        em.op("vector", lambda e, dh=dh, dcol=dcol, yps=yps: e.scalar_tensor_tensor(
            fa(P1_EP, 1024), fa(P1_UC + dh * 1024, 1024), dcol, yps[:, :],
            ALU.mult, ALU.add),
            waits=["sc_t", ("epi_t", em.val("epi_t"))], inc="epi_v")
        em.op("vector", lambda e, dh=dh, yg=yg: e.tensor_tensor(
            yg, fa(P1_EP, 1024), fa(P1_SZ + dh * 1024, 1024), ALU.mult),
            inc="epi_v")
        if DEBUG and h == 0 and ck == 0 and dh == 0:
            em.dma("gpsimd", dsc["dbg"].ap()[4][:, 0:1024], fa(P1_EP, 1024),
                   waits=["epi_v"], inc="dbg_wr")
            em.dma("gpsimd", dsc["dbgb"].ap()[4][:, 0:1024], yg,
                   waits=["epi_v"], inc="dbg_wr")
            em.op("vector", lambda e: e.engine_nop(), waits=["dbg_wr"], inc="epi_v")
        for q in range(2):
            em.op("tensor", lambda e, dh=dh, q=q, yg=yg: e.matmul(
                ps["pc" if q == 0 else "pd"][:, :],
                sb["opw"][:, (h * 2 + dh) * 128:(h * 2 + dh) * 128 + 128],
                yg[:, q * 512:q * 512 + 512], start=(dh == 0), stop=(dh == 1)),
                waits=["epi_v", ("f0_v", em.val("f0_v")), ("epi_s", em.val("epi_s"))],
                inc="epi_t")
    for q in range(2):
        em.op("scalar", lambda e, q=q: e.activation(
            fa(P1_SS + q * 512, 512), ps["pc" if q == 0 else "pd"][:, :], AF.Copy),
            waits=["epi_t", ("s_wr", em.val("s_wr"))], inc="epi_s")
    em.dma("gpsimd", dsc["sagi"].ap()[h * 128:(h + 1) * 128, t0:t0 + TC1],
           fa(P1_SS, 1024), waits=["epi_s"], inc="s_wr")


def ffn0_unit(em, din, dsc, sb, ps, fa, bfa, unit):
    hidc, tc = unit // 4, unit % 4
    if tc == 0:
        em.dma("sync", sb["w0slab"][:, 0:1024], din["wg0"].ap()[hidc],
               waits=[("f0_s", em.val("f0_s"))], inc="f0_ld")
        em.op("scalar", lambda e: e.activation(sb["w0bf"][:, 0:1024],
                                               sb["w0slab"][:, 0:1024],
                                               AF.Copy, scale=1.0 / 127.0),
              waits=["f0_ld", ("pm_v", 3), ("f0_t", em.val("f0_t"))], inc="f0_s")
        em.dma("sync", sb["w0slab"][:, 0:1024], din["wu0"].ap()[hidc],
               waits=[("f0_s", em.val("f0_s"))], inc="f0_ld")
        em.op("scalar", lambda e: e.activation(sb["w0bf"][:, 1024:2048],
                                               sb["w0slab"][:, 0:1024],
                                               AF.Copy, scale=1.0 / 127.0),
              waits=["f0_ld", ("f0_t", em.val("f0_t"))], inc="f0_s")
    for d in range(8):
        mv = bfa(B1_XNT + d * 2048 + tc * 512, 512)
        em.op("tensor", lambda e, d=d, mv=mv: e.matmul(
            ps["pc"][:, :], sb["w0bf"][:, d * 128:(d + 1) * 128], mv,
            start=(d == 0), stop=(d == 7)),
            waits=["f0_s", ("f0_v", em.val("f0_v")), ("epi_s", em.val("epi_s")),
                   ("st_v", em.val("st_v"))],
            inc="f0_t")
    for d in range(8):
        mv = bfa(B1_XNT + d * 2048 + tc * 512, 512)
        em.op("tensor", lambda e, d=d, mv=mv: e.matmul(
            ps["pd"][:, :], sb["w0bf"][:, 1024 + d * 128:1024 + (d + 1) * 128], mv,
            start=(d == 0), stop=(d == 7)), inc="f0_t")
    em.op("scalar", lambda e: e.activation(fa(P1_F0, 512), ps["pc"][:, :], AF.Silu),
          waits=["f0_t"], inc="f0_s")
    em.op("vector", lambda e: e.tensor_tensor(bfa(B1_HF, 512), fa(P1_F0, 512),
                                              ps["pd"][:, :], ALU.mult),
          waits=["f0_s", ("hf_wr", em.val("hf_wr"))], inc="f0_v")
    em.dma("gpsimd", dsc["hfd"].ap()[hidc, :, tc * 512:(tc + 1) * 512], bfa(B1_HF, 512),
           waits=["f0_v"], inc="hf_wr")


def phase2(em, nc, din, dout, dsc, sb, ps, fa, bfa):
    p1rel = [("pre_t", em.val("pre_t")), ("f0_t", em.val("f0_t")),
             ("sc_t", em.val("sc_t")), ("sc_v", em.val("sc_v")),
             ("epi_t", em.val("epi_t")), ("f0_v", em.val("f0_v")),
             ("sc_s", em.val("sc_s")), ("epi_v", em.val("epi_v"))]
    em.coll("AllGather", ALU.bypass, RG8, [dsc["sagi"].ap().opt()],
            [dsc["sago"].ap().opt()], waits=["s_wr"], inc="cc")
    for hc in range(24):
        em.dma("sync", sb["w0slab"][:, 0:1024], din["wd0"].ap()[hc],
               waits=[("g_s", em.val("g_s")), ("f0_t", em.val("f0_t"))],
               inc="f0_ld")
        em.op("scalar", lambda e, hc=hc: e.activation(
            bfa(B2_WD + hc * 1024, 1024), sb["w0slab"][:, 0:1024], AF.Copy,
            scale=1.0 / 127.0), waits=["f0_ld", ("pm_v", 3)] + p1rel, inc="g_s")
    if DEBUG:
        em.dma("gpsimd", dsc["dbgb"].ap()[7][:, 0:1024], bfa(B2_WD, 1024),
               waits=["g_s"], inc="dbg_wr")
    lp = lpbq_all(em, din, dsc, sb, fa, bfa, p1rel)
    for tc in range(4):
        for hc in range(24):
            em.dma("sync", bfa(B2_HF + hc * 512, 512),
                   dsc["hfd"].ap()[hc, :, tc * 512:(tc + 1) * 512],
                   waits=["hf_wr", ("g_t", tc * 192)] + p1rel, inc="hf_ld")
        if DEBUG and tc == 0:
            em.dma("gpsimd", dsc["dbgb"].ap()[7][:, 1024:1536], bfa(B2_HF, 512),
                   waits=["hf_ld"], inc="dbg_wr")
        for d in range(8):
            pst = ps["pc"] if d % 2 == 0 else ps["pd"]
            for hc in range(24):
                dbw = ["dbg_wr"] if DEBUG else []
                em.op("tensor", lambda e, d=d, hc=hc, pst=pst: e.matmul(
                    pst[:, :], bfa(B2_WD + hc * 1024 + d * 128, 128),
                    bfa(B2_HF + hc * 512, 512), start=(hc == 0), stop=(hc == 23)),
                    waits=["g_s", "hf_ld", ("g_v", em.val("g_v"))] + dbw, inc="g_t")
            em.op("scalar", lambda e, d=d, pst=pst: e.activation(
                fa(P2_G, 512), pst[:, :], AF.Silu),
                waits=["g_t", ("c_wr", em.val("c_wr"))], inc="g_s")
            em.dma("gpsimd", dsc["gd"].ap()[d, :, tc * 512:(tc + 1) * 512],
                   fa(P2_G, 512), waits=["g_s"], inc="c_wr")
        next(lp, None)
        next(lp, None)
    for _ in lp:
        pass
    def sbody(e, core, then):
        g = core // 4
        b = core % 4
        for rk, half in ((b, 0), (b + 4, 1)):
            ins = e.dma_start(
                out=dsc["sloc"].ap()[half * 512:(half + 1) * 512, :],
                in_=dsc["sago"].ap()[rk * 512:(rk + 1) * 512,
                                     g * THALF:(g + 1) * THALF])
            then(ins)

    em.switch("gpsimd", sbody, waits=["cc"], inc="loc_wr", nincs=2)
    sw_done = em.val("loc_wr")
    for tc in range(4):
        for d in range(8):
            em.dma("sync", fa(P2_X + d * 512, 512),
                   dsc["xth"].ap()[d * 128:(d + 1) * 128, tc * 512:(tc + 1) * 512],
                   waits=[("c_v", em.val("c_v")), ("loc_wr", sw_done)] + p1rel,
                   inc="c_ld")
        for d in range(8):
            em.dma("sync", fa(P2_S + d * 512, 512),
                   dsc["sloc"].ap()[d * 128:(d + 1) * 128, tc * 512:(tc + 1) * 512],
                   waits=[("loc_wr", sw_done), ("c_v", em.val("c_v"))], inc="c_ld")
        for d in range(8):
            em.dma("sync", fa(P2_X1 + d * 512, 512),
                   dsc["gd"].ap()[d, :, tc * 512:(tc + 1) * 512],
                   waits=["c_wr", ("c_v", em.val("c_v"))], inc="c_ld")
        for d in range(8):
            em.op("vector", lambda e, d=d: e.scalar_tensor_tensor(
                fa(P2_X1 + d * 512, 512), fa(P2_S + d * 512, 512), 1.0,
                fa(P2_X1 + d * 512, 512), ALU.mult, ALU.mult),
                waits=["c_ld"], inc="c_v")
            em.op("vector", lambda e, d=d: e.tensor_tensor(
                fa(P2_X1 + d * 512, 512), fa(P2_X1 + d * 512, 512),
                fa(P2_X + d * 512, 512), ALU.add), inc="c_v")
            em.dma("gpsimd", dsc["x1agi"].ap()[d * 128:(d + 1) * 128,
                                               tc * 512:(tc + 1) * 512],
                   fa(P2_X1 + d * 512, 512), waits=["c_v"], inc="c_wr")
    em.coll("AllGather", ALU.bypass, RG8, [dsc["x1agi"].ap().opt()],
            [dsc["x1ago"].ap().opt()], waits=["c_wr"], inc="cc")


def lpbq_all(em, din, dsc, sb, fa, bfa, p1rel):
    for wnm, qnm, ntile, icols in (("wg1", "wqg", 24, C), ("wu1", "wqu", 24, C),
                                   ("wd1", "wqd", 8, HID)):
        for ot in range(ntile):
            nb = icols // 32
            w = fa(P2_LW, icols)
            wv = w.rearrange("p (nb b) -> p nb b", b=32)
            smax = fa(P2_LS, nb)
            em.dma("sync", w, din[wnm].ap()[ot * 128:(ot + 1) * 128, :],
                   waits=[("q2_wr", em.val("q2_wr"))] + p1rel,
                   inc="q2_ld")
            em.op("vector", lambda e, wv=wv, smax=smax: e.tensor_reduce(
                smax, wv, AX.X, ALU.max, apply_absolute_value=True),
                waits=["q2_ld"], inc="q2_v")
            vp(em, sb, smax, "q2_v", 128, nb)
            em.op("vector", lambda e, smax=smax: e.tensor_scalar(
                smax, smax, 1.0 / 7.0, 1e-12, ALU.mult, ALU.max), inc="q2_v")
            vp(em, sb, smax, "q2_v", 128, nb)
            srec = fa(P2_LS + 128, nb)
            em.op("vector", lambda e, srec=srec, smax=smax: e.reciprocal(srec, smax),
                  inc="q2_v")
            vp(em, sb, srec, "q2_v", 128, nb)
            sbb = smax.unsqueeze(2).to_broadcast([128, nb, 32])
            srb = srec.unsqueeze(2).to_broadcast([128, nb, 32])
            em.op("vector", lambda e, wv=wv, srb=srb: e.tensor_tensor(
                wv, wv, srb, ALU.mult), inc="q2_v")
            for cchunk in range(icols // 512):
                c0 = cchunk * 512
                em.op("vector", lambda e, c0=c0, w=w: e.tensor_scalar(
                    w[:, c0:c0 + 512], w[:, c0:c0 + 512], RC, -RC, ALU.add, ALU.add),
                    inc="q2_v")
                em.op("vector", lambda e, c0=c0, w=w: e.tensor_copy(
                    sb["i0"][:, :], w[:, c0:c0 + 512]), inc="q2_v")
                em.op("vector", lambda e, c0=c0, w=w: e.tensor_copy(
                    w[:, c0:c0 + 512], sb["i0"][:, :]), inc="q2_v")
            em.op("vector", lambda e, wv=wv, sbb=sbb: e.tensor_tensor(
                bfa(B2_HF + 12288, icols).rearrange("p (nb b) -> p nb b", b=32),
                wv, sbb, ALU.mult), inc="q2_v")
            em.dma("gpsimd", dsc[qnm].ap()[ot * 128:(ot + 1) * 128, :],
                   bfa(B2_HF + 12288, icols), waits=["q2_v"], inc="q2_wr")
            yield


def phase3(em, nc, din, dout, dsc, sb, ps, fa, bfa):
    p2rel = [("g_t", em.val("g_t")), ("c_v", em.val("c_v")),
             ("q2_v", em.val("q2_v")), ("q2_wr", em.val("q2_wr")),
             ("g_s", em.val("g_s")), ("c_wr", em.val("c_wr"))]
    for nm, dst in [("wkT", B3_WK), ("wvT", B3_WV), ("wrT", B3_WR)]:
        for d in range(8):
            em.dma("sync", fa(P3_X1 + (d % 2) * 512, 512),
                   din[nm].ap()[d * 128:(d + 1) * 128, :],
                   waits=[("p3_s", em.val("p3_s"))] + p2rel, inc="p3_ld")
            em.op("scalar", lambda e, dst=dst, d=d: e.activation(
                bfa(dst + d * 512, 512), fa(P3_X1 + (d % 2) * 512, 512), AF.Copy),
                waits=["p3_ld"], inc="p3_s")
    for ct in range(4):
        em.dma("sync", fa(P3_X1 + (ct % 2) * 1024, 1024),
               din["woT"].ap()[ct * 128:(ct + 1) * 128, :],
               waits=[("p3_s", em.val("p3_s"))], inc="p3_ld")
        em.op("scalar", lambda e, ct=ct: e.activation(
            bfa(B3_WO + ct * 1024, 1024), fa(P3_X1 + (ct % 2) * 1024, 1024), AF.Copy),
            waits=["p3_ld"], inc="p3_s")
    em.op("vector", lambda e: e.memset(sb["pcarry"][:, :], 0.0),
          waits=[("p3_s", em.val("p3_s"))], inc="p3_v")
    vp(em, sb, sb["pcarry"][:, :], "p3_v", 128, 12)
    em.op("vector", lambda e: e.tensor_scalar(sb["pcarry"][:, 0:4],
                                              sb["pcarry"][:, 0:4], 0.0, -1e30,
                                              ALU.mult, ALU.add), inc="p3_v")
    vp(em, sb, sb["pcarry"][:, 0:4], "p3_v", 128, 4)

    def x1body(e, core, then):
        b = core % 4
        for half in range(2):
            rk = b + 4 * half
            ins = e.dma_start(
                out=dsc["x1loc"].ap()[:, half * THALF:(half + 1) * THALF],
                in_=dsc["x1ago"].ap()[rk * 1024:(rk + 1) * 1024, :])
            then(ins)

    em.switch("gpsimd", x1body, waits=["cc"], inc="loc_wr", nincs=2)
    x1loc_done = em.val("loc_wr")

    for ck in range(8):
        t0 = ck * 512
        for d in range(8):
            em.dma("sync", fa(P3_X1 + d * 513 + 1, 512),
                   dsc["x1loc"].ap()[d * 128:(d + 1) * 128, t0:t0 + 512],
                   waits=[("loc_wr", x1loc_done), ("p3_v", em.val("p3_v")),
                          ("p3_t", em.val("p3_t")), ("p3_s", em.val("p3_s"))],
                   inc="p3_ld")
        sqt = {}
        for d in range(8):
            xs = fa(P3_X1 + d * 513 + 1, 512)
            sq = fa(P3_XN + (d % 2) * 512, 512)
            em.op("tensor", lambda e, xs=xs, d=d: e.matmul(
                ps["pa"][0:1, :], sb["onesf"][:, :], xs, start=(d == 0), stop=(d == 7)),
                waits=["p3_ld"], inc="p3_t")
            em.op("scalar", lambda e, xs=xs, sq=sq: e.activation(sq, xs, AF.Square),
                  waits=["p3_ld", ("p3_t", sqt.get(d - 2, 0))], inc="p3_s")
            em.op("tensor", lambda e, sq=sq, d=d: e.matmul(
                ps["pa"][32:33, :], sb["onesf"][:, :], sq, start=(d == 0), stop=(d == 7)),
                waits=["p3_s"], inc="p3_t")
            sqt[d] = em.val("p3_t")
        ln_rows(em, sb, fa, ps["pa"], P3_RW, "p3_s", "p3_v", "p3_t",
                extraw=[("row3", em.val("row3"))])
        em.dma("gpsimd", dsc["lnrow3"].ap()[0:1, t0:t0 + 512], fa(P3_RW, 512)[0:1, :],
               waits=["p3_v"], inc="row3")
        em.dma("gpsimd", dsc["lnrow3"].ap()[1:2, t0:t0 + 512],
               fa(P3_RW + 512, 512)[0:1, :], inc="row3")
        em.dma("gpsimd", fa(P3_LR, 512),
               dsc["lnrow3"].ap()[0:1, t0:t0 + 512].to_broadcast([128, 512]),
               waits=["row3", ("p3_v", em.val("p3_v"))], inc="rep0")
        em.dma("gpsimd", fa(P3_LR + 512, 512),
               dsc["lnrow3"].ap()[1:2, t0:t0 + 512].to_broadcast([128, 512]),
               inc="rep0")
        for d in range(8):
            xs = fa(P3_X1 + d * 513 + 1, 512)
            xn = fa(P3_XN + d * 513 + 1, 512)
            em.op("vector", lambda e, xs=xs, xn=xn: e.tensor_tensor(
                xn, xs, fa(P3_LR, 512), ALU.mult), waits=["rep0"], inc="p3_v")
            em.op("vector", lambda e, xn=xn: e.tensor_tensor(
                xn, xn, fa(P3_LR + 512, 512), ALU.add), inc="p3_v")
            em.op("vector", lambda e, xn=xn: e.tensor_reduce(
                sb["mxc"][:, 1:2], xn, AX.X, ALU.max, apply_absolute_value=True),
                inc="p3_v")
            vp(em, sb, sb["mxc"][:, 1:2], "p3_v", 128, 1)
            em.op("vector", lambda e: e.tensor_tensor(
                sb["mxc"][:, 0:1], sb["mxc"][:, 0:1], sb["mxc"][:, 1:2], ALU.max),
                inc="p3_v")
            vp(em, sb, sb["mxc"][:, 0:1], "p3_v", 128, 1)
            em.op("vector", lambda e, d=d: e.tensor_copy(
                fa(P3_XN + d * 513, 1), sb["xhalo"][:, d:d + 1]), inc="p3_v")
            vp(em, sb, fa(P3_XN + d * 513, 1), "p3_v", 128, 1)
            for mixn, mixdst in (("tmk", B3_XK), ("tmv", B3_XV), ("tmr", B3_XR)):
                col = sb[mixn][:, d:d + 1]
                em.op("vector", lambda e, d=d, xn=xn: e.tensor_tensor(
                    fa(P3_Y, 512), xn, fa(P3_XN + d * 513, 512), ALU.subtract),
                    inc="p3_v")
                em.op("vector", lambda e, d=d, col=col, mixdst=mixdst: e.scalar_tensor_tensor(
                    bfa(mixdst + d * 512, 512), fa(P3_Y, 512), col,
                    fa(P3_XN + d * 513, 512), ALU.mult, ALU.add), inc="p3_v")
            em.op("vector", lambda e, d=d, xn=xn: e.tensor_copy(
                sb["xhalo"][:, d:d + 1], xn[:, 511:512]), inc="p3_v")
            vp(em, sb, sb["xhalo"][:, d:d + 1], "p3_v", 128, 1)
        for ct in range(4):
            for nm, wof, src, pst in (("k", B3_WK, B3_XK, "pb"),
                                      ("v", B3_WV, B3_XV, "pc"),
                                      ("r", B3_WR, B3_XR, "pd")):
                for d in range(8):
                    em.op("tensor", lambda e, wof=wof, ct=ct, d=d, src=src, pst=pst:
                          e.matmul(ps[pst][:, :],
                                   bfa(wof + d * 512 + ct * 128, 128),
                                   bfa(src + d * 512, 512),
                                   start=(d == 0), stop=(d == 7)),
                          waits=["p3_v", ("p3_s", em.val("p3_s"))], inc="p3_t")
                if nm == "k":
                    em.op("scalar", lambda e, ct=ct, pst=pst: e.activation(
                        fa(P3_K + ct * 512, 512), ps[pst][:, :], AF.Copy),
                        waits=["p3_t"], inc="p3_s")
                elif nm == "v":
                    em.op("scalar", lambda e, ct=ct, pst=pst: e.activation(
                        fa(P3_V + ct * 512, 512), ps[pst][:, :], AF.Copy),
                        waits=["p3_t"], inc="p3_s")
                else:
                    em.op("scalar", lambda e, ct=ct, pst=pst: e.activation(
                        fa(P3_SR + ct * 512, 512), ps[pst][:, :], AF.Sigmoid),
                        waits=["p3_t"], inc="p3_s")
        for ct in range(4):
            wkv_tile(em, sb, fa, bfa, ct)
        for od in range(8):
            for ct in range(4):
                em.op("tensor", lambda e, od=od, ct=ct: e.matmul(
                    ps["pa"][:, :],
                    bfa(B3_WO + ct * 1024 + od * 128, 128),
                    bfa(B3_Y1 + ct * 512, 512), start=(ct == 0), stop=(ct == 3)),
                    waits=["p3_v", ("p3_s", em.val("p3_s"))], inc="p3_t")
            em.op("scalar", lambda e, od=od: e.activation(
                bfa(B3_ST, 512), ps["pa"][:, :], AF.Copy),
                waits=["p3_t", ("p3_wr", em.val("p3_wr"))], inc="p3_s")
            em.dma("gpsimd", dsc["s1agi"].ap()[od * 128:(od + 1) * 128, t0:t0 + 512],
                   bfa(B3_ST, 512), waits=["p3_s"], inc="p3_wr")
    em.dma("gpsimd", dsc["pmax"].ap()[0:1, :], sb["mxc"][:, 0:1], waits=["p3_v"],
           inc="row3")
    em.dma("gpsimd", sb["rowz"][0:1, :],
           dsc["pmax"].ap()[0:1, :], waits=["row3"],
           inc="rep0")
    em.op("vector", lambda e: e.tensor_reduce(sb["colz"][0:1, 2:3], sb["rowz"][0:1, :],
                                              AX.X, ALU.max), waits=["rep0"], inc="p3_v")
    vp(em, sb, sb["colz"][0:1, 2:3], "p3_v", 1, 1)
    em.dma("gpsimd", dsc["ar1i"].ap()[0:1, 0:1], sb["colz"][0:1, 2:3],
           waits=["p3_v"], inc="row3")
    em.dma("gpsimd", dsc["ar1i"].ap()[0:1, 1:2], sb["colz"][0:1, 2:3], inc="row3")
    em.coll("AllReduce", ALU.max, RG8, [dsc["ar1i"].ap().opt()],
            [dsc["ar1o"].ap().opt()], waits=["row3"], inc="cc")
    em.coll("AllGather", ALU.bypass, RG8, [dsc["s1agi"].ap().opt()],
            [dsc["s1ago"].ap().opt()], waits=["p3_wr"], inc="cc")


def wkv_tile(em, sb, fa, bfa, ct):
    kk = fa(P3_K + ct * 512, 512)
    vv = fa(P3_V + ct * 512, 512)
    sr = fa(P3_SR + ct * 512, 512)
    ab = fa(P3_AB + ct * 513, 513)
    bb = fa(P3_BB + ct * 513, 513)
    yy = fa(P3_Y, 512)
    E = fa(P3_XN, 512)
    E2 = fa(P3_XN + 513, 512)
    Ev = fa(P3_XN + 2 * 513, 512)
    ucol = sb["wkvp"][:, ct:ct + 1]
    ewcol = sb["wkvp"][:, 4 + ct:5 + ct]
    pcol = sb["pcarry"][:, ct:ct + 1]
    m = sb["colz"][:, 4:5]
    t1 = sb["colz"][:, 5:6]
    t2 = sb["colz"][:, 6:7]
    nm = sb["colz"][:, 9:10]
    um = sb["colz"][:, 10:11]
    ewb = yy
    em.op("vector", lambda e: e.tensor_reduce(t1, kk, AX.X, ALU.max),
          waits=[("p3_s", em.val("p3_s"))], inc="p3_v")
    vp(em, sb, t1, "p3_v", 128, 1)
    em.op("vector", lambda e: e.tensor_scalar(t2, ucol, 0.0, None, ALU.max), inc="p3_v")
    vp(em, sb, t2, "p3_v", 128, 1)
    em.op("vector", lambda e: e.tensor_tensor(t2, t2, t1, ALU.add), inc="p3_v")
    vp(em, sb, t2, "p3_v", 128, 1)
    em.op("vector", lambda e: e.tensor_tensor(m, t2, pcol, ALU.max), inc="p3_v")
    vp(em, sb, m, "p3_v", 128, 1)
    em.op("vector", lambda e: e.tensor_scalar(nm, m, -1.0, None, ALU.mult), inc="p3_v")
    vp(em, sb, nm, "p3_v", 128, 1)
    em.op("vector", lambda e: e.tensor_tensor(um, ucol, nm, ALU.add), inc="p3_v")
    vp(em, sb, um, "p3_v", 128, 1)
    em.op("scalar", lambda e: e.activation(E, kk, AF.Exp, bias=nm),
          waits=["p3_v"], inc="p3_s")
    em.op("scalar", lambda e: e.activation(E2, kk, AF.Exp, bias=um), inc="p3_s")
    em.op("vector", lambda e: e.tensor_tensor(t1, pcol, nm, ALU.add),
          waits=["p3_s"], inc="p3_v")
    vp(em, sb, t1, "p3_v", 128, 1)
    em.op("scalar", lambda e: e.activation(t1, t1, AF.Exp), waits=["p3_v"], inc="p3_s")
    em.op("vector", lambda e: e.tensor_tensor(ab[:, 0:1], sb["pcarry"][:, 4 + ct:5 + ct],
                                              t1, ALU.mult), waits=["p3_s"], inc="p3_v")
    vp(em, sb, ab[:, 0:1], "p3_v", 128, 1)
    em.op("vector", lambda e: e.tensor_tensor(bb[:, 0:1], sb["pcarry"][:, 8 + ct:9 + ct],
                                              t1, ALU.mult), inc="p3_v")
    vp(em, sb, bb[:, 0:1], "p3_v", 128, 1)
    em.op("vector", lambda e, ewb=ewb: e.tensor_scalar(
        ewb, kk, 0.0, ewcol, ALU.mult, ALU.add), inc="p3_v")
    em.op("vector", lambda e: e.tensor_tensor(Ev, E, vv, ALU.mult), inc="p3_v")
    em.op("vector", lambda e, ewb=ewb: e.tensor_tensor_scan(
        ab[:, 1:513], ewb, Ev, ab[:, 0:1],
        ALU.mult, ALU.add), inc="p3_v")
    em.op("vector", lambda e, ewb=ewb: e.tensor_tensor_scan(
        bb[:, 1:513], ewb, E, bb[:, 0:1],
        ALU.mult, ALU.add), inc="p3_v")
    em.op("vector", lambda e: e.tensor_tensor(yy, E2, vv, ALU.mult), inc="p3_v")
    em.op("vector", lambda e: e.tensor_tensor(yy, yy, ab[:, 0:512], ALU.add), inc="p3_v")
    em.op("vector", lambda e: e.tensor_tensor(E2, E2, bb[:, 0:512], ALU.add), inc="p3_v")
    em.op("vector", lambda e: e.reciprocal(E2, E2), inc="p3_v")
    em.op("vector", lambda e: e.tensor_tensor(yy, yy, E2, ALU.mult), inc="p3_v")
    em.op("vector", lambda e, ct=ct: e.tensor_tensor(
        bfa(B3_Y1 + ct * 512, 512), yy, sr, ALU.mult), inc="p3_v")
    em.op("vector", lambda e: e.tensor_copy(pcol, m), inc="p3_v")
    vp(em, sb, pcol, "p3_v", 128, 1)
    em.op("vector", lambda e, ct=ct: e.tensor_copy(sb["pcarry"][:, 4 + ct:5 + ct],
                                                   ab[:, 512:513]), inc="p3_v")
    vp(em, sb, sb["pcarry"][:, 4 + ct:5 + ct], "p3_v", 128, 1)
    em.op("vector", lambda e, ct=ct: e.tensor_copy(sb["pcarry"][:, 8 + ct:9 + ct],
                                                   bb[:, 512:513]), inc="p3_v")
    vp(em, sb, sb["pcarry"][:, 8 + ct:9 + ct], "p3_v", 128, 1)


def colmax_to_scalar(em, dsc, sb, srccol, dstcol, pcol_idx, chV, chDMA):
    em.dma("gpsimd", dsc["pmax"].ap()[pcol_idx:pcol_idx + 1, :], srccol,
           waits=[chV], inc=chDMA)
    em.dma("gpsimd", sb["rowz"][0:1, :],
           dsc["pmax"].ap()[pcol_idx:pcol_idx + 1, :],
           waits=[chDMA], inc=chDMA)
    em.op("vector", lambda e: e.tensor_reduce(
        sb["colz"][0:1, dstcol:dstcol + 1], sb["rowz"][0:1, :], AX.X, ALU.max),
        waits=[chDMA], inc=chV)
    vp(em, sb, sb["colz"][0:1, dstcol:dstcol + 1], chV, 1, 1)


def phase4(em, nc, din, dout, dsc, sb, ps, fa, bfa):
    em.dma("gpsimd", sb["colz"][:, 0:1],
           dsc["ar1o"].ap()[0:1, 0:1].to_broadcast([128, 1]), waits=["cc"],
           inc="rep0")
    em.op("vector", lambda e: e.tensor_scalar(sb["colz"][:, 0:1], sb["colz"][:, 0:1],
                                              1.0 / QM15, 1e-12, ALU.mult, ALU.max),
          waits=["rep0"], inc="q_v")
    vp(em, sb, sb["colz"][:, 0:1], "q_v", 128, 1)
    em.op("vector", lambda e: e.reciprocal(sb["colz"][:, 1:2], sb["colz"][:, 0:1]),
          inc="q_v")
    vp(em, sb, sb["colz"][:, 1:2], "q_v", 128, 1)
    em.op("vector", lambda e: e.memset(sb["mxc"][:, 2:4], 0.0), inc="q_v")
    vp(em, sb, sb["mxc"][:, 2:4], "q_v", 128, 2)

    p3rel = [("p3_v", em.val("p3_v")), ("p3_t", em.val("p3_t")),
             ("p3_s", em.val("p3_s")), ("p3_wr", em.val("p3_wr"))]
    for tc in range(4):
        for d in range(8):
            em.dma("sync", fa(P4_X1 + d * 512, 512),
                   dsc["x1agi"].ap()[d * 128:(d + 1) * 128, tc * 512:(tc + 1) * 512],
                   waits=[("q_v", em.val("q_v")), ("q_s", em.val("q_s"))] + p3rel,
                   inc="q_ld")
        sqt = {}
        for d in range(8):
            xs = fa(P4_X1 + d * 512, 512)
            sq = fa(P4_SG, 512)
            em.op("tensor", lambda e, xs=xs, d=d: e.matmul(
                ps["pa"][0:1, :], sb["onesf"][:, :], xs, start=(d == 0), stop=(d == 7)),
                waits=["q_ld"], inc="q_t")
            em.op("scalar", lambda e, xs=xs, sq=sq: e.activation(sq, xs, AF.Square),
                  waits=["q_ld", ("q_t", sqt.get(d - 1, 0))], inc="q_s")
            em.op("tensor", lambda e, sq=sq, d=d: e.matmul(
                ps["pa"][32:33, :], sb["onesf"][:, :], sq, start=(d == 0), stop=(d == 7)),
                waits=["q_s"], inc="q_t")
            sqt[d] = em.val("q_t")
        ln_rows(em, sb, fa, ps["pa"], P4_RW, "q_s", "q_v", "q_t",
                extraw=[("row4", em.val("row4"))])
        em.dma("gpsimd", dsc["lnrow4"].ap()[0:1, tc * 512:(tc + 1) * 512],
               fa(P4_RW, 512)[0:1, :], waits=["q_v"], inc="row4")
        em.dma("gpsimd", dsc["lnrow4"].ap()[1:2, tc * 512:(tc + 1) * 512],
               fa(P4_RW + 512, 512)[0:1, :], inc="row4")
        em.dma("gpsimd", fa(P4_LR, 512),
               dsc["lnrow4"].ap()[0:1, tc * 512:(tc + 1) * 512].to_broadcast([128, 512]),
               waits=["row4", ("q_v", em.val("q_v"))], inc="rep0")
        em.dma("gpsimd", fa(P4_LR + 512, 512),
               dsc["lnrow4"].ap()[1:2, tc * 512:(tc + 1) * 512].to_broadcast([128, 512]),
               inc="rep0")
        for d in range(8):
            xn = fa(P4_XN + d * 512, 512)
            em.op("vector", lambda e, d=d, xn=xn: e.tensor_tensor(
                xn, fa(P4_X1 + d * 512, 512), fa(P4_LR, 512), ALU.mult),
                waits=["rep0"], inc="q_v")
            em.op("vector", lambda e, xn=xn: e.tensor_tensor(
                xn, xn, fa(P4_LR + 512, 512), ALU.add), inc="q_v")
            vst = fa(P4_CF, 512)
            em.op("vector", lambda e, xn=xn, vst=vst: e.tensor_scalar(
                vst, xn, sb["colz"][:, 1:2], RC, ALU.mult, ALU.add), inc="q_v")
            em.op("vector", lambda e, vst=vst: e.tensor_scalar(
                vst, vst, -RC, None, ALU.add), inc="q_v")
            em.op("vector", lambda e, vst=vst: e.tensor_copy(sb["i0"][:, :], vst),
                  inc="q_v")
            em.op("scalar", lambda e, d=d, tc=tc: e.activation(
                bfa(B4_XQ + tc * 4096 + d * 512, 512), sb["i0"][:, :], AF.Copy,
                scale=sb["colz"][:, 0:1]), waits=["q_v"], inc="q_s")

    for wi, (qnm, bdram) in enumerate((("wqg", "gtd"), ("wqu", "upd"))):
        for ic in range(8):
            em.dma("sync", bfa(B4_WQ + ic * 3072, 3072),
                   dsc[qnm].ap()[0:HID, ic * 128:(ic + 1) * 128],
                   waits=["q2_wr", ("q_t", em.val("q_t"))] + p3rel, inc="q_ld",
                   transpose=True)
        for hid in range(24):
            for tc in range(4):
                pst = ps["pc"] if (hid * 4 + tc) % 2 == 0 else ps["pd"]
                for d in range(8):
                    em.op("tensor", lambda e, hid=hid, tc=tc, d=d, pst=pst: e.matmul(
                        pst[:, :], bfa(B4_WQ + d * 3072 + hid * 128, 128),
                        bfa(B4_XQ + tc * 4096 + d * 512, 512),
                        start=(d == 0), stop=(d == 7)),
                        waits=["q_ld", "q_s", ("q_v", em.val("q_v"))], inc="q_t")
                em.op("vector", lambda e, pst=pst: e.tensor_reduce(
                    sb["mxc"][:, 4:5], pst[:, :], AX.X, ALU.max,
                    apply_absolute_value=True), waits=["q_t"], inc="q_v")
                vp(em, sb, sb["mxc"][:, 4:5], "q_v", 128, 1)
                em.op("vector", lambda e, wi=wi: e.tensor_tensor(
                    sb["mxc"][:, 2 + wi:3 + wi], sb["mxc"][:, 2 + wi:3 + wi],
                    sb["mxc"][:, 4:5], ALU.max), inc="q_v")
                vp(em, sb, sb["mxc"][:, 2 + wi:3 + wi], "q_v", 128, 1)
                em.op("scalar", lambda e, pst=pst: e.activation(
                    bfa(B4_ST, 512), pst[:, :], AF.Copy),
                    waits=["q_v", ("q_wr", em.val("q_wr"))], inc="q_s")
                em.dma("gpsimd", dsc[bdram].ap()[hid, :, tc * 512:(tc + 1) * 512],
                       bfa(B4_ST, 512), waits=["q_s"], inc="q_wr")
    colmax_to_scalar(em, dsc, sb, sb["mxc"][:, 2:3], 3, 1, "q_v", "row4")
    colmax_to_scalar(em, dsc, sb, sb["mxc"][:, 3:4], 4, 2, "q_v", "row4")
    em.dma("gpsimd", dsc["ar2i"].ap()[0:1, 0:2], sb["colz"][0:1, 3:5],
           waits=["q_v"], inc="row4")
    em.coll("AllReduce", ALU.max, RG8, [dsc["ar2i"].ap().opt()],
            [dsc["ar2o"].ap().opt()], waits=["row4"], inc="cc")
    em.dma("gpsimd", sb["colz"][:, 2:3],
           dsc["ar2o"].ap()[0:1, 0:1].to_broadcast([128, 1]), waits=["cc"], inc="rep0")
    em.dma("gpsimd", sb["colz"][:, 4:5],
           dsc["ar2o"].ap()[0:1, 1:2].to_broadcast([128, 1]), inc="rep0")
    for c0 in (2, 4):
        em.op("vector", lambda e, c0=c0: e.tensor_scalar(
            sb["colz"][:, c0:c0 + 1], sb["colz"][:, c0:c0 + 1], 1.0 / QM15, 1e-12,
            ALU.mult, ALU.max), waits=["rep0"], inc="q_v")
        vp(em, sb, sb["colz"][:, c0:c0 + 1], "q_v", 128, 1)
        em.op("vector", lambda e, c0=c0: e.reciprocal(
            sb["colz"][:, c0 + 1:c0 + 2], sb["colz"][:, c0:c0 + 1]), inc="q_v")
        vp(em, sb, sb["colz"][:, c0 + 1:c0 + 2], "q_v", 128, 1)
    em.op("vector", lambda e: e.memset(sb["mxi"][:, :], 0), inc="q_v")
    vp(em, sb, sb["mxi"][:, :], "q_v", 128, 2)
    qwr_all = em.val("q_wr")
    rb_guard = {0: 0, 1: 0}

    def rebuild_P(tc, hid, with_up):
        par = hid % 2
        qch = "q_gt0" if par == 0 else "q_gt1"
        gt = bfa(B4_GU + par * 1024, 512)
        ut = bfa(B4_GU + par * 1024 + 512, 512)
        em.dma("sync", gt, dsc["gtd"].ap()[hid, :, tc * 512:(tc + 1) * 512],
               waits=[("q_v", rb_guard[par]), ("q_wr", qwr_all)], inc=qch)
        if with_up:
            em.dma("sync", ut, dsc["upd"].ap()[hid, :, tc * 512:(tc + 1) * 512],
                   inc=qch)
        vst = fa(P4_CF, 512)
        em.op("vector", lambda e, gt=gt, vst=vst: e.tensor_scalar(
            vst, gt, sb["colz"][:, 3:4], RC, ALU.mult, ALU.add),
            waits=[qch, ("q_s", em.val("q_s"))], inc="q_v")
        em.op("vector", lambda e, vst=vst: e.tensor_scalar(
            vst, vst, -RC, None, ALU.add), inc="q_v")
        em.op("vector", lambda e, vst=vst: e.tensor_copy(sb["i0"][:, :], vst),
              inc="q_v")
        em.op("scalar", lambda e: e.activation(fa(P4_SG, 512), sb["i0"][:, :],
                                               AF.Sigmoid, scale=sb["colz"][:, 2:3]),
              waits=["q_v"], inc="q_s")
        em.op("vector", lambda e, vst=vst: e.tensor_scalar(
            vst, fa(P4_SG, 512), 65536.0, RC, ALU.mult, ALU.add),
            waits=["q_s"], inc="q_v")
        em.op("vector", lambda e, vst=vst: e.tensor_scalar(
            vst, vst, -RC, None, ALU.add), inc="q_v")
        em.op("vector", lambda e, vst=vst: e.tensor_copy(sb["i1"][:, :], vst),
              inc="q_v")
        em.op("vector", lambda e: e.tensor_scalar(sb["i1"][:, :], sb["i1"][:, :],
                                                  65535, None, ALU.min), inc="q_v")
        em.op("vector", lambda e: e.tensor_tensor(sb["i2"][:, :], sb["i0"][:, :],
                                                  sb["i1"][:, :], ALU.mult), inc="q_v")
        return ut

    for tc in range(4):
        for hid in range(24):
            rebuild_P(tc, hid, False)
            em.op("vector", lambda e: e.tensor_reduce(
                sb["mxi"][:, 1:2], sb["i2"][:, :], AX.X, ALU.max,
                apply_absolute_value=True), inc="q_v")
            vp(em, sb, sb["mxi"][:, 1:2], "q_v", 128, 1)
            em.op("vector", lambda e: e.tensor_tensor(
                sb["mxi"][:, 0:1], sb["mxi"][:, 0:1], sb["mxi"][:, 1:2], ALU.max),
                inc="q_v")
            vp(em, sb, sb["mxi"][:, 0:1], "q_v", 128, 1)
            rb_guard[hid % 2] = em.val("q_v")
    em.op("vector", lambda e: e.tensor_copy(sb["mxc"][:, 5:6], sb["mxi"][:, 0:1]),
          inc="q_v")
    vp(em, sb, sb["mxc"][:, 5:6], "q_v", 128, 1)
    colmax_to_scalar(em, dsc, sb, sb["mxc"][:, 5:6], 6, 3, "q_v", "row4")
    em.dma("gpsimd", dsc["ar3i"].ap()[0:1, 0:1], sb["colz"][0:1, 6:7],
           waits=["q_v"], inc="row4")
    em.dma("gpsimd", dsc["ar3i"].ap()[0:1, 1:2], sb["colz"][0:1, 6:7], inc="row4")
    em.coll("AllReduce", ALU.max, RG8, [dsc["ar3i"].ap().opt()],
            [dsc["ar3o"].ap().opt()], waits=["row4"], inc="cc")
    em.dma("gpsimd", sb["colz"][:, 6:7],
           dsc["ar3o"].ap()[0:1, 0:1].to_broadcast([128, 1]), waits=["cc"], inc="rep0")
    em.op("vector", lambda e: e.tensor_scalar(sb["colz"][:, 7:8], sb["colz"][:, 2:3],
                                              1.0 / 65536.0, None, ALU.mult),
          waits=["rep0"], inc="q_v")
    vp(em, sb, sb["colz"][:, 7:8], "q_v", 128, 1)
    em.op("vector", lambda e: e.tensor_tensor(sb["colz"][:, 8:9], sb["colz"][:, 6:7],
                                              sb["colz"][:, 7:8], ALU.mult), inc="q_v")
    vp(em, sb, sb["colz"][:, 8:9], "q_v", 128, 1)
    em.op("vector", lambda e: e.tensor_scalar(sb["colz"][:, 8:9], sb["colz"][:, 8:9],
                                              1.0 / QM15, 1e-12, ALU.mult, ALU.max),
          inc="q_v")
    vp(em, sb, sb["colz"][:, 8:9], "q_v", 128, 1)
    em.op("vector", lambda e: e.reciprocal(sb["colz"][:, 12:13], sb["colz"][:, 8:9]),
          inc="q_v")
    vp(em, sb, sb["colz"][:, 12:13], "q_v", 128, 1)
    em.op("vector", lambda e: e.tensor_tensor(sb["colz"][:, 6:7], sb["colz"][:, 7:8],
                                              sb["colz"][:, 12:13], ALU.mult),
          inc="q_v")
    vp(em, sb, sb["colz"][:, 6:7], "q_v", 128, 1)

    em.op("vector", lambda e: e.memset(sb["mxi"][:, 0:1], 0), inc="q_v")
    vp(em, sb, sb["mxi"][:, 0:1], "q_v", 128, 1)
    for tc in range(4):
        for hid in range(24):
            ut = rebuild_P(tc, hid, True)
            vst = fa(P4_CF, 512)
            em.op("vector", lambda e, vst=vst: e.tensor_scalar(
                vst, sb["i2"][:, :], sb["colz"][:, 6:7], RC, ALU.mult, ALU.add),
                inc="q_v")
            em.op("vector", lambda e, vst=vst: e.tensor_scalar(
                vst, vst, -RC, None, ALU.add), inc="q_v")
            em.op("vector", lambda e, vst=vst: e.tensor_copy(sb["i2"][:, :], vst),
                  inc="q_v")
            em.op("vector", lambda e, ut=ut, vst=vst: e.tensor_scalar(
                vst, ut, sb["colz"][:, 5:6], RC, ALU.mult, ALU.add),
                waits=[("q_s", em.val("q_s"))], inc="q_v")
            em.op("vector", lambda e, vst=vst: e.tensor_scalar(
                vst, vst, -RC, None, ALU.add), inc="q_v")
            em.op("vector", lambda e, vst=vst: e.tensor_copy(sb["i0"][:, :], vst),
                  inc="q_v")
            em.op("vector", lambda e: e.tensor_tensor(
                sb["i1"][:, :], sb["i2"][:, :], sb["i0"][:, :], ALU.mult), inc="q_v")
            em.op("vector", lambda e: e.tensor_reduce(
                sb["mxi"][:, 1:2], sb["i1"][:, :], AX.X, ALU.max,
                apply_absolute_value=True), inc="q_v")
            vp(em, sb, sb["mxi"][:, 1:2], "q_v", 128, 1)
            em.op("vector", lambda e: e.tensor_tensor(
                sb["mxi"][:, 0:1], sb["mxi"][:, 0:1], sb["mxi"][:, 1:2], ALU.max),
                inc="q_v")
            vp(em, sb, sb["mxi"][:, 0:1], "q_v", 128, 1)
            rb_guard[hid % 2] = em.val("q_v")
    em.op("vector", lambda e: e.tensor_copy(sb["mxc"][:, 6:7], sb["mxi"][:, 0:1]),
          inc="q_v")
    vp(em, sb, sb["mxc"][:, 6:7], "q_v", 128, 1)
    colmax_to_scalar(em, dsc, sb, sb["mxc"][:, 6:7], 9, 3, "q_v", "row4")
    em.dma("gpsimd", dsc["ar4i"].ap()[0:1, 0:1], sb["colz"][0:1, 9:10],
           waits=["q_v"], inc="row4")
    em.dma("gpsimd", dsc["ar4i"].ap()[0:1, 1:2], sb["colz"][0:1, 9:10], inc="row4")
    em.coll("AllReduce", ALU.max, RG8, [dsc["ar4i"].ap().opt()],
            [dsc["ar4o"].ap().opt()], waits=["row4"], inc="cc")
    em.dma("gpsimd", sb["colz"][:, 9:10],
           dsc["ar4o"].ap()[0:1, 0:1].to_broadcast([128, 1]), waits=["cc"], inc="rep0")
    em.op("vector", lambda e: e.tensor_tensor(sb["colz"][:, 10:11], sb["colz"][:, 8:9],
                                              sb["colz"][:, 4:5], ALU.mult),
          waits=["rep0"], inc="q_v")
    vp(em, sb, sb["colz"][:, 10:11], "q_v", 128, 1)
    em.op("vector", lambda e: e.tensor_tensor(sb["colz"][:, 11:12], sb["colz"][:, 9:10],
                                              sb["colz"][:, 10:11], ALU.mult), inc="q_v")
    vp(em, sb, sb["colz"][:, 11:12], "q_v", 128, 1)
    em.op("vector", lambda e: e.tensor_scalar(sb["colz"][:, 11:12], sb["colz"][:, 11:12],
                                              1.0 / QM15, 1e-12, ALU.mult, ALU.max),
          inc="q_v")
    vp(em, sb, sb["colz"][:, 11:12], "q_v", 128, 1)
    em.op("vector", lambda e: e.reciprocal(sb["colz"][:, 12:13], sb["colz"][:, 11:12]),
          inc="q_v")
    vp(em, sb, sb["colz"][:, 12:13], "q_v", 128, 1)
    em.op("vector", lambda e: e.tensor_tensor(sb["colz"][:, 9:10], sb["colz"][:, 10:11],
                                              sb["colz"][:, 12:13], ALU.mult),
          inc="q_v")
    vp(em, sb, sb["colz"][:, 9:10], "q_v", 128, 1)

    for hc in range(24):
        em.dma("sync", bfa(B4_WQ + hc * 1024, 1024),
               dsc["wqd"].ap()[0:C, hc * 128:(hc + 1) * 128],
               waits=["q2_wr", ("q_t", em.val("q_t"))], inc="q_ld", transpose=True)

    def s1body(e, core, then):
        g = core // 4
        b = core % 4
        for rep, rk in ((0, b), (1, b + 4)):
            ins = e.dma_start(
                out=dsc["s1loc"].ap()[rep * C:(rep + 1) * C, :],
                in_=dsc["s1ago"].ap()[rk * C:(rk + 1) * C,
                                      g * THALF:(g + 1) * THALF])
            then(ins)

    em.switch("gpsimd", s1body, waits=["cc"], inc="loc_wr", nincs=2)
    s1loc_done = em.val("loc_wr")

    for tc in range(4):
        for hid in range(24):
            ut = rebuild_P(tc, hid, True)
            vst = fa(P4_CF, 512)
            em.op("vector", lambda e, vst=vst: e.tensor_scalar(
                vst, sb["i2"][:, :], sb["colz"][:, 6:7], RC, ALU.mult, ALU.add),
                inc="q_v")
            em.op("vector", lambda e, vst=vst: e.tensor_scalar(
                vst, vst, -RC, None, ALU.add), inc="q_v")
            em.op("vector", lambda e, vst=vst: e.tensor_copy(sb["i2"][:, :], vst),
                  inc="q_v")
            em.op("vector", lambda e, ut=ut, vst=vst: e.tensor_scalar(
                vst, ut, sb["colz"][:, 5:6], RC, ALU.mult, ALU.add),
                waits=[("q_s", em.val("q_s"))], inc="q_v")
            em.op("vector", lambda e, vst=vst: e.tensor_scalar(
                vst, vst, -RC, None, ALU.add), inc="q_v")
            em.op("vector", lambda e, vst=vst: e.tensor_copy(sb["i0"][:, :], vst),
                  inc="q_v")
            em.op("vector", lambda e: e.tensor_tensor(
                sb["i1"][:, :], sb["i2"][:, :], sb["i0"][:, :], ALU.mult), inc="q_v")
            em.op("vector", lambda e, vst=vst: e.tensor_scalar(
                vst, sb["i1"][:, :], sb["colz"][:, 9:10], RC, ALU.mult, ALU.add),
                inc="q_v")
            em.op("vector", lambda e, vst=vst: e.tensor_scalar(
                vst, vst, -RC, None, ALU.add), inc="q_v")
            em.op("vector", lambda e, vst=vst: e.tensor_copy(sb["i1"][:, :], vst),
                  inc="q_v")
            em.op("scalar", lambda e, hid=hid: e.activation(
                bfa(B4_OQ + hid * 512, 512), sb["i1"][:, :], AF.Copy,
                scale=sb["colz"][:, 11:12]),
                waits=["q_v", ("q_t", em.val("q_t"))], inc="q_s")
            rb_guard[hid % 2] = em.val("q_v")
        for d in range(8):
            pst = ps["pc"] if d % 2 == 0 else ps["pd"]
            for hid in range(24):
                em.op("tensor", lambda e, d=d, hid=hid, pst=pst: e.matmul(
                    pst[:, :], bfa(B4_WQ + hid * 1024 + d * 128, 128),
                    bfa(B4_OQ + hid * 512, 512), start=(hid == 0), stop=(hid == 23)),
                    waits=["q_s", "q_ld", ("q_v", em.val("q_v"))], inc="q_t")
            em.op("scalar", lambda e, d=d, pst=pst: e.activation(
                fa(P4_G1 + d * 512, 512), pst[:, :], AF.Silu), waits=["q_t"],
                inc="q_s")
        for rep in range(2):
            base = 36864 if rep == 0 else 43008
            for d in range(8):
                em.dma("sync", bfa(base + d * 512, 512),
                       dsc["s1loc"].ap()[rep * C + d * 128:rep * C + (d + 1) * 128,
                                         tc * 512:(tc + 1) * 512],
                       waits=[("loc_wr", s1loc_done), ("q_v", em.val("q_v"))],
                       inc="q_ld")
        for d in range(8):
            em.dma("sync", fa(P4_X1 + d * 512, 512),
                   dsc["x1agi"].ap()[d * 128:(d + 1) * 128, tc * 512:(tc + 1) * 512],
                   waits=[("q_v", em.val("q_v"))], inc="q_ld")
        for d in range(8):
            em.op("vector", lambda e, d=d: e.tensor_tensor(
                fa(P4_SC + d * 512, 512), bfa(36864 + d * 512, 512),
                bfa(43008 + d * 512, 512), ALU.add),
                waits=["q_ld"], inc="q_v")
            em.op("vector", lambda e, d=d: e.tensor_tensor(
                fa(P4_SC + d * 512, 512), fa(P4_SC + d * 512, 512),
                fa(P4_G1 + d * 512, 512), ALU.mult), waits=["q_s"], inc="q_v")
            em.op("vector", lambda e, d=d: e.tensor_tensor(
                fa(P4_XN + d * 512, 512), fa(P4_SC + d * 512, 512),
                fa(P4_X1 + d * 512, 512), ALU.add), inc="q_v")
            em.dma("gpsimd", dout.ap()[d * 128:(d + 1) * 128, tc * 512:(tc + 1) * 512],
                   fa(P4_XN + d * 512, 512), waits=["q_v"], inc="out_wr")
    em.op("gpsimd", lambda e: e.nop(), waits=["out_wr"], inc=None)


def kernel(**inputs):
    in_maps = make_in_maps(inputs)
    nc = build_program()
    res = run_bass_kernel_spmd(nc, in_maps, core_ids=list(range(8)))
    out = np.zeros((B, L, C), dtype=np.float32)
    for c in range(8):
        b, g = c % 4, c // 4
        out[b, g * THALF:(g + 1) * THALF, :] = res.results[c]["out"].T
    return out


# version 04
# speedup vs baseline: 1.0380x; 1.0380x over previous
import numpy as np
import ml_dtypes
import concourse.bass as bass
import concourse.mybir as mybir
from concourse.bass_utils import run_bass_kernel_spmd

F32 = mybir.dt.float32
BF16 = mybir.dt.bfloat16
I32 = mybir.dt.int32
AF = mybir.ActivationFunctionType
ALU = mybir.AluOpType
AX = mybir.AxisListType

B, L, C = 4, 4096, 1024
DI, NST, KC, RNK = 256, 16, 4, 8
NHPC = 4
HID = 3072
TC1 = 1024
NCH = L // TC1
THALF = L // 2
EPS = 1e-5
QM15 = 32767.0
RC = 12582912.0

RG8 = [[0, 1, 2, 3, 4, 5, 6, 7]]
DEBUG = False
CUT = None
P1HCK = 16
P1U = 96
P1SCAN = True
P1PART = 9
P1VAR = 0
P1SKIP = False
P0PART = 3
P0MASK = 15

FA_COLS = 21016
BF_COLS = 47616

P0_XC = 0
P0_SQ = 8192
P0_RW = 9216
P0_LR = 10240
P0_WST = 12352
P0_HH = 15616

P1_U = 0
P1_UC = 2054
P1_SZ = 4102
P1_DL = 6150
P1_DB = 8198
P1_T0 = 10246
P1_T1 = 10758
P1_SS = 11270
P1_DA = 12294
P1_EP = 14342
P1_F0 = 15366

P2_X = 0
P2_S = 4096
P2_X1 = 8192
P2_G = 12288
P2_LW = 12800
P2_LS = 15872

P3_X1 = 0
P3_XN = 4104
P3_K = 8208
P3_V = 10256
P3_SR = 12304
P3_AB = 14352
P3_BB = 16404
P3_Y = 18456
P3_LR = 18968
P3_RW = 19992

P4_X1 = 0
P4_XN = 4096
P4_LR = 8192
P4_RW = 9216
P4_SG = 10240
P4_S1 = 10752
P4_G1 = 11776
P4_SC = 15872
P4_CF = 19968

B1_XNH = 0
B1_XNT = 16384
B1_DU = 32768
B1_X = 34816
B1_H = 35840
B1_PR = 36864
B1_BR = 40960
B1_CR = 43008
B1_YG = 45056
B1_HF = 47104
B2_WD = 0
B2_HF = 24576
B3_XK = 0
B3_XV = 4096
B3_XR = 8192
B3_WK = 12288
B3_WV = 16384
B3_WR = 20480
B3_WO = 24576
B3_Y1 = 28672
B3_ST = 30720
B4_WQ = 0
B4_XQ = 24576
B4_OQ = 24576
B4_GU = 40960
B4_ST = 43008


def _f(x):
    return np.ascontiguousarray(np.asarray(x), dtype=np.float32)


def make_in_maps(inp):
    eye = np.ascontiguousarray(np.eye(128, dtype=ml_dtypes.bfloat16))
    ones = np.ones((128, 1), np.float32)
    maps = []
    for c in range(8):
        b, g = c % 4, c // 4
        mh = slice(4 * g, 4 * g + 4)
        ch = slice(512 * g, 512 * g + 512)
        m = {"identb": eye, "onesf": ones}
        m["xbT"] = _f(np.asarray(inp["L0_attn_x"])[b].T)
        w = np.asarray(inp["L0_attn_in_proj_w"])[mh]
        m["ipwT"] = _f(np.transpose(w, (2, 0, 1)).reshape(128, 2048))
        cw = np.asarray(inp["L0_attn_conv_w"])[mh].reshape(4, 2, 128, KC)
        m["convw"] = _f(np.transpose(cw, (2, 0, 1, 3)).reshape(128, 32))
        cb = np.asarray(inp["L0_attn_conv_b"])[mh].reshape(4, 2, 128)
        m["convb"] = _f(np.transpose(cb, (2, 0, 1)).reshape(128, 8))
        xp = np.asarray(inp["L0_attn_x_proj_w"])[mh]
        xpT = np.transpose(xp, (2, 0, 1)).reshape(2, 128, 4, 40)
        m["xpwT"] = _f(np.transpose(xpT, (1, 2, 0, 3)).reshape(128, 320))
        dw = np.asarray(inp["L0_attn_dt_proj_w"])[mh].reshape(4, 2, 128, RNK)
        m["dtwT"] = _f(np.transpose(dw, (3, 0, 1, 2)).reshape(RNK, 1024))
        db = np.asarray(inp["L0_attn_dt_proj_b"])[mh].reshape(4, 2, 128)
        m["dtb"] = _f(np.transpose(db, (2, 0, 1)).reshape(128, 8))
        al = np.asarray(inp["L0_attn_A_log"])[mh].reshape(4, 2, 128, NST)
        m["alog"] = _f(np.transpose(al, (2, 0, 1, 3)).reshape(128, 128))
        dd = np.asarray(inp["L0_attn_D"])[mh].reshape(4, 2, 128)
        m["ddd"] = _f(np.transpose(dd, (2, 0, 1)).reshape(128, 8))
        ow = np.asarray(inp["L0_attn_out_proj_w"])[mh]
        owT = np.transpose(ow, (2, 0, 1)).reshape(2, 128, 4, 128)
        m["opwT"] = _f(np.transpose(owT, (1, 2, 0, 3)).reshape(128, 1024))
        wg = np.asarray(inp["L0_ffn_w_gate"]).reshape(24, 128, 8, 128)
        m["wg0"] = np.ascontiguousarray(
            np.transpose(wg, (0, 3, 2, 1)), dtype=np.int32).reshape(24, 128, 1024)
        wu = np.asarray(inp["L0_ffn_w_up"]).reshape(24, 128, 8, 128)
        m["wu0"] = np.ascontiguousarray(
            np.transpose(wu, (0, 3, 2, 1)), dtype=np.int32).reshape(24, 128, 1024)
        wd = np.asarray(inp["L0_ffn_w_down"]).reshape(8, 128, 24, 128)
        m["wd0"] = np.ascontiguousarray(
            np.transpose(wd, (2, 3, 0, 1)), dtype=np.int32).reshape(24, 128, 1024)
        m["wkT"] = _f(np.asarray(inp["L1_attn_Wk"]).T[:, ch])
        m["wvT"] = _f(np.asarray(inp["L1_attn_Wv"]).T[:, ch])
        m["wrT"] = _f(np.asarray(inp["L1_attn_Wr"]).T[:, ch])
        m["woT"] = _f(np.asarray(inp["L1_attn_Wo"]).T[ch, :])
        for nm, key in [("tdec", "L1_attn_time_decay"), ("tfir", "L1_attn_time_first"),
                        ("tmk", "L1_attn_time_mix_k"), ("tmv", "L1_attn_time_mix_v"),
                        ("tmr", "L1_attn_time_mix_r")]:
            m[nm] = _f(np.asarray(inp[key]).reshape(8, 128).T)
        m["wg1"] = _f(inp["L1_ffn_w_gate"])
        m["wu1"] = _f(inp["L1_ffn_w_up"])
        m["wd1"] = _f(inp["L1_ffn_w_down"])
        maps.append(m)
    return maps



class Ch:
    __slots__ = ("name", "n")

    def __init__(self, name):
        self.name = name
        self.n = 0


class Em:
    def __init__(self, nc, ename, eng, chans, sems):
        self.nc = nc
        self.ename = ename
        self.eng = eng
        self.chans = chans
        self.sems = sems
        self.lastw = {}
        self._pid = None

    def pid(self):
        if self._pid is None:
            self._pid = self.eng.partition_id()
        return self._pid

    def _wait(self, waits):
        for w in waits:
            if isinstance(w, tuple):
                name, val = w
            else:
                name, val = w, self.chans[w].n
            if val > 0 and self.lastw.get(name, -1) < val:
                self.eng.wait_ge(self.sems[name], val)
                self.lastw[name] = val

    def val(self, chname):
        return self.chans[chname].n

    def op(self, ename, fn, waits=(), inc=None, k=1):
        if ename == self.ename:
            self._wait(waits)
            ins = fn(self.eng)
            if inc is not None and ins is not None:
                ins.then_inc(self.sems[inc], k)
        if inc is not None:
            self.chans[inc].n += k

    def dma(self, ename, out, in_, waits=(), inc=None, transpose=False):
        if transpose:
            self.op(ename, lambda e: e.dma_start(out=out, in_=in_, transpose=True),
                    waits, inc, 16)
        else:
            self.op(ename, lambda e: e.dma_start(out=out, in_=in_), waits, inc, 16)

    def switch(self, ename, body, waits=(), inc=None, nincs=1, kper=16):
        if ename == self.ename:
            self._wait(waits)
            idx = self.pid()
            sem = self.sems[inc] if inc is not None else None
            for i in self.nc.Switch(engines=[self.eng], index=[idx], n=8):
                def then(instr, sem=sem):
                    if sem is not None:
                        instr.then_inc(sem, kper)
                body(self.eng, i, then)
        if inc is not None:
            self.chans[inc].n += kper * nincs

    def coll(self, kind, op, groups, ins, outs, waits=(), inc=None):
        self.op("gpsimd",
                lambda e: e.collective_compute(kind, op, replica_groups=groups,
                                               ins=ins, outs=outs),
                waits, inc, 1)


CHANNELS = [
    "w_ld", "x_ld", "x_ldb", "loc_wr", "pm_s", "pm_v",
    "st_t", "st_v", "st_s", "row_wr", "rep0", "rep0b",
    "pre_t", "pre_s", "pre_v", "brc_wr",
    "sc_s", "sc_v", "sc_t", "rep_ld", "rep_ldb",
    "epi_v", "epi_t", "epi_s", "s_wr",
    "f0_ld", "f0_t", "f0_s", "f0_v", "hf_wr",
    "cc", "g_t", "g_s", "g_v", "hf_ld",
    "c_ld", "c_v", "c_wr", "q2_v", "q2_s", "q2_ld", "q2_wr",
    "p3_ld", "p3_t", "p3_v", "p3_s", "p3_wr", "row3",
    "q_ld", "q_gt0", "q_gt1", "q_t", "q_v", "q_s", "q_wr", "row4",
    "out_wr", "dbg_wr",
]


def build_program():
    nc = bass.Bass("TRN2", target_bir_lowering=False, detect_race_conditions=False)

    din = {}
    for name, shape, dt in [
        ("xbT", [C, L], F32), ("ipwT", [128, 2048], F32), ("convw", [128, 32], F32),
        ("convb", [128, 8], F32), ("xpwT", [128, 320], F32), ("dtwT", [RNK, 1024], F32),
        ("dtb", [128, 8], F32), ("alog", [128, 128], F32), ("ddd", [128, 8], F32),
        ("opwT", [128, 1024], F32),
        ("wg0", [24, 128, 1024], I32), ("wu0", [24, 128, 1024], I32),
        ("wd0", [24, 128, 1024], I32),
        ("wkT", [C, 512], F32), ("wvT", [C, 512], F32), ("wrT", [C, 512], F32),
        ("woT", [512, C], F32),
        ("tdec", [128, 8], F32), ("tfir", [128, 8], F32), ("tmk", [128, 8], F32),
        ("tmv", [128, 8], F32), ("tmr", [128, 8], F32),
        ("wg1", [HID, C], F32), ("wu1", [HID, C], F32), ("wd1", [C, HID], F32),
        ("identb", [128, 128], BF16), ("onesf", [128, 1], F32),
    ]:
        din[name] = nc.dram_tensor(name, shape, dt, kind="ExternalInput")
    dout = nc.dram_tensor("out", [C, THALF], F32, kind="ExternalOutput")

    dsc = {}
    for name, shape, dt in [
        ("brc", [NHPC, 2 * NST, L], BF16),
        ("lnrow", [2, L], F32), ("lnrow3", [2, L], F32), ("lnrow4", [2, THALF], F32),
        ("hfd", [24, 128, THALF], BF16),
        ("gd", [8, 128, THALF], F32),
        ("wqg", [HID, C], BF16), ("wqu", [HID, C], BF16), ("wqd", [C, HID], BF16),
        ("upd", [24, 128, THALF], BF16), ("gtd", [24, 128, THALF], BF16),
        ("pmax", [4, 128], F32),
        ("sagi", [512, L], F32),
        ("x1agi", [C, THALF], F32),
        ("s1agi", [C, L], BF16),
        ("ar1i", [1, 2], F32), ("ar1o", [1, 2], F32),
        ("ar2i", [1, 2], F32), ("ar2o", [1, 2], F32),
        ("ar3i", [1, 2], F32), ("ar3o", [1, 2], F32),
        ("ar4i", [1, 2], F32), ("ar4o", [1, 2], F32),
        ("dbg", [8, 128, 2048], F32) if DEBUG else ("dbg", [1, 1, 1], F32),
        ("dbgb", [8, 128, 2048], BF16) if DEBUG else ("dbgb", [1, 1, 1], BF16),
        ("xhh", [512, L], F32), ("xth", [C, THALF], F32),
        ("lnrowth", [2, THALF], F32),
        ("sloc", [C, THALF], F32), ("x1loc", [C, L], F32),
        ("s1loc", [2 * C, THALF], BF16),
    ]:
        dbg_fetch = {"dbg", "dbgb", "lnrow", "brc", "hfd", "gd",
                     "wqg", "upd", "lnrow3"}
        kind = "ExternalOutput" if (DEBUG and name in dbg_fetch) else "Internal"
        dsc[name] = nc.dram_tensor(name, shape, dt, kind=kind)
    for name, shape, dt in [
        ("sago", [8 * 512, L], F32), ("x1ago", [8 * C, THALF], F32),
        ("s1ago", [8 * C, L], BF16),
    ]:
        dsc[name] = nc.dram_tensor(name, shape, dt, addr_space="Shared")
    if DEBUG:
        for name, src in [("dsagi", [512, L]), ("dx1", [C, THALF]),
                          ("dar", [1, 8])]:
            dsc[name] = nc.dram_tensor(name, src, F32, kind="ExternalOutput")
        dsc["ds1"] = nc.dram_tensor("ds1", [C, L], BF16, kind="ExternalOutput")

    small_specs = [
        ("ipw", [128, 2048], BF16), ("opw", [128, 1024], BF16),
        ("xpw", [128, 320], F32), ("dtw", [RNK, 1024], F32),
        ("convw", [128, 32], F32), ("convb", [128, 8], F32),
        ("dtb", [128, 8], F32), ("acoef", [128, 128], F32), ("ddd", [128, 8], F32),
        ("tdec", [128, 8], F32), ("tfir", [128, 8], F32), ("tmk", [128, 8], F32),
        ("tmv", [128, 8], F32), ("tmr", [128, 8], F32),
        ("wdec", [128, 8], F32), ("ewc", [128, 8], F32),
        ("identb", [128, 128], BF16), ("onesf", [128, 1], F32),
        ("carry", [128, 32], F32),
        ("pcarry", [128, 12], F32),
        ("wkvp", [128, 8], F32),
        ("xhalo", [128, 16], F32),
        ("w0slab", [128, 1024], I32), ("w0bf", [128, 2048], BF16),
        ("i0", [128, 512], I32), ("i1", [128, 512], I32), ("i2", [128, 512], I32),
        ("mxi", [128, 2], I32),
        ("colz", [128, 16], F32),
        ("mxc", [128, 8], F32),
        ("rowz", [1, 128], F32),
        ("bcst", [128, 1024], BF16),
        ("cvals", [128, 4], F32),
        ("jnk", [128, 96], F32),
    ]

    with nc.cleanup_on_exit():
        ctxs = []
        sb = {}
        for name, shape, dt in small_specs:
            t = nc.sbuf_tensor("sb_" + name, shape, dt)
            sb[name] = t.__enter__()
            ctxs.append(t)
        t = nc.sbuf_tensor("FA", [128, FA_COLS], F32)
        sb["FA"] = t.__enter__()
        ctxs.append(t)
        t = nc.sbuf_tensor("BFA", [128, BF_COLS], BF16)
        sb["BFA"] = t.__enter__()
        ctxs.append(t)

        nc.const_aps.register(F32, EPS, sb["cvals"][:, 0:1])
        nc.const_aps.register(F32, -0.5, sb["cvals"][:, 1:2])
        nc.const_aps.register(F32, 1.0 / 127.0, sb["cvals"][:, 2:3])

        ps = {}
        for name, shape in [("pa", [128, 512]), ("pb", [128, 512]),
                            ("pc", [128, 512]), ("pd", [128, 512]),
                            ("py0", [128, 1024]), ("py1", [128, 1024])]:
            t = nc.psum_tensor(name, shape, F32)
            ps[name] = t.__enter__()
            ctxs.append(t)

        sems = {}
        for name in CHANNELS:
            t = nc.semaphore(name)
            sems[name] = t.__enter__()
            ctxs.append(t)

        blk = nc.Block()
        blk.__enter__()
        chans = {}

        def mk(ename, eng):
            for cn in CHANNELS:
                chans[cn] = Ch(cn)
            em = Em(nc, ename, eng, chans, sems)
            program(em, nc, din, dout, dsc, sb, ps)

        @blk.sync
        def _(e):
            mk("sync", e)

        @blk.tensor
        def _(e):
            mk("tensor", e)

        @blk.scalar
        def _(e):
            mk("scalar", e)

        @blk.vector
        def _(e):
            mk("vector", e)

        @blk.gpsimd
        def _(e):
            mk("gpsimd", e)

        blk.__exit__(None, None, None)
        for t in reversed(ctxs):
            t.__exit__(None, None, None)
    return nc



def program(em, nc, din, dout, dsc, sb, ps):
    FA = sb["FA"]
    BFA = sb["BFA"]

    def fa(c0, n):
        return FA[:, c0:c0 + n]

    def bfa(c0, n):
        return BFA[:, c0:c0 + n]

    ctx = dict(em=em, nc=nc, din=din, dout=dout, dsc=dsc, sb=sb, ps=ps, fa=fa, bfa=bfa)
    phases = [phase0, phase1, phase2, phase3, phase4]
    upto = len(phases) if CUT is None else CUT + 1
    for ph in phases[:upto]:
        ph(**ctx)
    if DEBUG and upto == len(phases):
        em.dma("gpsimd", dsc["dsagi"].ap(), dsc["sagi"].ap(),
               waits=["s_wr"], inc="dbg_wr")
        em.dma("gpsimd", dsc["dx1"].ap(), dsc["x1agi"].ap(),
               waits=["c_wr"], inc="dbg_wr")
        em.dma("gpsimd", dsc["ds1"].ap(), dsc["s1agi"].ap(),
               waits=["p3_wr"], inc="dbg_wr")
        em.dma("gpsimd", dsc["dar"].ap()[0:1, 0:2], dsc["ar1o"].ap(),
               waits=["cc"], inc="dbg_wr")
        em.dma("gpsimd", dsc["dar"].ap()[0:1, 2:4], dsc["ar2o"].ap(),
               inc="dbg_wr")
        em.op("gpsimd", lambda e: e.nop(), waits=["dbg_wr"], inc=None)


def vp(em, sb, ap, ch, p, c):
    em.op("vector", lambda e, ap=ap, p=p, c=c: e.tensor_copy(
        sb["jnk"][0:p, 0:c], ap), inc=ch)


def ln_rows(em, sb, fa, psrow, rw0, chS, chV, chT, extraw=()):
    rrow = fa(rw0, 512)[0:1, :]
    mrow = fa(rw0 + 512, 512)[0:1, :]
    em.op("vector", lambda e: e.tensor_scalar(mrow, psrow[0:1, :], 1.0 / C, None,
                                              ALU.mult),
          waits=[chT] + list(extraw), inc=chV)
    em.op("vector", lambda e: e.scalar_tensor_tensor(rrow, mrow, -1.0, mrow,
                                                     ALU.mult, ALU.mult), inc=chV)
    em.op("vector", lambda e: e.scalar_tensor_tensor(rrow, psrow[32:33, :], 1.0 / C,
                                                     rrow, ALU.mult, ALU.add),
          inc=chV)
    em.op("scalar", lambda e: e.activation(rrow, rrow, AF.Ln, bias=EPS),
          waits=[chV], inc=chS)
    em.op("scalar", lambda e: e.activation(rrow, rrow, AF.Exp, scale=-0.5), inc=chS)
    em.op("vector", lambda e: e.scalar_tensor_tensor(mrow, mrow, -1.0, rrow,
                                                     ALU.mult, ALU.mult),
          waits=[chS], inc=chV)


def phase0(em, nc, din, dout, dsc, sb, ps, fa, bfa):
    em.op("vector", lambda e: e.memset(sb["cvals"][:, 0:1], EPS), inc="pm_v")
    em.op("vector", lambda e: e.memset(sb["cvals"][:, 1:2], -0.5), inc="pm_v")
    em.op("vector", lambda e: e.memset(sb["cvals"][:, 2:3], 1.0 / 127.0), inc="pm_v")
    if P0MASK & 1:
        for nm in ["convw", "convb", "dtb", "ddd", "tdec", "tfir", "tmk", "tmv",
                   "tmr", "identb", "onesf"]:
            em.dma("sync", sb[nm][:, :], din[nm].ap(), inc="w_ld")
        em.dma("sync", sb["xpw"][:, :], din["xpwT"].ap(), inc="w_ld")
        em.dma("sync", sb["dtw"][:, :], din["dtwT"].ap(), inc="w_ld")
        em.dma("sync", fa(P0_WST, 2048), din["ipwT"].ap(), inc="w_ld")
        em.dma("sync", fa(P0_WST + 2048, 1024), din["opwT"].ap(), inc="w_ld")
        em.dma("sync", fa(P0_WST + 3072, 128), din["alog"].ap(), inc="w_ld")
    if P0MASK & 2:
        em.op("scalar", lambda e: e.activation(sb["ipw"][:, :], fa(P0_WST, 2048),
                                               AF.Copy),
              waits=["w_ld"], inc="pm_s")
        em.op("scalar", lambda e: e.activation(sb["opw"][:, :], fa(P0_WST + 2048, 1024),
                                               AF.Copy), inc="pm_s")
    if P0MASK & 4:
        em.op("scalar", lambda e: e.activation(sb["acoef"][:, :], fa(P0_WST + 3072, 128),
                                               AF.Exp),
              waits=["w_ld"], inc="pm_s")
        em.op("vector", lambda e: e.tensor_scalar(sb["acoef"][:, :], sb["acoef"][:, :],
                                                  -1.0, None, ALU.mult),
              waits=["pm_s"], inc="pm_v")
        em.op("scalar", lambda e: e.activation(sb["wdec"][:, :], sb["tdec"][:, :],
                                               AF.Exp),
              inc="pm_s")
        em.op("vector", lambda e: e.tensor_scalar(sb["wdec"][:, :], sb["wdec"][:, :],
                                                  -1.0, None, ALU.mult),
              waits=["pm_s"], inc="pm_v")
        em.op("scalar", lambda e: e.activation(sb["ewc"][:, :], sb["wdec"][:, :],
                                               AF.Exp),
              waits=["pm_v"], inc="pm_s")
    if P0MASK & 8:
        em.op("vector", lambda e: e.memset(sb["carry"][:, :], 0.0), inc="pm_v")
        em.op("vector", lambda e: e.memset(sb["mxc"][:, :], 0.0), inc="pm_v")
        em.op("vector", lambda e: e.memset(sb["colz"][:, :], 0.0), inc="pm_v")
        em.op("vector", lambda e: e.memset(sb["xhalo"][:, :], 0.0), inc="pm_v")
    if P0PART < 1:
        return

    def pick(e, core, then):
        g = core // 4
        for ct in range(4):
            e.tensor_copy(sb["wkvp"][:, ct:ct + 1], sb["tfir"][:, 4 * g + ct:4 * g + ct + 1])
        for ct in range(4):
            ins = e.tensor_copy(sb["wkvp"][:, 4 + ct:5 + ct],
                                sb["ewc"][:, 4 * g + ct:4 * g + ct + 1])
        then(ins)

    em.switch("vector", pick, waits=["pm_s"], inc="pm_v", nincs=1, kper=1)
    vp(em, sb, sb["wkvp"][:, 0:8], "pm_v", 128, 8)

    def locbody(e, core, then):
        g = core // 4
        then(e.dma_start(out=dsc["xhh"].ap(),
                         in_=din["xbT"].ap()[g * 512:(g + 1) * 512, :]))
        then(e.dma_start(out=dsc["xth"].ap(),
                         in_=din["xbT"].ap()[:, g * THALF:(g + 1) * THALF]))

    em.switch("gpsimd", locbody, waits=(), inc="loc_wr", nincs=2)

    if P0PART < 2:
        return
    hist_v = {}
    for tc in range(8):
        xch = "x_ld" if tc % 2 == 0 else "x_ldb"
        xoff = P0_XC + (tc % 2) * 4096
        for i in range(8):
            dst = fa(xoff + i * 512, 512)
            em.dma("sync", dst,
                   din["xbT"].ap()[i * 128:(i + 1) * 128, tc * 512:(tc + 1) * 512],
                   waits=[("st_v", hist_v.get(tc - 2, 0))], inc=xch)
        hh0 = P0_HH + (tc % 2) * 2048
        for j in range(4):
            em.dma("sync", fa(hh0 + j * 512, 512),
                   dsc["xhh"].ap()[j * 128:(j + 1) * 128, tc * 512:(tc + 1) * 512],
                   waits=[("loc_wr", 32), ("st_v", hist_v.get(tc - 2, 0))], inc=xch)
        sqt = {}
        for i in range(8):
            dst = fa(xoff + i * 512, 512)
            sq = fa(P0_SQ + (i % 2) * 512, 512)
            em.op("scalar", lambda e, dst=dst, sq=sq: e.activation(sq, dst, AF.Square),
                  waits=[xch, ("st_t", sqt.get(i - 2, 0))], inc="st_s")
            em.op("tensor", lambda e, dst=dst, i=i: e.matmul(
                ps["pa"][0:1, :], sb["onesf"][:, :], dst, start=(i == 0), stop=(i == 7)),
                waits=[xch, "w_ld", ("st_v", em.val("st_v"))], inc="st_t")
            em.op("tensor", lambda e, sq=sq, i=i: e.matmul(
                ps["pa"][32:33, :], sb["onesf"][:, :], sq, start=(i == 0), stop=(i == 7)),
                waits=["st_s"], inc="st_t")
            sqt[i] = em.val("st_t")
        ln_rows(em, sb, fa, ps["pa"], P0_RW, "st_s", "st_v", "st_t",
                extraw=[("row_wr", em.val("row_wr"))])
        em.dma("gpsimd", dsc["lnrow"].ap()[0:1, tc * 512:(tc + 1) * 512],
               fa(P0_RW, 512)[0:1, :], waits=["st_v"], inc="row_wr")
        em.dma("gpsimd", dsc["lnrow"].ap()[1:2, tc * 512:(tc + 1) * 512],
               fa(P0_RW + 512, 512)[0:1, :], inc="row_wr")
        lr = P0_LR + (tc % 2) * 1024
        rch = "rep0" if tc % 2 == 0 else "rep0b"
        em.dma("gpsimd", fa(lr, 512),
               dsc["lnrow"].ap()[0:1, tc * 512:(tc + 1) * 512].to_broadcast([128, 512]),
               waits=["row_wr", ("st_v", hist_v.get(tc - 2, 0))], inc=rch)
        em.dma("gpsimd", fa(lr + 512, 512),
               dsc["lnrow"].ap()[1:2, tc * 512:(tc + 1) * 512].to_broadcast([128, 512]),
               inc=rch)
        for j in range(4):
            src = fa(hh0 + j * 512, 512)
            dst = bfa(B1_XNH + j * 4096 + tc * 512, 512)
            em.op("vector", lambda e, src=src, dst=dst, lr=lr: e.tensor_tensor(
                dst, src, fa(lr, 512), ALU.mult),
                waits=[rch, xch], inc="st_v")
            em.op("vector", lambda e, dst=dst, lr=lr: e.tensor_tensor(
                dst, dst, fa(lr + 512, 512), ALU.add), inc="st_v")
        hist_v[tc] = em.val("st_v")

    if P0PART < 3:
        return
    def rowbody(e, core, then):
        g = core // 4
        then(e.dma_start(out=dsc["lnrowth"].ap(),
                         in_=dsc["lnrow"].ap()[:, g * THALF:(g + 1) * THALF]))

    em.switch("gpsimd", rowbody, waits=["row_wr"], inc="row_wr", nincs=1)
    for tc in range(4):
        xch = "x_ld" if tc % 2 == 0 else "x_ldb"
        rch = "rep0" if tc % 2 == 0 else "rep0b"
        xoff = P0_XC + (tc % 2) * 4096
        for i in range(8):
            em.dma("sync", fa(xoff + i * 512, 512),
                   dsc["xth"].ap()[i * 128:(i + 1) * 128, tc * 512:(tc + 1) * 512],
                   waits=[("loc_wr", 32), ("st_v", hist_v.get(8 + tc - 2, hist_v[7]))],
                   inc=xch)
        lr = P0_LR + (tc % 2) * 1024
        em.dma("gpsimd", fa(lr, 512),
               dsc["lnrowth"].ap()[0:1, tc * 512:(tc + 1) * 512].to_broadcast([128, 512]),
               waits=["row_wr", ("st_v", hist_v.get(8 + tc - 2, hist_v[7]))],
               inc=rch)
        em.dma("gpsimd", fa(lr + 512, 512),
               dsc["lnrowth"].ap()[1:2, tc * 512:(tc + 1) * 512].to_broadcast([128, 512]),
               inc=rch)
        for i in range(8):
            src = fa(xoff + i * 512, 512)
            th = bfa(B1_XNT + i * 2048 + tc * 512, 512)
            em.op("vector", lambda e, src=src, th=th, lr=lr: e.tensor_tensor(
                th, src, fa(lr, 512), ALU.mult), waits=[rch, xch], inc="st_v")
            em.op("vector", lambda e, th=th, lr=lr: e.tensor_tensor(
                th, th, fa(lr + 512, 512), ALU.add), inc="st_v")
        hist_v[8 + tc] = em.val("st_v")
    if DEBUG:
        em.dma("gpsimd", dsc["dbgb"].ap()[5], bfa(B1_XNT, 2048), waits=["st_v"],
               inc="dbg_wr")
        em.dma("gpsimd", dsc["dbgb"].ap()[6], bfa(B1_XNH, 2048), waits=["st_v"],
               inc="dbg_wr")


def phase1(em, nc, din, dout, dsc, sb, ps, fa, bfa):
    unit = [0]
    it = 0
    for h in range(NHPC):
        for ck in range(NCH):
            if it < P1HCK:
                l0_pre(em, din, dsc, sb, ps, fa, bfa, h, ck)
                if P1SCAN:
                    l0_scan(em, dsc, sb, ps, fa, bfa, h, ck)
                it += 1
            for _ in range(6):
                if unit[0] < min(96, P1U):
                    ffn0_unit(em, din, dsc, sb, ps, fa, bfa, unit[0])
                    unit[0] += 1
    while unit[0] < min(96, P1U):
        ffn0_unit(em, din, dsc, sb, ps, fa, bfa, unit[0])
        unit[0] += 1


def l0_pre(em, din, dsc, sb, ps, fa, bfa, h, ck):
    t0 = ck * TC1
    war = [("sc_v", em.val("sc_v")), ("epi_v", em.val("epi_v")),
           ("sc_s", em.val("sc_s"))]
    if P1SKIP:
        if P1VAR >= 6:
            dtbcol = sb["dtb"][:, 0:1]
            dsl = fa(P1_DL, 512)
            if P1VAR == 6:
                em.op("vector", lambda e, dsl=dsl, dtbcol=dtbcol: e.tensor_scalar(
                    dsl, ps["pa"][:, :], dtbcol, 0.0, ALU.add, ALU.max),
                    waits=[("pm_v", em.val("pm_v"))], inc="pre_v")
            elif P1VAR == 7:
                em.op("vector", lambda e, dsl=dsl, dtbcol=dtbcol: e.tensor_scalar(
                    dsl, fa(P1_DB, 512), dtbcol, 0.0, ALU.add, ALU.max),
                    waits=[("pm_v", em.val("pm_v"))], inc="pre_v")
            elif P1VAR == 8:
                em.op("vector", lambda e, dsl=dsl: e.tensor_copy(
                    dsl, ps["pa"][:, :]),
                    waits=[("pm_v", em.val("pm_v"))], inc="pre_v")
            return
        nit = 1 if P1VAR >= 3 else 2
        for dh in range(nit):
            for tc in range(nit):
                if P1VAR != 5:
                    em.op("tensor", lambda e, dh=dh, tc=tc: e.matmul(
                        ps["pa"][:, :],
                        sb["dtw"][:, (h * 2 + dh) * 128:(h * 2 + dh) * 128 + 128],
                        fa(P1_DB + tc * 512, 512)[0:RNK, :], start=True, stop=True),
                        waits=["pre_s", ("pre_v", em.val("pre_v"))], inc="pre_t")
                dtbcol = sb["dtb"][:, h * 2 + dh:h * 2 + dh + 1]
                dsl = fa(P1_DL + dh * 1024 + tc * 512, 512)
                if P1VAR != 4:
                    em.op("vector", lambda e, dsl=dsl, dtbcol=dtbcol: e.tensor_scalar(
                        dsl, ps["pa"][:, :], dtbcol, 0.0, ALU.add, ALU.max),
                        waits=["pre_t", ("sc_s", em.val("sc_s"))], inc="pre_v")
                if P1PART >= 6:
                    em.op("scalar", lambda e, dtbcol=dtbcol: e.activation(
                        fa(P1_T0, 512), ps["pa"][:, :], AF.Abs, bias=dtbcol),
                        waits=["pre_t"], inc="pre_s")
        return
    for e4 in range(4):
        for nh in range(2):
            pst = ps["pa"] if (e4 * 2 + nh) % 2 == 0 else ps["pb"]
            em.op("tensor", lambda e, e4=e4, nh=nh, pst=pst: e.matmul(
                pst[:, :], sb["ipw"][:, h * 512 + e4 * 128:h * 512 + (e4 + 1) * 128],
                bfa(B1_XNH + h * 4096 + t0 + nh * 512, 512), start=True, stop=True),
                waits=["pm_v", ("st_v", em.val("st_v")), ("pre_s", em.val("pre_s")),
                       ("pre_v", em.val("pre_v"))],
                inc="pre_t")
            if e4 < 2:
                dst = fa(P1_U + e4 * 1027 + 3 + nh * 512, 512)
                em.op("scalar", lambda e, dst=dst, pst=pst: e.activation(
                    dst, pst[:, :], AF.Copy), waits=["pre_t"] + war, inc="pre_s")
            else:
                dst = fa(P1_SZ + (e4 - 2) * 1024 + nh * 512, 512)
                em.op("scalar", lambda e, dst=dst, pst=pst: e.activation(
                    dst, pst[:, :], AF.Silu), waits=["pre_t"] + war, inc="pre_s")
    if P1PART < 2:
        return
    for dh in range(2):
        if ck == 0:
            em.op("vector", lambda e, dh=dh: e.memset(fa(P1_U + dh * 1027, 3), 0.0),
                  waits=[("pre_s", em.val("pre_s"))], inc="pre_v")
        else:
            em.op("vector", lambda e, dh=dh: e.tensor_copy(
                fa(P1_U + dh * 1027, 3), sb["colz"][:, 8 + dh * 3:11 + dh * 3]),
                waits=[("pre_s", em.val("pre_s"))], inc="pre_v")
    for dh in range(2):
        ub = P1_U + dh * 1027
        ucd = fa(P1_UC + dh * 1024, 1024)
        cwcols = (h * 2 + dh) * 4
        cbcol = sb["convb"][:, h * 2 + dh:h * 2 + dh + 1]
        em.op("vector", lambda e, ub=ub, ucd=ucd, cwcols=cwcols, cbcol=cbcol:
              e.tensor_scalar(ucd, fa(ub, 1024), sb["convw"][:, cwcols:cwcols + 1],
                              cbcol, ALU.mult, ALU.add),
              waits=["pre_s"], inc="pre_v")
        for k in range(1, KC):
            em.op("vector", lambda e, k=k, ub=ub, ucd=ucd, cwcols=cwcols:
                  e.scalar_tensor_tensor(ucd, fa(ub + k, 1024),
                                         sb["convw"][:, cwcols + k:cwcols + k + 1],
                                         ucd, ALU.mult, ALU.add), inc="pre_v")
        if ck < NCH - 1:
            em.op("vector", lambda e, ub=ub, dh=dh: e.tensor_copy(
                sb["colz"][:, 8 + dh * 3:11 + dh * 3], fa(ub + 1024, 3)), inc="pre_v")
        em.op("scalar", lambda e, ucd=ucd: e.activation(ucd, ucd, AF.Silu),
              waits=["pre_v"], inc="pre_s")
    if P1PART < 3:
        return
    for tc in range(2):
        for dh in range(2):
            em.op("tensor", lambda e, tc=tc, dh=dh: e.matmul(
                ps["pb"][0:40, :], sb["xpw"][:, (h * 2 + dh) * 40:(h * 2 + dh) * 40 + 40],
                fa(P1_UC + dh * 1024 + tc * 512, 512), start=(dh == 0), stop=(dh == 1)),
                waits=["pre_s"], inc="pre_t")
        em.op("scalar", lambda e, tc=tc: e.activation(
            fa(P1_DB + tc * 512, 512)[0:40, :], ps["pb"][0:40, :], AF.Copy),
            waits=["pre_t"], inc="pre_s")
    if P1PART < 4:
        return
    em.op("scalar", lambda e: e.activation(sb["bcst"][0:40, :],
                                           fa(P1_DB, 1024)[0:40, :], AF.Copy),
          waits=["pre_s", ("brc_wr", em.val("brc_wr"))], inc="pre_s")
    em.dma("gpsimd", dsc["brc"].ap()[h, :, t0:t0 + TC1], sb["bcst"][8:40, :],
           waits=["pre_s"], inc="brc_wr")
    if P1PART < 5:
        return
    for dh in range(2):
        for tc in range(2):
            em.op("tensor", lambda e, dh=dh, tc=tc: e.matmul(
                ps["pa"][:, :], sb["dtw"][:, (h * 2 + dh) * 128:(h * 2 + dh) * 128 + 128],
                fa(P1_DB + tc * 512, 512)[0:RNK, :], start=True, stop=True),
                waits=["pre_s", ("pre_v", em.val("pre_v"))], inc="pre_t")
            dtbcol = sb["dtb"][:, h * 2 + dh:h * 2 + dh + 1]
            dsl = fa(P1_DL + dh * 1024 + tc * 512, 512)
            em.op("vector", lambda e, dtbcol=dtbcol: e.tensor_scalar(
                fa(P1_T1, 512), ps["pa"][:, :], dtbcol, None, ALU.add),
                waits=["pre_t", ("pre_s", em.val("pre_s")),
                       ("sc_s", em.val("sc_s"))], inc="pre_v")
            em.op("vector", lambda e, dsl=dsl: e.tensor_scalar(
                dsl, fa(P1_T1, 512), 0.0, None, ALU.max), inc="pre_v")
            if P1PART < 6:
                continue
            em.op("scalar", lambda e: e.activation(
                fa(P1_T0, 512), fa(P1_T1, 512), AF.Abs),
                waits=[("pre_v", em.val("pre_v"))], inc="pre_s")
            if P1PART >= 7:
                em.op("scalar", lambda e: e.activation(fa(P1_T1, 512), fa(P1_T0, 512),
                                                       AF.Exp, scale=-1.0),
                      waits=[("pre_v", em.val("pre_v"))], inc="pre_s")
            if P1PART >= 8:
                em.op("scalar", lambda e: e.activation(fa(P1_T1, 512), fa(P1_T1, 512),
                                                       AF.Ln, bias=1.0), inc="pre_s")
            if P1PART < 9:
                continue
            em.op("vector", lambda e, dsl=dsl: e.tensor_tensor(
                dsl, dsl, fa(P1_T1, 512), ALU.add), waits=["pre_s"], inc="pre_v")
        if P1PART < 9:
            continue
        em.op("vector", lambda e, dh=dh: e.tensor_tensor(
            bfa(B1_DU + dh * 1024, 1024), fa(P1_DL + dh * 1024, 1024),
            fa(P1_UC + dh * 1024, 1024), ALU.mult), inc="pre_v")
    if DEBUG and h == 0 and ck == 0:
        em.dma("gpsimd", dsc["dbg"].ap()[5][:, 0:128], sb["acoef"][:, :],
               waits=["pre_v"], inc="dbg_wr")
        em.dma("gpsimd", dsc["dbg"].ap()[0], fa(P1_DL, 2048), waits=["pre_v"],
               inc="dbg_wr")
        em.dma("gpsimd", dsc["dbg"].ap()[1], fa(P1_UC, 2048), waits=["pre_v"],
               inc="dbg_wr")
        em.dma("gpsimd", dsc["dbg"].ap()[2], fa(P1_SZ, 2048), waits=["pre_v"],
               inc="dbg_wr")
        em.dma("gpsimd", dsc["dbgb"].ap()[0], bfa(B1_DU, 2048), waits=["pre_v"],
               inc="dbg_wr")


def l0_scan(em, dsc, sb, ps, fa, bfa, h, ck):
    t0 = ck * TC1
    tbase = em.val("sc_t")
    da_guard = {0: 0, 1: 0}
    rep_guard = {0: 0, 1: 0}
    for n in range(NST):
        par = n % 2
        rch = "rep_ld" if par == 0 else "rep_ldb"
        em.dma("gpsimd", bfa(B1_BR + par * 1024, 1024),
               dsc["brc"].ap()[h, n:n + 1, t0:t0 + TC1].to_broadcast([128, TC1]),
               waits=["brc_wr", ("sc_v", rep_guard[par])],
               inc=rch)
        em.dma("gpsimd", bfa(B1_CR + par * 1024, 1024),
               dsc["brc"].ap()[h, NST + n:NST + n + 1, t0:t0 + TC1]
               .to_broadcast([128, TC1]),
               inc=rch)
        for dh in range(2):
            yps = ps["py0"] if dh == 0 else ps["py1"]
            acol = sb["acoef"][:, (h * 2 + dh) * NST + n:(h * 2 + dh) * NST + n + 1]
            da = fa(P1_DA + dh * 1024, 1024)
            em.op("scalar", lambda e, da=da, acol=acol, dh=dh: e.activation(
                da, fa(P1_DL + dh * 1024, 1024), AF.Exp, scale=acol),
                waits=["pre_v", ("sc_v", da_guard[dh])], inc="sc_s")
            xt = bfa(B1_X, 1024)
            ht = bfa(B1_H, 1024)
            pr = bfa(B1_PR + (par * 2 + dh) * 1024, 1024)
            em.op("vector", lambda e, xt=xt, dh=dh, par=par: e.tensor_tensor(
                xt, bfa(B1_DU + dh * 1024, 1024), bfa(B1_BR + par * 1024, 1024),
                ALU.mult),
                waits=[rch, "sc_s", "pre_v"], inc="sc_v")
            init = 0.0 if ck == 0 else sb["carry"][:, dh * NST + n:dh * NST + n + 1]
            em.op("vector", lambda e, ht=ht, da=da, xt=xt, init=init:
                  e.tensor_tensor_scan(ht, da, xt, init, ALU.mult, ALU.add),
                  inc="sc_v")
            da_guard[dh] = em.val("sc_v")
            prw = [("sc_t", tbase + 4 * (n - 1))] if n >= 2 else []
            em.op("vector", lambda e, pr=pr, ht=ht, par=par: e.tensor_tensor(
                pr, ht, bfa(B1_CR + par * 1024, 1024), ALU.mult),
                waits=prw, inc="sc_v")
            if ck < NCH - 1:
                em.op("vector", lambda e, ht=ht, dh=dh, n=n: e.tensor_copy(
                    sb["carry"][:, dh * NST + n:dh * NST + n + 1], ht[:, TC1 - 1:TC1]),
                    inc="sc_v")
            if DEBUG and h == 0 and ck == 0 and dh == 0 and n == 0:
                em.dma("gpsimd", dsc["dbg"].ap()[3][:, 0:1024], da,
                       waits=["sc_v"], inc="dbg_wr")
                em.dma("gpsimd", dsc["dbgb"].ap()[1][:, 0:1024], xt,
                       waits=["sc_v"], inc="dbg_wr")
                em.dma("gpsimd", dsc["dbgb"].ap()[2][:, 0:1024], ht,
                       waits=["sc_v"], inc="dbg_wr")
                em.dma("gpsimd", dsc["dbgb"].ap()[3][:, 0:1024], pr,
                       waits=["sc_v"], inc="dbg_wr")
            if DEBUG and h == 0 and ck == 0 and n <= 2:
                em.op("vector", lambda e: e.engine_nop(), waits=["dbg_wr"],
                      inc="sc_v")
            for q in range(2):
                em.op("tensor", lambda e, yps=yps, pr=pr, q=q, n=n: e.matmul(
                    yps[:, q * 512:(q + 1) * 512], sb["identb"][:, :],
                    pr[:, q * 512:(q + 1) * 512], start=(n == 0), stop=(n == NST - 1)),
                    waits=["sc_v"], inc="sc_t")
        rep_guard[par] = em.val("sc_v")
    for dh in range(2):
        yps = ps["py0"] if dh == 0 else ps["py1"]
        dcol = sb["ddd"][:, h * 2 + dh:h * 2 + dh + 1]
        yg = bfa(B1_YG + dh * 1024, 1024)
        em.op("vector", lambda e, dh=dh, dcol=dcol, yps=yps: e.scalar_tensor_tensor(
            fa(P1_EP, 1024), fa(P1_UC + dh * 1024, 1024), dcol, yps[:, :],
            ALU.mult, ALU.add),
            waits=["sc_t", ("epi_t", em.val("epi_t"))], inc="epi_v")
        em.op("vector", lambda e, dh=dh, yg=yg: e.tensor_tensor(
            yg, fa(P1_EP, 1024), fa(P1_SZ + dh * 1024, 1024), ALU.mult),
            inc="epi_v")
        if DEBUG and h == 0 and ck == 0 and dh == 0:
            em.dma("gpsimd", dsc["dbg"].ap()[4][:, 0:1024], fa(P1_EP, 1024),
                   waits=["epi_v"], inc="dbg_wr")
            em.dma("gpsimd", dsc["dbgb"].ap()[4][:, 0:1024], yg,
                   waits=["epi_v"], inc="dbg_wr")
            em.op("vector", lambda e: e.engine_nop(), waits=["dbg_wr"], inc="epi_v")
        for q in range(2):
            em.op("tensor", lambda e, dh=dh, q=q, yg=yg: e.matmul(
                ps["pc" if q == 0 else "pd"][:, :],
                sb["opw"][:, (h * 2 + dh) * 128:(h * 2 + dh) * 128 + 128],
                yg[:, q * 512:q * 512 + 512], start=(dh == 0), stop=(dh == 1)),
                waits=["epi_v", ("f0_v", em.val("f0_v")), ("epi_s", em.val("epi_s"))],
                inc="epi_t")
    for q in range(2):
        em.op("scalar", lambda e, q=q: e.activation(
            fa(P1_SS + q * 512, 512), ps["pc" if q == 0 else "pd"][:, :], AF.Copy),
            waits=["epi_t", ("s_wr", em.val("s_wr"))], inc="epi_s")
    em.dma("gpsimd", dsc["sagi"].ap()[h * 128:(h + 1) * 128, t0:t0 + TC1],
           fa(P1_SS, 1024), waits=["epi_s"], inc="s_wr")


def ffn0_unit(em, din, dsc, sb, ps, fa, bfa, unit):
    hidc, tc = unit // 4, unit % 4
    if tc == 0:
        em.dma("sync", sb["w0slab"][:, 0:1024], din["wg0"].ap()[hidc],
               waits=[("f0_s", em.val("f0_s"))], inc="f0_ld")
        em.op("scalar", lambda e: e.activation(sb["w0bf"][:, 0:1024],
                                               sb["w0slab"][:, 0:1024],
                                               AF.Copy, scale=1.0 / 127.0),
              waits=["f0_ld", ("pm_v", 3), ("f0_t", em.val("f0_t"))], inc="f0_s")
        em.dma("sync", sb["w0slab"][:, 0:1024], din["wu0"].ap()[hidc],
               waits=[("f0_s", em.val("f0_s"))], inc="f0_ld")
        em.op("scalar", lambda e: e.activation(sb["w0bf"][:, 1024:2048],
                                               sb["w0slab"][:, 0:1024],
                                               AF.Copy, scale=1.0 / 127.0),
              waits=["f0_ld", ("f0_t", em.val("f0_t"))], inc="f0_s")
    for d in range(8):
        mv = bfa(B1_XNT + d * 2048 + tc * 512, 512)
        em.op("tensor", lambda e, d=d, mv=mv: e.matmul(
            ps["pc"][:, :], sb["w0bf"][:, d * 128:(d + 1) * 128], mv,
            start=(d == 0), stop=(d == 7)),
            waits=["f0_s", ("f0_v", em.val("f0_v")), ("epi_s", em.val("epi_s")),
                   ("st_v", em.val("st_v"))],
            inc="f0_t")
    for d in range(8):
        mv = bfa(B1_XNT + d * 2048 + tc * 512, 512)
        em.op("tensor", lambda e, d=d, mv=mv: e.matmul(
            ps["pd"][:, :], sb["w0bf"][:, 1024 + d * 128:1024 + (d + 1) * 128], mv,
            start=(d == 0), stop=(d == 7)), inc="f0_t")
    em.op("scalar", lambda e: e.activation(fa(P1_F0, 512), ps["pc"][:, :], AF.Silu),
          waits=["f0_t"], inc="f0_s")
    em.op("vector", lambda e: e.tensor_tensor(bfa(B1_HF, 512), fa(P1_F0, 512),
                                              ps["pd"][:, :], ALU.mult),
          waits=["f0_s", ("hf_wr", em.val("hf_wr"))], inc="f0_v")
    em.dma("gpsimd", dsc["hfd"].ap()[hidc, :, tc * 512:(tc + 1) * 512], bfa(B1_HF, 512),
           waits=["f0_v"], inc="hf_wr")


def phase2(em, nc, din, dout, dsc, sb, ps, fa, bfa):
    p1rel = [("pre_t", em.val("pre_t")), ("f0_t", em.val("f0_t")),
             ("sc_t", em.val("sc_t")), ("sc_v", em.val("sc_v")),
             ("epi_t", em.val("epi_t")), ("f0_v", em.val("f0_v")),
             ("sc_s", em.val("sc_s")), ("epi_v", em.val("epi_v"))]
    em.coll("AllGather", ALU.bypass, RG8, [dsc["sagi"].ap().opt()],
            [dsc["sago"].ap().opt()], waits=["s_wr"], inc="cc")
    for hc in range(24):
        em.dma("sync", sb["w0slab"][:, 0:1024], din["wd0"].ap()[hc],
               waits=[("g_s", em.val("g_s")), ("f0_t", em.val("f0_t"))],
               inc="f0_ld")
        em.op("scalar", lambda e, hc=hc: e.activation(
            bfa(B2_WD + hc * 1024, 1024), sb["w0slab"][:, 0:1024], AF.Copy,
            scale=1.0 / 127.0), waits=["f0_ld", ("pm_v", 3)] + p1rel, inc="g_s")
    if DEBUG:
        em.dma("gpsimd", dsc["dbgb"].ap()[7][:, 0:1024], bfa(B2_WD, 1024),
               waits=["g_s"], inc="dbg_wr")
    lp = lpbq_all(em, din, dsc, sb, fa, bfa, p1rel)
    for tc in range(4):
        for hc in range(24):
            em.dma("sync", bfa(B2_HF + hc * 512, 512),
                   dsc["hfd"].ap()[hc, :, tc * 512:(tc + 1) * 512],
                   waits=["hf_wr", ("g_t", tc * 192)] + p1rel, inc="hf_ld")
        if DEBUG and tc == 0:
            em.dma("gpsimd", dsc["dbgb"].ap()[7][:, 1024:1536], bfa(B2_HF, 512),
                   waits=["hf_ld"], inc="dbg_wr")
        for d in range(8):
            pst = ps["pc"] if d % 2 == 0 else ps["pd"]
            for hc in range(24):
                dbw = ["dbg_wr"] if DEBUG else []
                em.op("tensor", lambda e, d=d, hc=hc, pst=pst: e.matmul(
                    pst[:, :], bfa(B2_WD + hc * 1024 + d * 128, 128),
                    bfa(B2_HF + hc * 512, 512), start=(hc == 0), stop=(hc == 23)),
                    waits=["g_s", "hf_ld", ("g_v", em.val("g_v"))] + dbw, inc="g_t")
            em.op("scalar", lambda e, d=d, pst=pst: e.activation(
                fa(P2_G, 512), pst[:, :], AF.Silu),
                waits=["g_t", ("c_wr", em.val("c_wr"))], inc="g_s")
            em.dma("gpsimd", dsc["gd"].ap()[d, :, tc * 512:(tc + 1) * 512],
                   fa(P2_G, 512), waits=["g_s"], inc="c_wr")
        next(lp, None)
        next(lp, None)
    for _ in lp:
        pass
    def sbody(e, core, then):
        g = core // 4
        b = core % 4
        for rk, half in ((b, 0), (b + 4, 1)):
            ins = e.dma_start(
                out=dsc["sloc"].ap()[half * 512:(half + 1) * 512, :],
                in_=dsc["sago"].ap()[rk * 512:(rk + 1) * 512,
                                     g * THALF:(g + 1) * THALF])
            then(ins)

    em.switch("gpsimd", sbody, waits=["cc"], inc="loc_wr", nincs=2)
    sw_done = em.val("loc_wr")
    for tc in range(4):
        for d in range(8):
            em.dma("sync", fa(P2_X + d * 512, 512),
                   dsc["xth"].ap()[d * 128:(d + 1) * 128, tc * 512:(tc + 1) * 512],
                   waits=[("c_v", em.val("c_v")), ("loc_wr", sw_done)] + p1rel,
                   inc="c_ld")
        for d in range(8):
            em.dma("sync", fa(P2_S + d * 512, 512),
                   dsc["sloc"].ap()[d * 128:(d + 1) * 128, tc * 512:(tc + 1) * 512],
                   waits=[("loc_wr", sw_done), ("c_v", em.val("c_v"))], inc="c_ld")
        for d in range(8):
            em.dma("sync", fa(P2_X1 + d * 512, 512),
                   dsc["gd"].ap()[d, :, tc * 512:(tc + 1) * 512],
                   waits=["c_wr", ("c_v", em.val("c_v"))], inc="c_ld")
        for d in range(8):
            em.op("vector", lambda e, d=d: e.scalar_tensor_tensor(
                fa(P2_X1 + d * 512, 512), fa(P2_S + d * 512, 512), 1.0,
                fa(P2_X1 + d * 512, 512), ALU.mult, ALU.mult),
                waits=["c_ld"], inc="c_v")
            em.op("vector", lambda e, d=d: e.tensor_tensor(
                fa(P2_X1 + d * 512, 512), fa(P2_X1 + d * 512, 512),
                fa(P2_X + d * 512, 512), ALU.add), inc="c_v")
            em.dma("gpsimd", dsc["x1agi"].ap()[d * 128:(d + 1) * 128,
                                               tc * 512:(tc + 1) * 512],
                   fa(P2_X1 + d * 512, 512), waits=["c_v"], inc="c_wr")
    em.coll("AllGather", ALU.bypass, RG8, [dsc["x1agi"].ap().opt()],
            [dsc["x1ago"].ap().opt()], waits=["c_wr"], inc="cc")


def lpbq_all(em, din, dsc, sb, fa, bfa, p1rel):
    for wnm, qnm, ntile, icols in (("wg1", "wqg", 24, C), ("wu1", "wqu", 24, C),
                                   ("wd1", "wqd", 8, HID)):
        for ot in range(ntile):
            nb = icols // 32
            w = fa(P2_LW, icols)
            wv = w.rearrange("p (nb b) -> p nb b", b=32)
            smax = fa(P2_LS, nb)
            em.dma("sync", w, din[wnm].ap()[ot * 128:(ot + 1) * 128, :],
                   waits=[("q2_wr", em.val("q2_wr"))] + p1rel,
                   inc="q2_ld")
            em.op("vector", lambda e, wv=wv, smax=smax: e.tensor_reduce(
                smax, wv, AX.X, ALU.max, apply_absolute_value=True),
                waits=["q2_ld"], inc="q2_v")
            vp(em, sb, smax, "q2_v", 128, nb)
            em.op("vector", lambda e, smax=smax: e.tensor_scalar(
                smax, smax, 1.0 / 7.0, 1e-12, ALU.mult, ALU.max), inc="q2_v")
            vp(em, sb, smax, "q2_v", 128, nb)
            srec = fa(P2_LS + 128, nb)
            em.op("vector", lambda e, srec=srec, smax=smax: e.reciprocal(srec, smax),
                  inc="q2_v")
            vp(em, sb, srec, "q2_v", 128, nb)
            sbb = smax.unsqueeze(2).to_broadcast([128, nb, 32])
            srb = srec.unsqueeze(2).to_broadcast([128, nb, 32])
            em.op("vector", lambda e, wv=wv, srb=srb: e.tensor_tensor(
                wv, wv, srb, ALU.mult), inc="q2_v")
            for cchunk in range(icols // 512):
                c0 = cchunk * 512
                em.op("vector", lambda e, c0=c0, w=w: e.tensor_scalar(
                    w[:, c0:c0 + 512], w[:, c0:c0 + 512], RC, -RC, ALU.add, ALU.add),
                    inc="q2_v")
                em.op("vector", lambda e, c0=c0, w=w: e.tensor_copy(
                    sb["i0"][:, :], w[:, c0:c0 + 512]), inc="q2_v")
                em.op("vector", lambda e, c0=c0, w=w: e.tensor_copy(
                    w[:, c0:c0 + 512], sb["i0"][:, :]), inc="q2_v")
            em.op("vector", lambda e, wv=wv, sbb=sbb: e.tensor_tensor(
                bfa(B2_HF + 12288, icols).rearrange("p (nb b) -> p nb b", b=32),
                wv, sbb, ALU.mult), inc="q2_v")
            em.dma("gpsimd", dsc[qnm].ap()[ot * 128:(ot + 1) * 128, :],
                   bfa(B2_HF + 12288, icols), waits=["q2_v"], inc="q2_wr")
            yield


def phase3(em, nc, din, dout, dsc, sb, ps, fa, bfa):
    p2rel = [("g_t", em.val("g_t")), ("c_v", em.val("c_v")),
             ("q2_v", em.val("q2_v")), ("q2_wr", em.val("q2_wr")),
             ("g_s", em.val("g_s")), ("c_wr", em.val("c_wr"))]
    for nm, dst in [("wkT", B3_WK), ("wvT", B3_WV), ("wrT", B3_WR)]:
        for d in range(8):
            em.dma("sync", fa(P3_X1 + (d % 2) * 512, 512),
                   din[nm].ap()[d * 128:(d + 1) * 128, :],
                   waits=[("p3_s", em.val("p3_s"))] + p2rel, inc="p3_ld")
            em.op("scalar", lambda e, dst=dst, d=d: e.activation(
                bfa(dst + d * 512, 512), fa(P3_X1 + (d % 2) * 512, 512), AF.Copy),
                waits=["p3_ld"], inc="p3_s")
    for ct in range(4):
        em.dma("sync", fa(P3_X1 + (ct % 2) * 1024, 1024),
               din["woT"].ap()[ct * 128:(ct + 1) * 128, :],
               waits=[("p3_s", em.val("p3_s"))], inc="p3_ld")
        em.op("scalar", lambda e, ct=ct: e.activation(
            bfa(B3_WO + ct * 1024, 1024), fa(P3_X1 + (ct % 2) * 1024, 1024), AF.Copy),
            waits=["p3_ld"], inc="p3_s")
    em.op("vector", lambda e: e.memset(sb["pcarry"][:, :], 0.0),
          waits=[("p3_s", em.val("p3_s"))], inc="p3_v")
    vp(em, sb, sb["pcarry"][:, :], "p3_v", 128, 12)
    em.op("vector", lambda e: e.tensor_scalar(sb["pcarry"][:, 0:4],
                                              sb["pcarry"][:, 0:4], 0.0, -1e30,
                                              ALU.mult, ALU.add), inc="p3_v")
    vp(em, sb, sb["pcarry"][:, 0:4], "p3_v", 128, 4)

    def x1body(e, core, then):
        b = core % 4
        for half in range(2):
            rk = b + 4 * half
            ins = e.dma_start(
                out=dsc["x1loc"].ap()[:, half * THALF:(half + 1) * THALF],
                in_=dsc["x1ago"].ap()[rk * 1024:(rk + 1) * 1024, :])
            then(ins)

    em.switch("gpsimd", x1body, waits=["cc"], inc="loc_wr", nincs=2)
    x1loc_done = em.val("loc_wr")

    for ck in range(8):
        t0 = ck * 512
        for d in range(8):
            em.dma("sync", fa(P3_X1 + d * 513 + 1, 512),
                   dsc["x1loc"].ap()[d * 128:(d + 1) * 128, t0:t0 + 512],
                   waits=[("loc_wr", x1loc_done), ("p3_v", em.val("p3_v")),
                          ("p3_t", em.val("p3_t")), ("p3_s", em.val("p3_s"))],
                   inc="p3_ld")
        sqt = {}
        for d in range(8):
            xs = fa(P3_X1 + d * 513 + 1, 512)
            sq = fa(P3_XN + (d % 2) * 512, 512)
            em.op("tensor", lambda e, xs=xs, d=d: e.matmul(
                ps["pa"][0:1, :], sb["onesf"][:, :], xs, start=(d == 0), stop=(d == 7)),
                waits=["p3_ld"], inc="p3_t")
            em.op("scalar", lambda e, xs=xs, sq=sq: e.activation(sq, xs, AF.Square),
                  waits=["p3_ld", ("p3_t", sqt.get(d - 2, 0))], inc="p3_s")
            em.op("tensor", lambda e, sq=sq, d=d: e.matmul(
                ps["pa"][32:33, :], sb["onesf"][:, :], sq, start=(d == 0), stop=(d == 7)),
                waits=["p3_s"], inc="p3_t")
            sqt[d] = em.val("p3_t")
        ln_rows(em, sb, fa, ps["pa"], P3_RW, "p3_s", "p3_v", "p3_t",
                extraw=[("row3", em.val("row3"))])
        em.dma("gpsimd", dsc["lnrow3"].ap()[0:1, t0:t0 + 512], fa(P3_RW, 512)[0:1, :],
               waits=["p3_v"], inc="row3")
        em.dma("gpsimd", dsc["lnrow3"].ap()[1:2, t0:t0 + 512],
               fa(P3_RW + 512, 512)[0:1, :], inc="row3")
        em.dma("gpsimd", fa(P3_LR, 512),
               dsc["lnrow3"].ap()[0:1, t0:t0 + 512].to_broadcast([128, 512]),
               waits=["row3", ("p3_v", em.val("p3_v"))], inc="rep0")
        em.dma("gpsimd", fa(P3_LR + 512, 512),
               dsc["lnrow3"].ap()[1:2, t0:t0 + 512].to_broadcast([128, 512]),
               inc="rep0")
        for d in range(8):
            xs = fa(P3_X1 + d * 513 + 1, 512)
            xn = fa(P3_XN + d * 513 + 1, 512)
            em.op("vector", lambda e, xs=xs, xn=xn: e.tensor_tensor(
                xn, xs, fa(P3_LR, 512), ALU.mult), waits=["rep0"], inc="p3_v")
            em.op("vector", lambda e, xn=xn: e.tensor_tensor(
                xn, xn, fa(P3_LR + 512, 512), ALU.add), inc="p3_v")
            em.op("vector", lambda e, xn=xn: e.tensor_reduce(
                sb["mxc"][:, 1:2], xn, AX.X, ALU.max, apply_absolute_value=True),
                inc="p3_v")
            vp(em, sb, sb["mxc"][:, 1:2], "p3_v", 128, 1)
            em.op("vector", lambda e: e.tensor_tensor(
                sb["mxc"][:, 0:1], sb["mxc"][:, 0:1], sb["mxc"][:, 1:2], ALU.max),
                inc="p3_v")
            vp(em, sb, sb["mxc"][:, 0:1], "p3_v", 128, 1)
            em.op("vector", lambda e, d=d: e.tensor_copy(
                fa(P3_XN + d * 513, 1), sb["xhalo"][:, d:d + 1]), inc="p3_v")
            vp(em, sb, fa(P3_XN + d * 513, 1), "p3_v", 128, 1)
            for mixn, mixdst in (("tmk", B3_XK), ("tmv", B3_XV), ("tmr", B3_XR)):
                col = sb[mixn][:, d:d + 1]
                em.op("vector", lambda e, d=d, xn=xn: e.tensor_tensor(
                    fa(P3_Y, 512), xn, fa(P3_XN + d * 513, 512), ALU.subtract),
                    inc="p3_v")
                em.op("vector", lambda e, d=d, col=col, mixdst=mixdst: e.scalar_tensor_tensor(
                    bfa(mixdst + d * 512, 512), fa(P3_Y, 512), col,
                    fa(P3_XN + d * 513, 512), ALU.mult, ALU.add), inc="p3_v")
            em.op("vector", lambda e, d=d, xn=xn: e.tensor_copy(
                sb["xhalo"][:, d:d + 1], xn[:, 511:512]), inc="p3_v")
            vp(em, sb, sb["xhalo"][:, d:d + 1], "p3_v", 128, 1)
        for ct in range(4):
            for nm, wof, src, pst in (("k", B3_WK, B3_XK, "pb"),
                                      ("v", B3_WV, B3_XV, "pc"),
                                      ("r", B3_WR, B3_XR, "pd")):
                for d in range(8):
                    em.op("tensor", lambda e, wof=wof, ct=ct, d=d, src=src, pst=pst:
                          e.matmul(ps[pst][:, :],
                                   bfa(wof + d * 512 + ct * 128, 128),
                                   bfa(src + d * 512, 512),
                                   start=(d == 0), stop=(d == 7)),
                          waits=["p3_v", ("p3_s", em.val("p3_s"))], inc="p3_t")
                if nm == "k":
                    em.op("scalar", lambda e, ct=ct, pst=pst: e.activation(
                        fa(P3_K + ct * 512, 512), ps[pst][:, :], AF.Copy),
                        waits=["p3_t"], inc="p3_s")
                elif nm == "v":
                    em.op("scalar", lambda e, ct=ct, pst=pst: e.activation(
                        fa(P3_V + ct * 512, 512), ps[pst][:, :], AF.Copy),
                        waits=["p3_t"], inc="p3_s")
                else:
                    em.op("scalar", lambda e, ct=ct, pst=pst: e.activation(
                        fa(P3_SR + ct * 512, 512), ps[pst][:, :], AF.Sigmoid),
                        waits=["p3_t"], inc="p3_s")
        for ct in range(4):
            wkv_tile(em, sb, fa, bfa, ct)
        for od in range(8):
            for ct in range(4):
                em.op("tensor", lambda e, od=od, ct=ct: e.matmul(
                    ps["pa"][:, :],
                    bfa(B3_WO + ct * 1024 + od * 128, 128),
                    bfa(B3_Y1 + ct * 512, 512), start=(ct == 0), stop=(ct == 3)),
                    waits=["p3_v", ("p3_s", em.val("p3_s"))], inc="p3_t")
            em.op("scalar", lambda e, od=od: e.activation(
                bfa(B3_ST, 512), ps["pa"][:, :], AF.Copy),
                waits=["p3_t", ("p3_wr", em.val("p3_wr"))], inc="p3_s")
            em.dma("gpsimd", dsc["s1agi"].ap()[od * 128:(od + 1) * 128, t0:t0 + 512],
                   bfa(B3_ST, 512), waits=["p3_s"], inc="p3_wr")
    em.dma("gpsimd", dsc["pmax"].ap()[0:1, :], sb["mxc"][:, 0:1], waits=["p3_v"],
           inc="row3")
    em.dma("gpsimd", sb["rowz"][0:1, :],
           dsc["pmax"].ap()[0:1, :], waits=["row3"],
           inc="rep0")
    em.op("vector", lambda e: e.tensor_reduce(sb["colz"][0:1, 2:3], sb["rowz"][0:1, :],
                                              AX.X, ALU.max), waits=["rep0"], inc="p3_v")
    vp(em, sb, sb["colz"][0:1, 2:3], "p3_v", 1, 1)
    em.dma("gpsimd", dsc["ar1i"].ap()[0:1, 0:1], sb["colz"][0:1, 2:3],
           waits=["p3_v"], inc="row3")
    em.dma("gpsimd", dsc["ar1i"].ap()[0:1, 1:2], sb["colz"][0:1, 2:3], inc="row3")
    em.coll("AllReduce", ALU.max, RG8, [dsc["ar1i"].ap().opt()],
            [dsc["ar1o"].ap().opt()], waits=["row3"], inc="cc")
    em.coll("AllGather", ALU.bypass, RG8, [dsc["s1agi"].ap().opt()],
            [dsc["s1ago"].ap().opt()], waits=["p3_wr"], inc="cc")


def wkv_tile(em, sb, fa, bfa, ct):
    kk = fa(P3_K + ct * 512, 512)
    vv = fa(P3_V + ct * 512, 512)
    sr = fa(P3_SR + ct * 512, 512)
    ab = fa(P3_AB + ct * 513, 513)
    bb = fa(P3_BB + ct * 513, 513)
    yy = fa(P3_Y, 512)
    E = fa(P3_XN, 512)
    E2 = fa(P3_XN + 513, 512)
    Ev = fa(P3_XN + 2 * 513, 512)
    ucol = sb["wkvp"][:, ct:ct + 1]
    ewcol = sb["wkvp"][:, 4 + ct:5 + ct]
    pcol = sb["pcarry"][:, ct:ct + 1]
    m = sb["colz"][:, 4:5]
    t1 = sb["colz"][:, 5:6]
    t2 = sb["colz"][:, 6:7]
    nm = sb["colz"][:, 9:10]
    um = sb["colz"][:, 10:11]
    ewb = yy
    em.op("vector", lambda e: e.tensor_reduce(t1, kk, AX.X, ALU.max),
          waits=[("p3_s", em.val("p3_s"))], inc="p3_v")
    vp(em, sb, t1, "p3_v", 128, 1)
    em.op("vector", lambda e: e.tensor_scalar(t2, ucol, 0.0, None, ALU.max), inc="p3_v")
    vp(em, sb, t2, "p3_v", 128, 1)
    em.op("vector", lambda e: e.tensor_tensor(t2, t2, t1, ALU.add), inc="p3_v")
    vp(em, sb, t2, "p3_v", 128, 1)
    em.op("vector", lambda e: e.tensor_tensor(m, t2, pcol, ALU.max), inc="p3_v")
    vp(em, sb, m, "p3_v", 128, 1)
    em.op("vector", lambda e: e.tensor_scalar(nm, m, -1.0, None, ALU.mult), inc="p3_v")
    vp(em, sb, nm, "p3_v", 128, 1)
    em.op("vector", lambda e: e.tensor_tensor(um, ucol, nm, ALU.add), inc="p3_v")
    vp(em, sb, um, "p3_v", 128, 1)
    em.op("scalar", lambda e: e.activation(E, kk, AF.Exp, bias=nm),
          waits=["p3_v"], inc="p3_s")
    em.op("scalar", lambda e: e.activation(E2, kk, AF.Exp, bias=um), inc="p3_s")
    em.op("vector", lambda e: e.tensor_tensor(t1, pcol, nm, ALU.add),
          waits=["p3_s"], inc="p3_v")
    vp(em, sb, t1, "p3_v", 128, 1)
    em.op("scalar", lambda e: e.activation(t1, t1, AF.Exp), waits=["p3_v"], inc="p3_s")
    em.op("vector", lambda e: e.tensor_tensor(ab[:, 0:1], sb["pcarry"][:, 4 + ct:5 + ct],
                                              t1, ALU.mult), waits=["p3_s"], inc="p3_v")
    vp(em, sb, ab[:, 0:1], "p3_v", 128, 1)
    em.op("vector", lambda e: e.tensor_tensor(bb[:, 0:1], sb["pcarry"][:, 8 + ct:9 + ct],
                                              t1, ALU.mult), inc="p3_v")
    vp(em, sb, bb[:, 0:1], "p3_v", 128, 1)
    em.op("vector", lambda e, ewb=ewb: e.tensor_scalar(
        ewb, kk, 0.0, ewcol, ALU.mult, ALU.add), inc="p3_v")
    em.op("vector", lambda e: e.tensor_tensor(Ev, E, vv, ALU.mult), inc="p3_v")
    em.op("vector", lambda e, ewb=ewb: e.tensor_tensor_scan(
        ab[:, 1:513], ewb, Ev, ab[:, 0:1],
        ALU.mult, ALU.add), inc="p3_v")
    em.op("vector", lambda e, ewb=ewb: e.tensor_tensor_scan(
        bb[:, 1:513], ewb, E, bb[:, 0:1],
        ALU.mult, ALU.add), inc="p3_v")
    em.op("vector", lambda e: e.tensor_tensor(yy, E2, vv, ALU.mult), inc="p3_v")
    em.op("vector", lambda e: e.tensor_tensor(yy, yy, ab[:, 0:512], ALU.add), inc="p3_v")
    em.op("vector", lambda e: e.tensor_tensor(E2, E2, bb[:, 0:512], ALU.add), inc="p3_v")
    em.op("vector", lambda e: e.reciprocal(E2, E2), inc="p3_v")
    em.op("vector", lambda e: e.tensor_tensor(yy, yy, E2, ALU.mult), inc="p3_v")
    em.op("vector", lambda e, ct=ct: e.tensor_tensor(
        bfa(B3_Y1 + ct * 512, 512), yy, sr, ALU.mult), inc="p3_v")
    em.op("vector", lambda e: e.tensor_copy(pcol, m), inc="p3_v")
    vp(em, sb, pcol, "p3_v", 128, 1)
    em.op("vector", lambda e, ct=ct: e.tensor_copy(sb["pcarry"][:, 4 + ct:5 + ct],
                                                   ab[:, 512:513]), inc="p3_v")
    vp(em, sb, sb["pcarry"][:, 4 + ct:5 + ct], "p3_v", 128, 1)
    em.op("vector", lambda e, ct=ct: e.tensor_copy(sb["pcarry"][:, 8 + ct:9 + ct],
                                                   bb[:, 512:513]), inc="p3_v")
    vp(em, sb, sb["pcarry"][:, 8 + ct:9 + ct], "p3_v", 128, 1)


def colmax_to_scalar(em, dsc, sb, srccol, dstcol, pcol_idx, chV, chDMA):
    em.dma("gpsimd", dsc["pmax"].ap()[pcol_idx:pcol_idx + 1, :], srccol,
           waits=[chV], inc=chDMA)
    em.dma("gpsimd", sb["rowz"][0:1, :],
           dsc["pmax"].ap()[pcol_idx:pcol_idx + 1, :],
           waits=[chDMA], inc=chDMA)
    em.op("vector", lambda e: e.tensor_reduce(
        sb["colz"][0:1, dstcol:dstcol + 1], sb["rowz"][0:1, :], AX.X, ALU.max),
        waits=[chDMA], inc=chV)
    vp(em, sb, sb["colz"][0:1, dstcol:dstcol + 1], chV, 1, 1)


def phase4(em, nc, din, dout, dsc, sb, ps, fa, bfa):
    em.dma("gpsimd", sb["colz"][:, 0:1],
           dsc["ar1o"].ap()[0:1, 0:1].to_broadcast([128, 1]), waits=["cc"],
           inc="rep0")
    em.op("vector", lambda e: e.tensor_scalar(sb["colz"][:, 0:1], sb["colz"][:, 0:1],
                                              1.0 / QM15, 1e-12, ALU.mult, ALU.max),
          waits=["rep0"], inc="q_v")
    vp(em, sb, sb["colz"][:, 0:1], "q_v", 128, 1)
    em.op("vector", lambda e: e.reciprocal(sb["colz"][:, 1:2], sb["colz"][:, 0:1]),
          inc="q_v")
    vp(em, sb, sb["colz"][:, 1:2], "q_v", 128, 1)
    em.op("vector", lambda e: e.memset(sb["mxc"][:, 2:4], 0.0), inc="q_v")
    vp(em, sb, sb["mxc"][:, 2:4], "q_v", 128, 2)

    p3rel = [("p3_v", em.val("p3_v")), ("p3_t", em.val("p3_t")),
             ("p3_s", em.val("p3_s")), ("p3_wr", em.val("p3_wr"))]
    for tc in range(4):
        for d in range(8):
            em.dma("sync", fa(P4_X1 + d * 512, 512),
                   dsc["x1agi"].ap()[d * 128:(d + 1) * 128, tc * 512:(tc + 1) * 512],
                   waits=[("q_v", em.val("q_v")), ("q_s", em.val("q_s"))] + p3rel,
                   inc="q_ld")
        sqt = {}
        for d in range(8):
            xs = fa(P4_X1 + d * 512, 512)
            sq = fa(P4_SG, 512)
            em.op("tensor", lambda e, xs=xs, d=d: e.matmul(
                ps["pa"][0:1, :], sb["onesf"][:, :], xs, start=(d == 0), stop=(d == 7)),
                waits=["q_ld"], inc="q_t")
            em.op("scalar", lambda e, xs=xs, sq=sq: e.activation(sq, xs, AF.Square),
                  waits=["q_ld", ("q_t", sqt.get(d - 1, 0))], inc="q_s")
            em.op("tensor", lambda e, sq=sq, d=d: e.matmul(
                ps["pa"][32:33, :], sb["onesf"][:, :], sq, start=(d == 0), stop=(d == 7)),
                waits=["q_s"], inc="q_t")
            sqt[d] = em.val("q_t")
        ln_rows(em, sb, fa, ps["pa"], P4_RW, "q_s", "q_v", "q_t",
                extraw=[("row4", em.val("row4"))])
        em.dma("gpsimd", dsc["lnrow4"].ap()[0:1, tc * 512:(tc + 1) * 512],
               fa(P4_RW, 512)[0:1, :], waits=["q_v"], inc="row4")
        em.dma("gpsimd", dsc["lnrow4"].ap()[1:2, tc * 512:(tc + 1) * 512],
               fa(P4_RW + 512, 512)[0:1, :], inc="row4")
        em.dma("gpsimd", fa(P4_LR, 512),
               dsc["lnrow4"].ap()[0:1, tc * 512:(tc + 1) * 512].to_broadcast([128, 512]),
               waits=["row4", ("q_v", em.val("q_v"))], inc="rep0")
        em.dma("gpsimd", fa(P4_LR + 512, 512),
               dsc["lnrow4"].ap()[1:2, tc * 512:(tc + 1) * 512].to_broadcast([128, 512]),
               inc="rep0")
        for d in range(8):
            xn = fa(P4_XN + d * 512, 512)
            em.op("vector", lambda e, d=d, xn=xn: e.tensor_tensor(
                xn, fa(P4_X1 + d * 512, 512), fa(P4_LR, 512), ALU.mult),
                waits=["rep0"], inc="q_v")
            em.op("vector", lambda e, xn=xn: e.tensor_tensor(
                xn, xn, fa(P4_LR + 512, 512), ALU.add), inc="q_v")
            vst = fa(P4_CF, 512)
            em.op("vector", lambda e, xn=xn, vst=vst: e.tensor_scalar(
                vst, xn, sb["colz"][:, 1:2], RC, ALU.mult, ALU.add), inc="q_v")
            em.op("vector", lambda e, vst=vst: e.tensor_scalar(
                vst, vst, -RC, None, ALU.add), inc="q_v")
            em.op("vector", lambda e, vst=vst: e.tensor_copy(sb["i0"][:, :], vst),
                  inc="q_v")
            em.op("scalar", lambda e, d=d, tc=tc: e.activation(
                bfa(B4_XQ + tc * 4096 + d * 512, 512), sb["i0"][:, :], AF.Copy,
                scale=sb["colz"][:, 0:1]), waits=["q_v"], inc="q_s")

    for wi, (qnm, bdram) in enumerate((("wqg", "gtd"), ("wqu", "upd"))):
        for ic in range(8):
            em.dma("sync", bfa(B4_WQ + ic * 3072, 3072),
                   dsc[qnm].ap()[0:HID, ic * 128:(ic + 1) * 128],
                   waits=["q2_wr", ("q_t", em.val("q_t"))] + p3rel, inc="q_ld",
                   transpose=True)
        for hid in range(24):
            for tc in range(4):
                pst = ps["pc"] if (hid * 4 + tc) % 2 == 0 else ps["pd"]
                for d in range(8):
                    em.op("tensor", lambda e, hid=hid, tc=tc, d=d, pst=pst: e.matmul(
                        pst[:, :], bfa(B4_WQ + d * 3072 + hid * 128, 128),
                        bfa(B4_XQ + tc * 4096 + d * 512, 512),
                        start=(d == 0), stop=(d == 7)),
                        waits=["q_ld", "q_s", ("q_v", em.val("q_v"))], inc="q_t")
                em.op("vector", lambda e, pst=pst: e.tensor_reduce(
                    sb["mxc"][:, 4:5], pst[:, :], AX.X, ALU.max,
                    apply_absolute_value=True), waits=["q_t"], inc="q_v")
                vp(em, sb, sb["mxc"][:, 4:5], "q_v", 128, 1)
                em.op("vector", lambda e, wi=wi: e.tensor_tensor(
                    sb["mxc"][:, 2 + wi:3 + wi], sb["mxc"][:, 2 + wi:3 + wi],
                    sb["mxc"][:, 4:5], ALU.max), inc="q_v")
                vp(em, sb, sb["mxc"][:, 2 + wi:3 + wi], "q_v", 128, 1)
                em.op("scalar", lambda e, pst=pst: e.activation(
                    bfa(B4_ST, 512), pst[:, :], AF.Copy),
                    waits=["q_v", ("q_wr", em.val("q_wr"))], inc="q_s")
                em.dma("gpsimd", dsc[bdram].ap()[hid, :, tc * 512:(tc + 1) * 512],
                       bfa(B4_ST, 512), waits=["q_s"], inc="q_wr")
    colmax_to_scalar(em, dsc, sb, sb["mxc"][:, 2:3], 3, 1, "q_v", "row4")
    colmax_to_scalar(em, dsc, sb, sb["mxc"][:, 3:4], 4, 2, "q_v", "row4")
    em.dma("gpsimd", dsc["ar2i"].ap()[0:1, 0:2], sb["colz"][0:1, 3:5],
           waits=["q_v"], inc="row4")
    em.coll("AllReduce", ALU.max, RG8, [dsc["ar2i"].ap().opt()],
            [dsc["ar2o"].ap().opt()], waits=["row4"], inc="cc")
    em.dma("gpsimd", sb["colz"][:, 2:3],
           dsc["ar2o"].ap()[0:1, 0:1].to_broadcast([128, 1]), waits=["cc"], inc="rep0")
    em.dma("gpsimd", sb["colz"][:, 4:5],
           dsc["ar2o"].ap()[0:1, 1:2].to_broadcast([128, 1]), inc="rep0")
    for c0 in (2, 4):
        em.op("vector", lambda e, c0=c0: e.tensor_scalar(
            sb["colz"][:, c0:c0 + 1], sb["colz"][:, c0:c0 + 1], 1.0 / QM15, 1e-12,
            ALU.mult, ALU.max), waits=["rep0"], inc="q_v")
        vp(em, sb, sb["colz"][:, c0:c0 + 1], "q_v", 128, 1)
        em.op("vector", lambda e, c0=c0: e.reciprocal(
            sb["colz"][:, c0 + 1:c0 + 2], sb["colz"][:, c0:c0 + 1]), inc="q_v")
        vp(em, sb, sb["colz"][:, c0 + 1:c0 + 2], "q_v", 128, 1)
    em.op("vector", lambda e: e.memset(sb["mxi"][:, :], 0), inc="q_v")
    vp(em, sb, sb["mxi"][:, :], "q_v", 128, 2)
    qwr_all = em.val("q_wr")
    rb_guard = {0: 0, 1: 0}

    def rebuild_P(tc, hid, with_up):
        par = hid % 2
        qch = "q_gt0" if par == 0 else "q_gt1"
        gt = bfa(B4_GU + par * 1024, 512)
        ut = bfa(B4_GU + par * 1024 + 512, 512)
        em.dma("sync", gt, dsc["gtd"].ap()[hid, :, tc * 512:(tc + 1) * 512],
               waits=[("q_v", rb_guard[par]), ("q_wr", qwr_all)], inc=qch)
        if with_up:
            em.dma("sync", ut, dsc["upd"].ap()[hid, :, tc * 512:(tc + 1) * 512],
                   inc=qch)
        vst = fa(P4_CF, 512)
        em.op("vector", lambda e, gt=gt, vst=vst: e.tensor_scalar(
            vst, gt, sb["colz"][:, 3:4], RC, ALU.mult, ALU.add),
            waits=[qch, ("q_s", em.val("q_s"))], inc="q_v")
        em.op("vector", lambda e, vst=vst: e.tensor_scalar(
            vst, vst, -RC, None, ALU.add), inc="q_v")
        em.op("vector", lambda e, vst=vst: e.tensor_copy(sb["i0"][:, :], vst),
              inc="q_v")
        em.op("scalar", lambda e: e.activation(fa(P4_SG, 512), sb["i0"][:, :],
                                               AF.Sigmoid, scale=sb["colz"][:, 2:3]),
              waits=["q_v"], inc="q_s")
        em.op("vector", lambda e, vst=vst: e.tensor_scalar(
            vst, fa(P4_SG, 512), 65536.0, RC, ALU.mult, ALU.add),
            waits=["q_s"], inc="q_v")
        em.op("vector", lambda e, vst=vst: e.tensor_scalar(
            vst, vst, -RC, None, ALU.add), inc="q_v")
        em.op("vector", lambda e, vst=vst: e.tensor_copy(sb["i1"][:, :], vst),
              inc="q_v")
        em.op("vector", lambda e: e.tensor_scalar(sb["i1"][:, :], sb["i1"][:, :],
                                                  65535, None, ALU.min), inc="q_v")
        em.op("vector", lambda e: e.tensor_tensor(sb["i2"][:, :], sb["i0"][:, :],
                                                  sb["i1"][:, :], ALU.mult), inc="q_v")
        return ut

    for tc in range(4):
        for hid in range(24):
            rebuild_P(tc, hid, False)
            em.op("vector", lambda e: e.tensor_reduce(
                sb["mxi"][:, 1:2], sb["i2"][:, :], AX.X, ALU.max,
                apply_absolute_value=True), inc="q_v")
            vp(em, sb, sb["mxi"][:, 1:2], "q_v", 128, 1)
            em.op("vector", lambda e: e.tensor_tensor(
                sb["mxi"][:, 0:1], sb["mxi"][:, 0:1], sb["mxi"][:, 1:2], ALU.max),
                inc="q_v")
            vp(em, sb, sb["mxi"][:, 0:1], "q_v", 128, 1)
            rb_guard[hid % 2] = em.val("q_v")
    em.op("vector", lambda e: e.tensor_copy(sb["mxc"][:, 5:6], sb["mxi"][:, 0:1]),
          inc="q_v")
    vp(em, sb, sb["mxc"][:, 5:6], "q_v", 128, 1)
    colmax_to_scalar(em, dsc, sb, sb["mxc"][:, 5:6], 6, 3, "q_v", "row4")
    em.dma("gpsimd", dsc["ar3i"].ap()[0:1, 0:1], sb["colz"][0:1, 6:7],
           waits=["q_v"], inc="row4")
    em.dma("gpsimd", dsc["ar3i"].ap()[0:1, 1:2], sb["colz"][0:1, 6:7], inc="row4")
    em.coll("AllReduce", ALU.max, RG8, [dsc["ar3i"].ap().opt()],
            [dsc["ar3o"].ap().opt()], waits=["row4"], inc="cc")
    em.dma("gpsimd", sb["colz"][:, 6:7],
           dsc["ar3o"].ap()[0:1, 0:1].to_broadcast([128, 1]), waits=["cc"], inc="rep0")
    em.op("vector", lambda e: e.tensor_scalar(sb["colz"][:, 7:8], sb["colz"][:, 2:3],
                                              1.0 / 65536.0, None, ALU.mult),
          waits=["rep0"], inc="q_v")
    vp(em, sb, sb["colz"][:, 7:8], "q_v", 128, 1)
    em.op("vector", lambda e: e.tensor_tensor(sb["colz"][:, 8:9], sb["colz"][:, 6:7],
                                              sb["colz"][:, 7:8], ALU.mult), inc="q_v")
    vp(em, sb, sb["colz"][:, 8:9], "q_v", 128, 1)
    em.op("vector", lambda e: e.tensor_scalar(sb["colz"][:, 8:9], sb["colz"][:, 8:9],
                                              1.0 / QM15, 1e-12, ALU.mult, ALU.max),
          inc="q_v")
    vp(em, sb, sb["colz"][:, 8:9], "q_v", 128, 1)
    em.op("vector", lambda e: e.reciprocal(sb["colz"][:, 12:13], sb["colz"][:, 8:9]),
          inc="q_v")
    vp(em, sb, sb["colz"][:, 12:13], "q_v", 128, 1)
    em.op("vector", lambda e: e.tensor_tensor(sb["colz"][:, 6:7], sb["colz"][:, 7:8],
                                              sb["colz"][:, 12:13], ALU.mult),
          inc="q_v")
    vp(em, sb, sb["colz"][:, 6:7], "q_v", 128, 1)

    em.op("vector", lambda e: e.memset(sb["mxi"][:, 0:1], 0), inc="q_v")
    vp(em, sb, sb["mxi"][:, 0:1], "q_v", 128, 1)
    for tc in range(4):
        for hid in range(24):
            ut = rebuild_P(tc, hid, True)
            vst = fa(P4_CF, 512)
            em.op("vector", lambda e, vst=vst: e.tensor_scalar(
                vst, sb["i2"][:, :], sb["colz"][:, 6:7], RC, ALU.mult, ALU.add),
                inc="q_v")
            em.op("vector", lambda e, vst=vst: e.tensor_scalar(
                vst, vst, -RC, None, ALU.add), inc="q_v")
            em.op("vector", lambda e, vst=vst: e.tensor_copy(sb["i2"][:, :], vst),
                  inc="q_v")
            em.op("vector", lambda e, ut=ut, vst=vst: e.tensor_scalar(
                vst, ut, sb["colz"][:, 5:6], RC, ALU.mult, ALU.add),
                waits=[("q_s", em.val("q_s"))], inc="q_v")
            em.op("vector", lambda e, vst=vst: e.tensor_scalar(
                vst, vst, -RC, None, ALU.add), inc="q_v")
            em.op("vector", lambda e, vst=vst: e.tensor_copy(sb["i0"][:, :], vst),
                  inc="q_v")
            em.op("vector", lambda e: e.tensor_tensor(
                sb["i1"][:, :], sb["i2"][:, :], sb["i0"][:, :], ALU.mult), inc="q_v")
            em.op("vector", lambda e: e.tensor_reduce(
                sb["mxi"][:, 1:2], sb["i1"][:, :], AX.X, ALU.max,
                apply_absolute_value=True), inc="q_v")
            vp(em, sb, sb["mxi"][:, 1:2], "q_v", 128, 1)
            em.op("vector", lambda e: e.tensor_tensor(
                sb["mxi"][:, 0:1], sb["mxi"][:, 0:1], sb["mxi"][:, 1:2], ALU.max),
                inc="q_v")
            vp(em, sb, sb["mxi"][:, 0:1], "q_v", 128, 1)
            rb_guard[hid % 2] = em.val("q_v")
    em.op("vector", lambda e: e.tensor_copy(sb["mxc"][:, 6:7], sb["mxi"][:, 0:1]),
          inc="q_v")
    vp(em, sb, sb["mxc"][:, 6:7], "q_v", 128, 1)
    colmax_to_scalar(em, dsc, sb, sb["mxc"][:, 6:7], 9, 3, "q_v", "row4")
    em.dma("gpsimd", dsc["ar4i"].ap()[0:1, 0:1], sb["colz"][0:1, 9:10],
           waits=["q_v"], inc="row4")
    em.dma("gpsimd", dsc["ar4i"].ap()[0:1, 1:2], sb["colz"][0:1, 9:10], inc="row4")
    em.coll("AllReduce", ALU.max, RG8, [dsc["ar4i"].ap().opt()],
            [dsc["ar4o"].ap().opt()], waits=["row4"], inc="cc")
    em.dma("gpsimd", sb["colz"][:, 9:10],
           dsc["ar4o"].ap()[0:1, 0:1].to_broadcast([128, 1]), waits=["cc"], inc="rep0")
    em.op("vector", lambda e: e.tensor_tensor(sb["colz"][:, 10:11], sb["colz"][:, 8:9],
                                              sb["colz"][:, 4:5], ALU.mult),
          waits=["rep0"], inc="q_v")
    vp(em, sb, sb["colz"][:, 10:11], "q_v", 128, 1)
    em.op("vector", lambda e: e.tensor_tensor(sb["colz"][:, 11:12], sb["colz"][:, 9:10],
                                              sb["colz"][:, 10:11], ALU.mult), inc="q_v")
    vp(em, sb, sb["colz"][:, 11:12], "q_v", 128, 1)
    em.op("vector", lambda e: e.tensor_scalar(sb["colz"][:, 11:12], sb["colz"][:, 11:12],
                                              1.0 / QM15, 1e-12, ALU.mult, ALU.max),
          inc="q_v")
    vp(em, sb, sb["colz"][:, 11:12], "q_v", 128, 1)
    em.op("vector", lambda e: e.reciprocal(sb["colz"][:, 12:13], sb["colz"][:, 11:12]),
          inc="q_v")
    vp(em, sb, sb["colz"][:, 12:13], "q_v", 128, 1)
    em.op("vector", lambda e: e.tensor_tensor(sb["colz"][:, 9:10], sb["colz"][:, 10:11],
                                              sb["colz"][:, 12:13], ALU.mult),
          inc="q_v")
    vp(em, sb, sb["colz"][:, 9:10], "q_v", 128, 1)

    for hc in range(24):
        em.dma("sync", bfa(B4_WQ + hc * 1024, 1024),
               dsc["wqd"].ap()[0:C, hc * 128:(hc + 1) * 128],
               waits=["q2_wr", ("q_t", em.val("q_t"))], inc="q_ld", transpose=True)

    def s1body(e, core, then):
        g = core // 4
        b = core % 4
        for rep, rk in ((0, b), (1, b + 4)):
            ins = e.dma_start(
                out=dsc["s1loc"].ap()[rep * C:(rep + 1) * C, :],
                in_=dsc["s1ago"].ap()[rk * C:(rk + 1) * C,
                                      g * THALF:(g + 1) * THALF])
            then(ins)

    em.switch("gpsimd", s1body, waits=["cc"], inc="loc_wr", nincs=2)
    s1loc_done = em.val("loc_wr")

    for tc in range(4):
        for hid in range(24):
            ut = rebuild_P(tc, hid, True)
            vst = fa(P4_CF, 512)
            em.op("vector", lambda e, vst=vst: e.tensor_scalar(
                vst, sb["i2"][:, :], sb["colz"][:, 6:7], RC, ALU.mult, ALU.add),
                inc="q_v")
            em.op("vector", lambda e, vst=vst: e.tensor_scalar(
                vst, vst, -RC, None, ALU.add), inc="q_v")
            em.op("vector", lambda e, vst=vst: e.tensor_copy(sb["i2"][:, :], vst),
                  inc="q_v")
            em.op("vector", lambda e, ut=ut, vst=vst: e.tensor_scalar(
                vst, ut, sb["colz"][:, 5:6], RC, ALU.mult, ALU.add),
                waits=[("q_s", em.val("q_s"))], inc="q_v")
            em.op("vector", lambda e, vst=vst: e.tensor_scalar(
                vst, vst, -RC, None, ALU.add), inc="q_v")
            em.op("vector", lambda e, vst=vst: e.tensor_copy(sb["i0"][:, :], vst),
                  inc="q_v")
            em.op("vector", lambda e: e.tensor_tensor(
                sb["i1"][:, :], sb["i2"][:, :], sb["i0"][:, :], ALU.mult), inc="q_v")
            em.op("vector", lambda e, vst=vst: e.tensor_scalar(
                vst, sb["i1"][:, :], sb["colz"][:, 9:10], RC, ALU.mult, ALU.add),
                inc="q_v")
            em.op("vector", lambda e, vst=vst: e.tensor_scalar(
                vst, vst, -RC, None, ALU.add), inc="q_v")
            em.op("vector", lambda e, vst=vst: e.tensor_copy(sb["i1"][:, :], vst),
                  inc="q_v")
            em.op("scalar", lambda e, hid=hid: e.activation(
                bfa(B4_OQ + hid * 512, 512), sb["i1"][:, :], AF.Copy,
                scale=sb["colz"][:, 11:12]),
                waits=["q_v", ("q_t", em.val("q_t"))], inc="q_s")
            rb_guard[hid % 2] = em.val("q_v")
        for d in range(8):
            pst = ps["pc"] if d % 2 == 0 else ps["pd"]
            for hid in range(24):
                em.op("tensor", lambda e, d=d, hid=hid, pst=pst: e.matmul(
                    pst[:, :], bfa(B4_WQ + hid * 1024 + d * 128, 128),
                    bfa(B4_OQ + hid * 512, 512), start=(hid == 0), stop=(hid == 23)),
                    waits=["q_s", "q_ld", ("q_v", em.val("q_v"))], inc="q_t")
            em.op("scalar", lambda e, d=d, pst=pst: e.activation(
                fa(P4_G1 + d * 512, 512), pst[:, :], AF.Silu), waits=["q_t"],
                inc="q_s")
        for rep in range(2):
            base = 36864 if rep == 0 else 43008
            for d in range(8):
                em.dma("sync", bfa(base + d * 512, 512),
                       dsc["s1loc"].ap()[rep * C + d * 128:rep * C + (d + 1) * 128,
                                         tc * 512:(tc + 1) * 512],
                       waits=[("loc_wr", s1loc_done), ("q_v", em.val("q_v"))],
                       inc="q_ld")
        for d in range(8):
            em.dma("sync", fa(P4_X1 + d * 512, 512),
                   dsc["x1agi"].ap()[d * 128:(d + 1) * 128, tc * 512:(tc + 1) * 512],
                   waits=[("q_v", em.val("q_v"))], inc="q_ld")
        for d in range(8):
            em.op("vector", lambda e, d=d: e.tensor_tensor(
                fa(P4_SC + d * 512, 512), bfa(36864 + d * 512, 512),
                bfa(43008 + d * 512, 512), ALU.add),
                waits=["q_ld"], inc="q_v")
            em.op("vector", lambda e, d=d: e.tensor_tensor(
                fa(P4_SC + d * 512, 512), fa(P4_SC + d * 512, 512),
                fa(P4_G1 + d * 512, 512), ALU.mult), waits=["q_s"], inc="q_v")
            em.op("vector", lambda e, d=d: e.tensor_tensor(
                fa(P4_XN + d * 512, 512), fa(P4_SC + d * 512, 512),
                fa(P4_X1 + d * 512, 512), ALU.add), inc="q_v")
            em.dma("gpsimd", dout.ap()[d * 128:(d + 1) * 128, tc * 512:(tc + 1) * 512],
                   fa(P4_XN + d * 512, 512), waits=["q_v"], inc="out_wr")
    em.op("gpsimd", lambda e: e.nop(), waits=["out_wr"], inc=None)


def kernel(**inputs):
    in_maps = make_in_maps(inputs)
    nc = build_program()
    res = run_bass_kernel_spmd(nc, in_maps, core_ids=list(range(8)))
    out = np.zeros((B, L, C), dtype=np.float32)
    for c in range(8):
        b, g = c % 4, c // 4
        out[b, g * THALF:(g + 1) * THALF, :] = res.results[c]["out"].T
    return out


# version 05
# speedup vs baseline: 1.0757x; 1.0363x over previous
import numpy as np
import ml_dtypes
import concourse.bass as bass
import concourse.mybir as mybir
from concourse.bass_utils import run_bass_kernel_spmd

F32 = mybir.dt.float32
BF16 = mybir.dt.bfloat16
I32 = mybir.dt.int32
AF = mybir.ActivationFunctionType
ALU = mybir.AluOpType
AX = mybir.AxisListType

B, L, C = 4, 4096, 1024
DI, NST, KC, RNK = 256, 16, 4, 8
NHPC = 4
HID = 3072
TC1 = 1024
NCH = L // TC1
THALF = L // 2
EPS = 1e-5
QM15 = 32767.0
RC = 12582912.0

RG8 = [[0, 1, 2, 3, 4, 5, 6, 7]]
DEBUG = False
CUT = None
P1HCK = 16
P1U = 96
P1SCAN = True
P1PART = 9
P1VAR = 0
P1SKIP = False
P0PART = 3
P0MASK = 15

FA_COLS = 21016
BF_COLS = 47616

P0_XC = 0
P0_SQ = 8192
P0_RW = 9216
P0_LR = 10240
P0_WST = 12352
P0_HH = 15616

P1_U = 0
P1_UC = 2054
P1_SZ = 4102
P1_DL = 6150
P1_DB = 8198
P1_T0 = 10246
P1_T1 = 10758
P1_SS = 11270
P1_DA = 12294
P1_EP = 14342
P1_F0 = 15366

P2_X = 0
P2_S = 4096
P2_X1 = 8192
P2_G = 12288
P2_LW = 12800
P2_LS = 15872

P3_X1 = 0
P3_XN = 4104
P3_K = 8208
P3_V = 10256
P3_SR = 12304
P3_AB = 14352
P3_BB = 16404
P3_Y = 18456
P3_LR = 18968
P3_RW = 19992

P4_X1 = 0
P4_XN = 4096
P4_LR = 8192
P4_RW = 9216
P4_SG = 10240
P4_S1 = 10752
P4_G1 = 11776
P4_SC = 15872
P4_CF = 19968

B1_XNH = 0
B1_XNT = 16384
B1_DU = 32768
B1_X = 34816
B1_H = 35840
B1_PR = 36864
B1_BR = 40960
B1_CR = 43008
B1_YG = 45056
B1_HF = 47104
B2_WD = 0
B2_HF = 24576
B3_XK = 0
B3_XV = 4096
B3_XR = 8192
B3_WK = 12288
B3_WV = 16384
B3_WR = 20480
B3_WO = 24576
B3_Y1 = 28672
B3_ST = 30720
B4_WQ = 0
B4_XQ = 24576
B4_OQ = 24576
B4_GU = 40960
B4_ST = 43008


def _f(x):
    return np.ascontiguousarray(np.asarray(x), dtype=np.float32)


def make_in_maps(inp):
    eye = np.ascontiguousarray(np.eye(128, dtype=ml_dtypes.bfloat16))
    ones = np.ones((128, 1), np.float32)
    maps = []
    for c in range(8):
        b, g = c % 4, c // 4
        mh = slice(4 * g, 4 * g + 4)
        ch = slice(512 * g, 512 * g + 512)
        m = {"identb": eye, "onesf": ones}
        m["xbT"] = _f(np.asarray(inp["L0_attn_x"])[b].T)
        w = np.asarray(inp["L0_attn_in_proj_w"])[mh]
        m["ipwT"] = _f(np.transpose(w, (2, 0, 1)).reshape(128, 2048))
        cw = np.asarray(inp["L0_attn_conv_w"])[mh].reshape(4, 2, 128, KC)
        m["convw"] = _f(np.transpose(cw, (2, 0, 1, 3)).reshape(128, 32))
        cb = np.asarray(inp["L0_attn_conv_b"])[mh].reshape(4, 2, 128)
        m["convb"] = _f(np.transpose(cb, (2, 0, 1)).reshape(128, 8))
        xp = np.asarray(inp["L0_attn_x_proj_w"])[mh]
        xpT = np.transpose(xp, (2, 0, 1)).reshape(2, 128, 4, 40)
        m["xpwT"] = _f(np.transpose(xpT, (1, 2, 0, 3)).reshape(128, 320))
        dw = np.asarray(inp["L0_attn_dt_proj_w"])[mh].reshape(4, 2, 128, RNK)
        m["dtwT"] = _f(np.transpose(dw, (3, 0, 1, 2)).reshape(RNK, 1024))
        db = np.asarray(inp["L0_attn_dt_proj_b"])[mh].reshape(4, 2, 128)
        m["dtb"] = _f(np.transpose(db, (2, 0, 1)).reshape(128, 8))
        al = np.asarray(inp["L0_attn_A_log"])[mh].reshape(4, 2, 128, NST)
        m["alog"] = _f(np.transpose(al, (2, 0, 1, 3)).reshape(128, 128))
        dd = np.asarray(inp["L0_attn_D"])[mh].reshape(4, 2, 128)
        m["ddd"] = _f(np.transpose(dd, (2, 0, 1)).reshape(128, 8))
        ow = np.asarray(inp["L0_attn_out_proj_w"])[mh]
        owT = np.transpose(ow, (2, 0, 1)).reshape(2, 128, 4, 128)
        m["opwT"] = _f(np.transpose(owT, (1, 2, 0, 3)).reshape(128, 1024))
        wg = np.asarray(inp["L0_ffn_w_gate"]).reshape(24, 128, 8, 128)
        m["wg0"] = np.ascontiguousarray(
            np.transpose(wg, (0, 3, 2, 1)), dtype=np.int32).reshape(24, 128, 1024)
        wu = np.asarray(inp["L0_ffn_w_up"]).reshape(24, 128, 8, 128)
        m["wu0"] = np.ascontiguousarray(
            np.transpose(wu, (0, 3, 2, 1)), dtype=np.int32).reshape(24, 128, 1024)
        wd = np.asarray(inp["L0_ffn_w_down"]).reshape(8, 128, 24, 128)
        m["wd0"] = np.ascontiguousarray(
            np.transpose(wd, (2, 3, 0, 1)), dtype=np.int32).reshape(24, 128, 1024)
        m["wkT"] = _f(np.asarray(inp["L1_attn_Wk"]).T[:, ch])
        m["wvT"] = _f(np.asarray(inp["L1_attn_Wv"]).T[:, ch])
        m["wrT"] = _f(np.asarray(inp["L1_attn_Wr"]).T[:, ch])
        m["woT"] = _f(np.asarray(inp["L1_attn_Wo"]).T[ch, :])
        for nm, key in [("tdec", "L1_attn_time_decay"), ("tfir", "L1_attn_time_first"),
                        ("tmk", "L1_attn_time_mix_k"), ("tmv", "L1_attn_time_mix_v"),
                        ("tmr", "L1_attn_time_mix_r")]:
            m[nm] = _f(np.asarray(inp[key]).reshape(8, 128).T)
        m["wg1"] = _f(inp["L1_ffn_w_gate"])
        m["wu1"] = _f(inp["L1_ffn_w_up"])
        m["wd1"] = _f(inp["L1_ffn_w_down"])
        maps.append(m)
    return maps



class Ch:
    __slots__ = ("name", "n")

    def __init__(self, name):
        self.name = name
        self.n = 0


class Em:
    def __init__(self, nc, ename, eng, chans, sems):
        self.nc = nc
        self.ename = ename
        self.eng = eng
        self.chans = chans
        self.sems = sems
        self.lastw = {}
        self._pid = None

    def pid(self):
        if self._pid is None:
            self._pid = self.eng.partition_id()
        return self._pid

    def _wait(self, waits):
        for w in waits:
            if isinstance(w, tuple):
                name, val = w
            else:
                name, val = w, self.chans[w].n
            if val > 0 and self.lastw.get(name, -1) < val:
                self.eng.wait_ge(self.sems[name], val)
                self.lastw[name] = val

    def val(self, chname):
        return self.chans[chname].n

    def op(self, ename, fn, waits=(), inc=None, k=1):
        if ename == self.ename:
            self._wait(waits)
            ins = fn(self.eng)
            if inc is not None and ins is not None:
                ins.then_inc(self.sems[inc], k)
        if inc is not None:
            self.chans[inc].n += k

    def dma(self, ename, out, in_, waits=(), inc=None, transpose=False):
        if transpose:
            self.op(ename, lambda e: e.dma_start(out=out, in_=in_, transpose=True),
                    waits, inc, 16)
        else:
            self.op(ename, lambda e: e.dma_start(out=out, in_=in_), waits, inc, 16)

    def switch(self, ename, body, waits=(), inc=None, nincs=1, kper=16):
        if ename == self.ename:
            self._wait(waits)
            idx = self.pid()
            sem = self.sems[inc] if inc is not None else None
            for i in self.nc.Switch(engines=[self.eng], index=[idx], n=8):
                def then(instr, sem=sem):
                    if sem is not None:
                        instr.then_inc(sem, kper)
                body(self.eng, i, then)
        if inc is not None:
            self.chans[inc].n += kper * nincs

    def coll(self, kind, op, groups, ins, outs, waits=(), inc=None):
        self.op("gpsimd",
                lambda e: e.collective_compute(kind, op, replica_groups=groups,
                                               ins=ins, outs=outs),
                waits, inc, 1)


CHANNELS = [
    "w_ld", "x_ld", "x_ldb", "loc_wr", "pm_s", "pm_v",
    "st_t", "st_v", "st_s", "row_wr", "rep0", "rep0b",
    "pre_t", "pre_s", "pre_v", "brc_wr",
    "sc_s", "sc_v", "sc_t", "rep_ld", "rep_ldb",
    "epi_v", "epi_t", "epi_s", "s_wr",
    "f0_ld", "f0_t", "f0_s", "f0_v", "hf_wr",
    "cc", "g_t", "g_s", "g_v", "hf_ld",
    "c_ld", "c_v", "c_wr", "q2_v", "q2_s", "q2_ld", "q2_wr",
    "p3_ld", "p3_t", "p3_v", "p3_s", "p3_wr", "row3",
    "q_ld", "q_gt0", "q_gt1", "q_t", "q_v", "q_s", "q_wr", "row4",
    "out_wr", "dbg_wr",
]


def build_program():
    nc = bass.Bass("TRN2", target_bir_lowering=False, detect_race_conditions=False)

    din = {}
    for name, shape, dt in [
        ("xbT", [C, L], F32), ("ipwT", [128, 2048], F32), ("convw", [128, 32], F32),
        ("convb", [128, 8], F32), ("xpwT", [128, 320], F32), ("dtwT", [RNK, 1024], F32),
        ("dtb", [128, 8], F32), ("alog", [128, 128], F32), ("ddd", [128, 8], F32),
        ("opwT", [128, 1024], F32),
        ("wg0", [24, 128, 1024], I32), ("wu0", [24, 128, 1024], I32),
        ("wd0", [24, 128, 1024], I32),
        ("wkT", [C, 512], F32), ("wvT", [C, 512], F32), ("wrT", [C, 512], F32),
        ("woT", [512, C], F32),
        ("tdec", [128, 8], F32), ("tfir", [128, 8], F32), ("tmk", [128, 8], F32),
        ("tmv", [128, 8], F32), ("tmr", [128, 8], F32),
        ("wg1", [HID, C], F32), ("wu1", [HID, C], F32), ("wd1", [C, HID], F32),
        ("identb", [128, 128], BF16), ("onesf", [128, 1], F32),
    ]:
        din[name] = nc.dram_tensor(name, shape, dt, kind="ExternalInput")
    dout = nc.dram_tensor("out", [C, THALF], F32, kind="ExternalOutput")

    dsc = {}
    for name, shape, dt in [
        ("brc", [NHPC, 2 * NST, L], BF16),
        ("lnrow", [2, L], F32), ("lnrow3", [2, L], F32), ("lnrow4", [2, THALF], F32),
        ("hfd", [24, 128, THALF], BF16),
        ("gd", [8, 128, THALF], F32),
        ("wqg", [HID, C], BF16), ("wqu", [HID, C], BF16), ("wqd", [C, HID], BF16),
        ("upd", [24, 128, THALF], BF16), ("gtd", [24, 128, THALF], BF16),
        ("pmax", [4, 128], F32),
        ("sagi", [512, L], F32),
        ("x1agi", [C, THALF], F32),
        ("s1agi", [C, L], BF16),
        ("ar1i", [1, 2], F32), ("ar1o", [1, 2], F32),
        ("ar2i", [1, 2], F32), ("ar2o", [1, 2], F32),
        ("ar3i", [1, 2], F32), ("ar3o", [1, 2], F32),
        ("ar4i", [1, 2], F32), ("ar4o", [1, 2], F32),
        ("dbg", [8, 128, 2048], F32) if DEBUG else ("dbg", [1, 1, 1], F32),
        ("dbgb", [8, 128, 2048], BF16) if DEBUG else ("dbgb", [1, 1, 1], BF16),
        ("xhh", [512, L], F32), ("xth", [C, THALF], F32),
        ("lnrowth", [2, THALF], F32),
        ("sloc", [C, THALF], F32), ("x1loc", [C, L], F32),
        ("s1loc", [2 * C, THALF], BF16),
    ]:
        dbg_fetch = {"dbg", "dbgb", "lnrow", "brc", "hfd", "gd",
                     "wqg", "upd", "lnrow3"}
        kind = "ExternalOutput" if (DEBUG and name in dbg_fetch) else "Internal"
        dsc[name] = nc.dram_tensor(name, shape, dt, kind=kind)
    for name, shape, dt in [
        ("sago", [8 * 512, L], F32), ("x1ago", [8 * C, THALF], F32),
        ("s1ago", [8 * C, L], BF16),
    ]:
        dsc[name] = nc.dram_tensor(name, shape, dt, addr_space="Shared")
    if DEBUG:
        for name, src in [("dsagi", [512, L]), ("dx1", [C, THALF]),
                          ("dar", [1, 8])]:
            dsc[name] = nc.dram_tensor(name, src, F32, kind="ExternalOutput")
        dsc["ds1"] = nc.dram_tensor("ds1", [C, L], BF16, kind="ExternalOutput")

    small_specs = [
        ("ipw", [128, 2048], BF16), ("opw", [128, 1024], BF16),
        ("xpw", [128, 320], F32), ("dtw", [RNK, 1024], F32),
        ("convw", [128, 32], F32), ("convb", [128, 8], F32),
        ("dtb", [128, 8], F32), ("acoef", [128, 128], F32), ("ddd", [128, 8], F32),
        ("tdec", [128, 8], F32), ("tfir", [128, 8], F32), ("tmk", [128, 8], F32),
        ("tmv", [128, 8], F32), ("tmr", [128, 8], F32),
        ("wdec", [128, 8], F32), ("ewc", [128, 8], F32),
        ("identb", [128, 128], BF16), ("onesf", [128, 1], F32),
        ("carry", [128, 32], F32),
        ("pcarry", [128, 12], F32),
        ("wkvp", [128, 8], F32),
        ("xhalo", [128, 16], F32),
        ("w0slab", [128, 1024], I32), ("w0bf", [128, 2048], BF16),
        ("i0", [128, 512], I32), ("i1", [128, 512], I32), ("i2", [128, 512], I32),
        ("mxi", [128, 2], I32),
        ("colz", [128, 16], F32),
        ("mxc", [128, 8], F32),
        ("rowz", [1, 128], F32),
        ("bcst", [128, 1024], BF16),
        ("cvals", [128, 4], F32),
        ("jnk", [128, 96], F32),
    ]

    with nc.cleanup_on_exit():
        ctxs = []
        sb = {}
        for name, shape, dt in small_specs:
            t = nc.sbuf_tensor("sb_" + name, shape, dt)
            sb[name] = t.__enter__()
            ctxs.append(t)
        t = nc.sbuf_tensor("FA", [128, FA_COLS], F32)
        sb["FA"] = t.__enter__()
        ctxs.append(t)
        t = nc.sbuf_tensor("BFA", [128, BF_COLS], BF16)
        sb["BFA"] = t.__enter__()
        ctxs.append(t)

        nc.const_aps.register(F32, EPS, sb["cvals"][:, 0:1])
        nc.const_aps.register(F32, -0.5, sb["cvals"][:, 1:2])
        nc.const_aps.register(F32, 1.0 / 127.0, sb["cvals"][:, 2:3])

        ps = {}
        for name, shape in [("pa", [128, 512]), ("pb", [128, 512]),
                            ("pc", [128, 512]), ("pd", [128, 512]),
                            ("py0", [128, 1024]), ("py1", [128, 1024])]:
            t = nc.psum_tensor(name, shape, F32)
            ps[name] = t.__enter__()
            ctxs.append(t)

        sems = {}
        for name in CHANNELS:
            t = nc.semaphore(name)
            sems[name] = t.__enter__()
            ctxs.append(t)

        blk = nc.Block()
        blk.__enter__()
        chans = {}

        def mk(ename, eng):
            for cn in CHANNELS:
                chans[cn] = Ch(cn)
            em = Em(nc, ename, eng, chans, sems)
            program(em, nc, din, dout, dsc, sb, ps)

        @blk.sync
        def _(e):
            mk("sync", e)

        @blk.tensor
        def _(e):
            mk("tensor", e)

        @blk.scalar
        def _(e):
            mk("scalar", e)

        @blk.vector
        def _(e):
            mk("vector", e)

        @blk.gpsimd
        def _(e):
            mk("gpsimd", e)

        blk.__exit__(None, None, None)
        for t in reversed(ctxs):
            t.__exit__(None, None, None)
    return nc



def program(em, nc, din, dout, dsc, sb, ps):
    FA = sb["FA"]
    BFA = sb["BFA"]

    def fa(c0, n):
        return FA[:, c0:c0 + n]

    def bfa(c0, n):
        return BFA[:, c0:c0 + n]

    ctx = dict(em=em, nc=nc, din=din, dout=dout, dsc=dsc, sb=sb, ps=ps, fa=fa, bfa=bfa)
    phases = [phase0, phase1, phase2, phase3, phase4]
    upto = len(phases) if CUT is None else CUT + 1
    for ph in phases[:upto]:
        ph(**ctx)
    if DEBUG and upto == len(phases):
        em.dma("gpsimd", dsc["dsagi"].ap(), dsc["sagi"].ap(),
               waits=["s_wr"], inc="dbg_wr")
        em.dma("gpsimd", dsc["dx1"].ap(), dsc["x1agi"].ap(),
               waits=["c_wr"], inc="dbg_wr")
        em.dma("gpsimd", dsc["ds1"].ap(), dsc["s1agi"].ap(),
               waits=["p3_wr"], inc="dbg_wr")
        em.dma("gpsimd", dsc["dar"].ap()[0:1, 0:2], dsc["ar1o"].ap(),
               waits=["cc"], inc="dbg_wr")
        em.dma("gpsimd", dsc["dar"].ap()[0:1, 2:4], dsc["ar2o"].ap(),
               inc="dbg_wr")
        em.op("gpsimd", lambda e: e.nop(), waits=["dbg_wr"], inc=None)


def vp(em, sb, ap, ch, p, c):
    em.op("vector", lambda e, ap=ap, p=p, c=c: e.tensor_copy(
        sb["jnk"][0:p, 0:c], ap), inc=ch)


def ln_rows(em, sb, fa, psrow, rw0, chS, chV, chT, extraw=()):
    rrow = fa(rw0, 512)[0:1, :]
    mrow = fa(rw0 + 512, 512)[0:1, :]
    em.op("vector", lambda e: e.tensor_scalar(mrow, psrow[0:1, :], 1.0 / C, None,
                                              ALU.mult),
          waits=[chT] + list(extraw), inc=chV)
    em.op("vector", lambda e: e.scalar_tensor_tensor(rrow, mrow, -1.0, mrow,
                                                     ALU.mult, ALU.mult), inc=chV)
    em.op("vector", lambda e: e.scalar_tensor_tensor(rrow, psrow[32:33, :], 1.0 / C,
                                                     rrow, ALU.mult, ALU.add),
          inc=chV)
    em.op("scalar", lambda e: e.activation(rrow, rrow, AF.Ln, bias=EPS),
          waits=[chV], inc=chS)
    em.op("scalar", lambda e: e.activation(rrow, rrow, AF.Exp, scale=-0.5), inc=chS)
    em.op("vector", lambda e: e.scalar_tensor_tensor(mrow, mrow, -1.0, rrow,
                                                     ALU.mult, ALU.mult),
          waits=[chS], inc=chV)


def phase0(em, nc, din, dout, dsc, sb, ps, fa, bfa):
    em.op("vector", lambda e: e.memset(sb["cvals"][:, 0:1], EPS), inc="pm_v")
    em.op("vector", lambda e: e.memset(sb["cvals"][:, 1:2], -0.5), inc="pm_v")
    em.op("vector", lambda e: e.memset(sb["cvals"][:, 2:3], 1.0 / 127.0), inc="pm_v")
    if P0MASK & 1:
        for nm in ["convw", "convb", "dtb", "ddd", "tdec", "tfir", "tmk", "tmv",
                   "tmr", "identb", "onesf"]:
            em.dma("sync", sb[nm][:, :], din[nm].ap(), inc="w_ld")
        em.dma("sync", sb["xpw"][:, :], din["xpwT"].ap(), inc="w_ld")
        em.dma("sync", sb["dtw"][:, :], din["dtwT"].ap(), inc="w_ld")
        em.dma("sync", fa(P0_WST, 2048), din["ipwT"].ap(), inc="w_ld")
        em.dma("sync", fa(P0_WST + 2048, 1024), din["opwT"].ap(), inc="w_ld")
        em.dma("sync", fa(P0_WST + 3072, 128), din["alog"].ap(), inc="w_ld")
    if P0MASK & 2:
        em.op("scalar", lambda e: e.activation(sb["ipw"][:, :], fa(P0_WST, 2048),
                                               AF.Copy),
              waits=["w_ld"], inc="pm_s")
        em.op("scalar", lambda e: e.activation(sb["opw"][:, :], fa(P0_WST + 2048, 1024),
                                               AF.Copy), inc="pm_s")
    if P0MASK & 4:
        em.op("scalar", lambda e: e.activation(sb["acoef"][:, :], fa(P0_WST + 3072, 128),
                                               AF.Exp),
              waits=["w_ld"], inc="pm_s")
        em.op("vector", lambda e: e.tensor_scalar(sb["acoef"][:, :], sb["acoef"][:, :],
                                                  -1.0, None, ALU.mult),
              waits=["pm_s"], inc="pm_v")
        em.op("scalar", lambda e: e.activation(sb["wdec"][:, :], sb["tdec"][:, :],
                                               AF.Exp),
              inc="pm_s")
        em.op("vector", lambda e: e.tensor_scalar(sb["wdec"][:, :], sb["wdec"][:, :],
                                                  -1.0, None, ALU.mult),
              waits=["pm_s"], inc="pm_v")
        em.op("scalar", lambda e: e.activation(sb["ewc"][:, :], sb["wdec"][:, :],
                                               AF.Exp),
              waits=["pm_v"], inc="pm_s")
    if P0MASK & 8:
        em.op("vector", lambda e: e.memset(sb["carry"][:, :], 0.0), inc="pm_v")
        em.op("vector", lambda e: e.memset(sb["mxc"][:, :], 0.0), inc="pm_v")
        em.op("vector", lambda e: e.memset(sb["colz"][:, :], 0.0), inc="pm_v")
        em.op("vector", lambda e: e.memset(sb["xhalo"][:, :], 0.0), inc="pm_v")
    if P0PART < 1:
        return

    def pick(e, core, then):
        g = core // 4
        for ct in range(4):
            e.tensor_copy(sb["wkvp"][:, ct:ct + 1], sb["tfir"][:, 4 * g + ct:4 * g + ct + 1])
        for ct in range(4):
            ins = e.tensor_copy(sb["wkvp"][:, 4 + ct:5 + ct],
                                sb["ewc"][:, 4 * g + ct:4 * g + ct + 1])
        then(ins)

    em.switch("vector", pick, waits=["pm_s"], inc="pm_v", nincs=1, kper=1)
    vp(em, sb, sb["wkvp"][:, 0:8], "pm_v", 128, 8)

    def locbody(e, core, then):
        g = core // 4
        then(e.dma_start(out=dsc["xhh"].ap(),
                         in_=din["xbT"].ap()[g * 512:(g + 1) * 512, :]))
        then(e.dma_start(out=dsc["xth"].ap(),
                         in_=din["xbT"].ap()[:, g * THALF:(g + 1) * THALF]))

    em.switch("gpsimd", locbody, waits=(), inc="loc_wr", nincs=2)

    if P0PART < 2:
        return
    hist_v = {}
    for tc in range(8):
        xch = "x_ld" if tc % 2 == 0 else "x_ldb"
        xoff = P0_XC + (tc % 2) * 4096
        for i in range(8):
            dst = fa(xoff + i * 512, 512)
            em.dma("sync", dst,
                   din["xbT"].ap()[i * 128:(i + 1) * 128, tc * 512:(tc + 1) * 512],
                   waits=[("st_v", hist_v.get(tc - 2, 0))], inc=xch)
        hh0 = P0_HH + (tc % 2) * 2048
        for j in range(4):
            em.dma("sync", fa(hh0 + j * 512, 512),
                   dsc["xhh"].ap()[j * 128:(j + 1) * 128, tc * 512:(tc + 1) * 512],
                   waits=[("loc_wr", 32), ("st_v", hist_v.get(tc - 2, 0))], inc=xch)
        sqt = {}
        for i in range(8):
            dst = fa(xoff + i * 512, 512)
            sq = fa(P0_SQ + (i % 2) * 512, 512)
            em.op("scalar", lambda e, dst=dst, sq=sq: e.activation(sq, dst, AF.Square),
                  waits=[xch, ("st_t", sqt.get(i - 2, 0))], inc="st_s")
            em.op("tensor", lambda e, dst=dst, i=i: e.matmul(
                ps["pa"][0:1, :], sb["onesf"][:, :], dst, start=(i == 0), stop=(i == 7)),
                waits=[xch, "w_ld", ("st_v", em.val("st_v"))], inc="st_t")
            em.op("tensor", lambda e, sq=sq, i=i: e.matmul(
                ps["pa"][32:33, :], sb["onesf"][:, :], sq, start=(i == 0), stop=(i == 7)),
                waits=["st_s"], inc="st_t")
            sqt[i] = em.val("st_t")
        ln_rows(em, sb, fa, ps["pa"], P0_RW, "st_s", "st_v", "st_t",
                extraw=[("row_wr", em.val("row_wr"))])
        em.dma("gpsimd", dsc["lnrow"].ap()[0:1, tc * 512:(tc + 1) * 512],
               fa(P0_RW, 512)[0:1, :], waits=["st_v"], inc="row_wr")
        em.dma("gpsimd", dsc["lnrow"].ap()[1:2, tc * 512:(tc + 1) * 512],
               fa(P0_RW + 512, 512)[0:1, :], inc="row_wr")
        lr = P0_LR + (tc % 2) * 1024
        rch = "rep0" if tc % 2 == 0 else "rep0b"
        em.dma("gpsimd", fa(lr, 512),
               dsc["lnrow"].ap()[0:1, tc * 512:(tc + 1) * 512].to_broadcast([128, 512]),
               waits=["row_wr", ("st_v", hist_v.get(tc - 2, 0))], inc=rch)
        em.dma("gpsimd", fa(lr + 512, 512),
               dsc["lnrow"].ap()[1:2, tc * 512:(tc + 1) * 512].to_broadcast([128, 512]),
               inc=rch)
        for j in range(4):
            src = fa(hh0 + j * 512, 512)
            dst = bfa(B1_XNH + j * 4096 + tc * 512, 512)
            em.op("vector", lambda e, src=src, dst=dst, lr=lr: e.tensor_tensor(
                dst, src, fa(lr, 512), ALU.mult),
                waits=[rch, xch], inc="st_v")
            em.op("vector", lambda e, dst=dst, lr=lr: e.tensor_tensor(
                dst, dst, fa(lr + 512, 512), ALU.add), inc="st_v")
        hist_v[tc] = em.val("st_v")

    if P0PART < 3:
        return
    def rowbody(e, core, then):
        g = core // 4
        then(e.dma_start(out=dsc["lnrowth"].ap(),
                         in_=dsc["lnrow"].ap()[:, g * THALF:(g + 1) * THALF]))

    em.switch("gpsimd", rowbody, waits=["row_wr"], inc="row_wr", nincs=1)
    for tc in range(4):
        xch = "x_ld" if tc % 2 == 0 else "x_ldb"
        rch = "rep0" if tc % 2 == 0 else "rep0b"
        xoff = P0_XC + (tc % 2) * 4096
        for i in range(8):
            em.dma("sync", fa(xoff + i * 512, 512),
                   dsc["xth"].ap()[i * 128:(i + 1) * 128, tc * 512:(tc + 1) * 512],
                   waits=[("loc_wr", 32), ("st_v", hist_v.get(8 + tc - 2, hist_v[7]))],
                   inc=xch)
        lr = P0_LR + (tc % 2) * 1024
        em.dma("gpsimd", fa(lr, 512),
               dsc["lnrowth"].ap()[0:1, tc * 512:(tc + 1) * 512].to_broadcast([128, 512]),
               waits=["row_wr", ("st_v", hist_v.get(8 + tc - 2, hist_v[7]))],
               inc=rch)
        em.dma("gpsimd", fa(lr + 512, 512),
               dsc["lnrowth"].ap()[1:2, tc * 512:(tc + 1) * 512].to_broadcast([128, 512]),
               inc=rch)
        for i in range(8):
            src = fa(xoff + i * 512, 512)
            th = bfa(B1_XNT + i * 2048 + tc * 512, 512)
            em.op("vector", lambda e, src=src, th=th, lr=lr: e.tensor_tensor(
                th, src, fa(lr, 512), ALU.mult), waits=[rch, xch], inc="st_v")
            em.op("vector", lambda e, th=th, lr=lr: e.tensor_tensor(
                th, th, fa(lr + 512, 512), ALU.add), inc="st_v")
        hist_v[8 + tc] = em.val("st_v")
    if DEBUG:
        em.dma("gpsimd", dsc["dbgb"].ap()[5], bfa(B1_XNT, 2048), waits=["st_v"],
               inc="dbg_wr")
        em.dma("gpsimd", dsc["dbgb"].ap()[6], bfa(B1_XNH, 2048), waits=["st_v"],
               inc="dbg_wr")


def phase1(em, nc, din, dout, dsc, sb, ps, fa, bfa):
    unit = [0]
    it = 0
    for h in range(NHPC):
        for ck in range(NCH):
            if it < P1HCK:
                l0_pre(em, din, dsc, sb, ps, fa, bfa, h, ck)
                if P1SCAN:
                    l0_scan(em, dsc, sb, ps, fa, bfa, h, ck)
                it += 1
            for _ in range(6):
                if unit[0] < min(96, P1U):
                    ffn0_unit(em, din, dsc, sb, ps, fa, bfa, unit[0])
                    unit[0] += 1
    while unit[0] < min(96, P1U):
        ffn0_unit(em, din, dsc, sb, ps, fa, bfa, unit[0])
        unit[0] += 1


def l0_pre(em, din, dsc, sb, ps, fa, bfa, h, ck):
    t0 = ck * TC1
    war = [("sc_v", em.val("sc_v")), ("epi_v", em.val("epi_v")),
           ("sc_s", em.val("sc_s"))]
    if P1SKIP:
        if P1VAR >= 6:
            dtbcol = sb["dtb"][:, 0:1]
            dsl = fa(P1_DL, 512)
            if P1VAR == 6:
                em.op("vector", lambda e, dsl=dsl, dtbcol=dtbcol: e.tensor_scalar(
                    dsl, ps["pa"][:, :], dtbcol, 0.0, ALU.add, ALU.max),
                    waits=[("pm_v", em.val("pm_v"))], inc="pre_v")
            elif P1VAR == 7:
                em.op("vector", lambda e, dsl=dsl, dtbcol=dtbcol: e.tensor_scalar(
                    dsl, fa(P1_DB, 512), dtbcol, 0.0, ALU.add, ALU.max),
                    waits=[("pm_v", em.val("pm_v"))], inc="pre_v")
            elif P1VAR == 8:
                em.op("vector", lambda e, dsl=dsl: e.tensor_copy(
                    dsl, ps["pa"][:, :]),
                    waits=[("pm_v", em.val("pm_v"))], inc="pre_v")
            return
        nit = 1 if P1VAR >= 3 else 2
        for dh in range(nit):
            for tc in range(nit):
                if P1VAR != 5:
                    em.op("tensor", lambda e, dh=dh, tc=tc: e.matmul(
                        ps["pa"][:, :],
                        sb["dtw"][:, (h * 2 + dh) * 128:(h * 2 + dh) * 128 + 128],
                        fa(P1_DB + tc * 512, 512)[0:RNK, :], start=True, stop=True),
                        waits=["pre_s", ("pre_v", em.val("pre_v"))], inc="pre_t")
                dtbcol = sb["dtb"][:, h * 2 + dh:h * 2 + dh + 1]
                dsl = fa(P1_DL + dh * 1024 + tc * 512, 512)
                if P1VAR != 4:
                    em.op("vector", lambda e, dsl=dsl, dtbcol=dtbcol: e.tensor_scalar(
                        dsl, ps["pa"][:, :], dtbcol, 0.0, ALU.add, ALU.max),
                        waits=["pre_t", ("sc_s", em.val("sc_s"))], inc="pre_v")
                if P1PART >= 6:
                    em.op("scalar", lambda e, dtbcol=dtbcol: e.activation(
                        fa(P1_T0, 512), ps["pa"][:, :], AF.Abs, bias=dtbcol),
                        waits=["pre_t"], inc="pre_s")
        return
    for e4 in range(4):
        for nh in range(2):
            pst = ps["pa"] if (e4 * 2 + nh) % 2 == 0 else ps["pb"]
            em.op("tensor", lambda e, e4=e4, nh=nh, pst=pst: e.matmul(
                pst[:, :], sb["ipw"][:, h * 512 + e4 * 128:h * 512 + (e4 + 1) * 128],
                bfa(B1_XNH + h * 4096 + t0 + nh * 512, 512), start=True, stop=True),
                waits=["pm_v", ("st_v", em.val("st_v")), ("pre_s", em.val("pre_s")),
                       ("pre_v", em.val("pre_v"))],
                inc="pre_t")
            if e4 < 2:
                dst = fa(P1_U + e4 * 1027 + 3 + nh * 512, 512)
                em.op("scalar", lambda e, dst=dst, pst=pst: e.activation(
                    dst, pst[:, :], AF.Copy), waits=["pre_t"] + war, inc="pre_s")
            else:
                dst = fa(P1_SZ + (e4 - 2) * 1024 + nh * 512, 512)
                em.op("scalar", lambda e, dst=dst, pst=pst: e.activation(
                    dst, pst[:, :], AF.Silu), waits=["pre_t"] + war, inc="pre_s")
    if P1PART < 2:
        return
    for dh in range(2):
        if ck == 0:
            em.op("vector", lambda e, dh=dh: e.memset(fa(P1_U + dh * 1027, 3), 0.0),
                  waits=[("pre_s", em.val("pre_s"))], inc="pre_v")
        else:
            em.op("vector", lambda e, dh=dh: e.tensor_copy(
                fa(P1_U + dh * 1027, 3), sb["colz"][:, 8 + dh * 3:11 + dh * 3]),
                waits=[("pre_s", em.val("pre_s"))], inc="pre_v")
    for dh in range(2):
        ub = P1_U + dh * 1027
        ucd = fa(P1_UC + dh * 1024, 1024)
        cwcols = (h * 2 + dh) * 4
        cbcol = sb["convb"][:, h * 2 + dh:h * 2 + dh + 1]
        em.op("vector", lambda e, ub=ub, ucd=ucd, cwcols=cwcols, cbcol=cbcol:
              e.tensor_scalar(ucd, fa(ub, 1024), sb["convw"][:, cwcols:cwcols + 1],
                              cbcol, ALU.mult, ALU.add),
              waits=["pre_s"], inc="pre_v")
        for k in range(1, KC):
            em.op("vector", lambda e, k=k, ub=ub, ucd=ucd, cwcols=cwcols:
                  e.scalar_tensor_tensor(ucd, fa(ub + k, 1024),
                                         sb["convw"][:, cwcols + k:cwcols + k + 1],
                                         ucd, ALU.mult, ALU.add), inc="pre_v")
        if ck < NCH - 1:
            em.op("vector", lambda e, ub=ub, dh=dh: e.tensor_copy(
                sb["colz"][:, 8 + dh * 3:11 + dh * 3], fa(ub + 1024, 3)), inc="pre_v")
        em.op("scalar", lambda e, ucd=ucd: e.activation(ucd, ucd, AF.Silu),
              waits=["pre_v"], inc="pre_s")
    if P1PART < 3:
        return
    for tc in range(2):
        for dh in range(2):
            em.op("tensor", lambda e, tc=tc, dh=dh: e.matmul(
                ps["pb"][0:40, :], sb["xpw"][:, (h * 2 + dh) * 40:(h * 2 + dh) * 40 + 40],
                fa(P1_UC + dh * 1024 + tc * 512, 512), start=(dh == 0), stop=(dh == 1)),
                waits=["pre_s"], inc="pre_t")
        em.op("scalar", lambda e, tc=tc: e.activation(
            fa(P1_DB + tc * 512, 512)[0:40, :], ps["pb"][0:40, :], AF.Copy),
            waits=["pre_t"], inc="pre_s")
    if P1PART < 4:
        return
    em.op("scalar", lambda e: e.activation(sb["bcst"][0:40, :],
                                           fa(P1_DB, 1024)[0:40, :], AF.Copy),
          waits=["pre_s", ("brc_wr", em.val("brc_wr"))], inc="pre_s")
    em.dma("gpsimd", dsc["brc"].ap()[h, :, t0:t0 + TC1], sb["bcst"][8:40, :],
           waits=["pre_s"], inc="brc_wr")
    if P1PART < 5:
        return
    for dh in range(2):
        for tc in range(2):
            em.op("tensor", lambda e, dh=dh, tc=tc: e.matmul(
                ps["pa"][:, :], sb["dtw"][:, (h * 2 + dh) * 128:(h * 2 + dh) * 128 + 128],
                fa(P1_DB + tc * 512, 512)[0:RNK, :], start=True, stop=True),
                waits=["pre_s", ("pre_v", em.val("pre_v"))], inc="pre_t")
            dtbcol = sb["dtb"][:, h * 2 + dh:h * 2 + dh + 1]
            dsl = fa(P1_DL + dh * 1024 + tc * 512, 512)
            em.op("vector", lambda e, dtbcol=dtbcol: e.tensor_scalar(
                fa(P1_T1, 512), ps["pa"][:, :], dtbcol, None, ALU.add),
                waits=["pre_t", ("pre_s", em.val("pre_s")),
                       ("sc_s", em.val("sc_s"))], inc="pre_v")
            em.op("vector", lambda e, dsl=dsl: e.tensor_scalar(
                dsl, fa(P1_T1, 512), 0.0, None, ALU.max), inc="pre_v")
            if P1PART < 6:
                continue
            em.op("scalar", lambda e: e.activation(
                fa(P1_T0, 512), fa(P1_T1, 512), AF.Abs),
                waits=[("pre_v", em.val("pre_v"))], inc="pre_s")
            if P1PART >= 7:
                em.op("scalar", lambda e: e.activation(fa(P1_T1, 512), fa(P1_T0, 512),
                                                       AF.Exp, scale=-1.0),
                      waits=[("pre_v", em.val("pre_v"))], inc="pre_s")
            if P1PART >= 8:
                em.op("scalar", lambda e: e.activation(fa(P1_T1, 512), fa(P1_T1, 512),
                                                       AF.Ln, bias=1.0), inc="pre_s")
            if P1PART < 9:
                continue
            em.op("vector", lambda e, dsl=dsl: e.tensor_tensor(
                dsl, dsl, fa(P1_T1, 512), ALU.add), waits=["pre_s"], inc="pre_v")
        if P1PART < 9:
            continue
        em.op("vector", lambda e, dh=dh: e.tensor_tensor(
            bfa(B1_DU + dh * 1024, 1024), fa(P1_DL + dh * 1024, 1024),
            fa(P1_UC + dh * 1024, 1024), ALU.mult), inc="pre_v")
    if DEBUG and h == 0 and ck == 0:
        em.dma("gpsimd", dsc["dbg"].ap()[5][:, 0:128], sb["acoef"][:, :],
               waits=["pre_v"], inc="dbg_wr")
        em.dma("gpsimd", dsc["dbg"].ap()[0], fa(P1_DL, 2048), waits=["pre_v"],
               inc="dbg_wr")
        em.dma("gpsimd", dsc["dbg"].ap()[1], fa(P1_UC, 2048), waits=["pre_v"],
               inc="dbg_wr")
        em.dma("gpsimd", dsc["dbg"].ap()[2], fa(P1_SZ, 2048), waits=["pre_v"],
               inc="dbg_wr")
        em.dma("gpsimd", dsc["dbgb"].ap()[0], bfa(B1_DU, 2048), waits=["pre_v"],
               inc="dbg_wr")


def l0_scan(em, dsc, sb, ps, fa, bfa, h, ck):
    t0 = ck * TC1
    tbase = em.val("sc_t")
    da_guard = {0: 0, 1: 0}
    rep_guard = {0: 0, 1: 0}
    for n in range(NST):
        par = n % 2
        rch = "rep_ld" if par == 0 else "rep_ldb"
        em.dma("gpsimd", bfa(B1_BR + par * 1024, 1024),
               dsc["brc"].ap()[h, n:n + 1, t0:t0 + TC1].to_broadcast([128, TC1]),
               waits=["brc_wr", ("sc_v", rep_guard[par])],
               inc=rch)
        em.dma("gpsimd", bfa(B1_CR + par * 1024, 1024),
               dsc["brc"].ap()[h, NST + n:NST + n + 1, t0:t0 + TC1]
               .to_broadcast([128, TC1]),
               inc=rch)
        for dh in range(2):
            yps = ps["py0"] if dh == 0 else ps["py1"]
            acol = sb["acoef"][:, (h * 2 + dh) * NST + n:(h * 2 + dh) * NST + n + 1]
            da = fa(P1_DA + dh * 1024, 1024)
            em.op("scalar", lambda e, da=da, acol=acol, dh=dh: e.activation(
                da, fa(P1_DL + dh * 1024, 1024), AF.Exp, scale=acol),
                waits=["pre_v", ("sc_v", da_guard[dh])], inc="sc_s")
            xt = bfa(B1_X, 1024)
            ht = bfa(B1_H, 1024)
            pr = bfa(B1_PR + (par * 2 + dh) * 1024, 1024)
            em.op("vector", lambda e, xt=xt, dh=dh, par=par: e.tensor_tensor(
                xt, bfa(B1_DU + dh * 1024, 1024), bfa(B1_BR + par * 1024, 1024),
                ALU.mult),
                waits=[rch, "sc_s", "pre_v"], inc="sc_v")
            init = 0.0 if ck == 0 else sb["carry"][:, dh * NST + n:dh * NST + n + 1]
            em.op("vector", lambda e, ht=ht, da=da, xt=xt, init=init:
                  e.tensor_tensor_scan(ht, da, xt, init, ALU.mult, ALU.add),
                  inc="sc_v")
            da_guard[dh] = em.val("sc_v")
            prw = [("sc_t", tbase + 4 * (n - 1))] if n >= 2 else []
            em.op("vector", lambda e, pr=pr, ht=ht, par=par: e.tensor_tensor(
                pr, ht, bfa(B1_CR + par * 1024, 1024), ALU.mult),
                waits=prw, inc="sc_v")
            if ck < NCH - 1:
                em.op("vector", lambda e, ht=ht, dh=dh, n=n: e.tensor_copy(
                    sb["carry"][:, dh * NST + n:dh * NST + n + 1], ht[:, TC1 - 1:TC1]),
                    inc="sc_v")
            if DEBUG and h == 0 and ck == 0 and dh == 0 and n == 0:
                em.dma("gpsimd", dsc["dbg"].ap()[3][:, 0:1024], da,
                       waits=["sc_v"], inc="dbg_wr")
                em.dma("gpsimd", dsc["dbgb"].ap()[1][:, 0:1024], xt,
                       waits=["sc_v"], inc="dbg_wr")
                em.dma("gpsimd", dsc["dbgb"].ap()[2][:, 0:1024], ht,
                       waits=["sc_v"], inc="dbg_wr")
                em.dma("gpsimd", dsc["dbgb"].ap()[3][:, 0:1024], pr,
                       waits=["sc_v"], inc="dbg_wr")
            if DEBUG and h == 0 and ck == 0 and n <= 2:
                em.op("vector", lambda e: e.engine_nop(), waits=["dbg_wr"],
                      inc="sc_v")
            for q in range(2):
                em.op("tensor", lambda e, yps=yps, pr=pr, q=q, n=n: e.matmul(
                    yps[:, q * 512:(q + 1) * 512], sb["identb"][:, :],
                    pr[:, q * 512:(q + 1) * 512], start=(n == 0), stop=(n == NST - 1)),
                    waits=["sc_v"], inc="sc_t")
        rep_guard[par] = em.val("sc_v")
    for dh in range(2):
        yps = ps["py0"] if dh == 0 else ps["py1"]
        dcol = sb["ddd"][:, h * 2 + dh:h * 2 + dh + 1]
        yg = bfa(B1_YG + dh * 1024, 1024)
        em.op("vector", lambda e, dh=dh, dcol=dcol, yps=yps: e.scalar_tensor_tensor(
            fa(P1_EP, 1024), fa(P1_UC + dh * 1024, 1024), dcol, yps[:, :],
            ALU.mult, ALU.add),
            waits=["sc_t", ("epi_t", em.val("epi_t"))], inc="epi_v")
        em.op("vector", lambda e, dh=dh, yg=yg: e.tensor_tensor(
            yg, fa(P1_EP, 1024), fa(P1_SZ + dh * 1024, 1024), ALU.mult),
            inc="epi_v")
        if DEBUG and h == 0 and ck == 0 and dh == 0:
            em.dma("gpsimd", dsc["dbg"].ap()[4][:, 0:1024], fa(P1_EP, 1024),
                   waits=["epi_v"], inc="dbg_wr")
            em.dma("gpsimd", dsc["dbgb"].ap()[4][:, 0:1024], yg,
                   waits=["epi_v"], inc="dbg_wr")
            em.op("vector", lambda e: e.engine_nop(), waits=["dbg_wr"], inc="epi_v")
        for q in range(2):
            em.op("tensor", lambda e, dh=dh, q=q, yg=yg: e.matmul(
                ps["pc" if q == 0 else "pd"][:, :],
                sb["opw"][:, (h * 2 + dh) * 128:(h * 2 + dh) * 128 + 128],
                yg[:, q * 512:q * 512 + 512], start=(dh == 0), stop=(dh == 1)),
                waits=["epi_v", ("f0_v", em.val("f0_v")), ("epi_s", em.val("epi_s"))],
                inc="epi_t")
    for q in range(2):
        em.op("scalar", lambda e, q=q: e.activation(
            fa(P1_SS + q * 512, 512), ps["pc" if q == 0 else "pd"][:, :], AF.Copy),
            waits=["epi_t", ("s_wr", em.val("s_wr"))], inc="epi_s")
    em.dma("gpsimd", dsc["sagi"].ap()[h * 128:(h + 1) * 128, t0:t0 + TC1],
           fa(P1_SS, 1024), waits=["epi_s"], inc="s_wr")


def ffn0_unit(em, din, dsc, sb, ps, fa, bfa, unit):
    hidc, tc = unit // 4, unit % 4
    if tc == 0:
        em.dma("sync", sb["w0slab"][:, 0:1024], din["wg0"].ap()[hidc],
               waits=[("f0_s", em.val("f0_s"))], inc="f0_ld")
        em.op("scalar", lambda e: e.activation(sb["w0bf"][:, 0:1024],
                                               sb["w0slab"][:, 0:1024],
                                               AF.Copy, scale=1.0 / 127.0),
              waits=["f0_ld", ("pm_v", 3), ("f0_t", em.val("f0_t"))], inc="f0_s")
        em.dma("sync", sb["w0slab"][:, 0:1024], din["wu0"].ap()[hidc],
               waits=[("f0_s", em.val("f0_s"))], inc="f0_ld")
        em.op("scalar", lambda e: e.activation(sb["w0bf"][:, 1024:2048],
                                               sb["w0slab"][:, 0:1024],
                                               AF.Copy, scale=1.0 / 127.0),
              waits=["f0_ld", ("f0_t", em.val("f0_t"))], inc="f0_s")
    for d in range(8):
        mv = bfa(B1_XNT + d * 2048 + tc * 512, 512)
        em.op("tensor", lambda e, d=d, mv=mv: e.matmul(
            ps["pc"][:, :], sb["w0bf"][:, d * 128:(d + 1) * 128], mv,
            start=(d == 0), stop=(d == 7)),
            waits=["f0_s", ("f0_v", em.val("f0_v")), ("epi_s", em.val("epi_s")),
                   ("st_v", em.val("st_v"))],
            inc="f0_t")
    for d in range(8):
        mv = bfa(B1_XNT + d * 2048 + tc * 512, 512)
        em.op("tensor", lambda e, d=d, mv=mv: e.matmul(
            ps["pd"][:, :], sb["w0bf"][:, 1024 + d * 128:1024 + (d + 1) * 128], mv,
            start=(d == 0), stop=(d == 7)), inc="f0_t")
    em.op("scalar", lambda e: e.activation(fa(P1_F0, 512), ps["pc"][:, :], AF.Silu),
          waits=["f0_t"], inc="f0_s")
    em.op("vector", lambda e: e.tensor_tensor(bfa(B1_HF, 512), fa(P1_F0, 512),
                                              ps["pd"][:, :], ALU.mult),
          waits=["f0_s", ("hf_wr", em.val("hf_wr"))], inc="f0_v")
    em.dma("gpsimd", dsc["hfd"].ap()[hidc, :, tc * 512:(tc + 1) * 512], bfa(B1_HF, 512),
           waits=["f0_v"], inc="hf_wr")


def phase2(em, nc, din, dout, dsc, sb, ps, fa, bfa):
    p1rel = [("pre_t", em.val("pre_t")), ("f0_t", em.val("f0_t")),
             ("sc_t", em.val("sc_t")), ("sc_v", em.val("sc_v")),
             ("epi_t", em.val("epi_t")), ("f0_v", em.val("f0_v")),
             ("sc_s", em.val("sc_s")), ("epi_v", em.val("epi_v"))]
    em.coll("AllGather", ALU.bypass, RG8, [dsc["sagi"].ap().opt()],
            [dsc["sago"].ap().opt()], waits=["s_wr"], inc="cc")
    for hc in range(24):
        em.dma("sync", sb["w0slab"][:, 0:1024], din["wd0"].ap()[hc],
               waits=[("g_s", em.val("g_s")), ("f0_t", em.val("f0_t"))],
               inc="f0_ld")
        em.op("scalar", lambda e, hc=hc: e.activation(
            bfa(B2_WD + hc * 1024, 1024), sb["w0slab"][:, 0:1024], AF.Copy,
            scale=1.0 / 127.0), waits=["f0_ld", ("pm_v", 3)] + p1rel, inc="g_s")
    if DEBUG:
        em.dma("gpsimd", dsc["dbgb"].ap()[7][:, 0:1024], bfa(B2_WD, 1024),
               waits=["g_s"], inc="dbg_wr")
    lp = lpbq_all(em, din, dsc, sb, fa, bfa, p1rel)
    for tc in range(4):
        for hc in range(24):
            em.dma("sync", bfa(B2_HF + hc * 512, 512),
                   dsc["hfd"].ap()[hc, :, tc * 512:(tc + 1) * 512],
                   waits=["hf_wr", ("g_t", tc * 192)] + p1rel, inc="hf_ld")
        if DEBUG and tc == 0:
            em.dma("gpsimd", dsc["dbgb"].ap()[7][:, 1024:1536], bfa(B2_HF, 512),
                   waits=["hf_ld"], inc="dbg_wr")
        for d in range(8):
            pst = ps["pc"] if d % 2 == 0 else ps["pd"]
            for hc in range(24):
                dbw = ["dbg_wr"] if DEBUG else []
                em.op("tensor", lambda e, d=d, hc=hc, pst=pst: e.matmul(
                    pst[:, :], bfa(B2_WD + hc * 1024 + d * 128, 128),
                    bfa(B2_HF + hc * 512, 512), start=(hc == 0), stop=(hc == 23)),
                    waits=["g_s", "hf_ld", ("g_v", em.val("g_v"))] + dbw, inc="g_t")
            em.op("scalar", lambda e, d=d, pst=pst: e.activation(
                fa(P2_G, 512), pst[:, :], AF.Silu),
                waits=["g_t", ("c_wr", em.val("c_wr"))], inc="g_s")
            em.dma("gpsimd", dsc["gd"].ap()[d, :, tc * 512:(tc + 1) * 512],
                   fa(P2_G, 512), waits=["g_s"], inc="c_wr")
        next(lp, None)
        next(lp, None)
    for _ in lp:
        pass
    def sbody(e, core, then):
        g = core // 4
        b = core % 4
        for rk, half in ((b, 0), (b + 4, 1)):
            ins = e.dma_start(
                out=dsc["sloc"].ap()[half * 512:(half + 1) * 512, :],
                in_=dsc["sago"].ap()[rk * 512:(rk + 1) * 512,
                                     g * THALF:(g + 1) * THALF])
            then(ins)

    em.switch("gpsimd", sbody, waits=["cc"], inc="loc_wr", nincs=2)
    sw_done = em.val("loc_wr")
    for tc in range(4):
        for d in range(8):
            em.dma("sync", fa(P2_X + d * 512, 512),
                   dsc["xth"].ap()[d * 128:(d + 1) * 128, tc * 512:(tc + 1) * 512],
                   waits=[("c_v", em.val("c_v")), ("loc_wr", sw_done)] + p1rel,
                   inc="c_ld")
        for d in range(8):
            em.dma("sync", fa(P2_S + d * 512, 512),
                   dsc["sloc"].ap()[d * 128:(d + 1) * 128, tc * 512:(tc + 1) * 512],
                   waits=[("loc_wr", sw_done), ("c_v", em.val("c_v"))], inc="c_ld")
        for d in range(8):
            em.dma("sync", fa(P2_X1 + d * 512, 512),
                   dsc["gd"].ap()[d, :, tc * 512:(tc + 1) * 512],
                   waits=["c_wr", ("c_v", em.val("c_v"))], inc="c_ld")
        for d in range(8):
            em.op("vector", lambda e, d=d: e.scalar_tensor_tensor(
                fa(P2_X1 + d * 512, 512), fa(P2_S + d * 512, 512), 1.0,
                fa(P2_X1 + d * 512, 512), ALU.mult, ALU.mult),
                waits=["c_ld"], inc="c_v")
            em.op("vector", lambda e, d=d: e.tensor_tensor(
                fa(P2_X1 + d * 512, 512), fa(P2_X1 + d * 512, 512),
                fa(P2_X + d * 512, 512), ALU.add), inc="c_v")
            em.dma("gpsimd", dsc["x1agi"].ap()[d * 128:(d + 1) * 128,
                                               tc * 512:(tc + 1) * 512],
                   fa(P2_X1 + d * 512, 512), waits=["c_v"], inc="c_wr")
    em.coll("AllGather", ALU.bypass, RG8, [dsc["x1agi"].ap().opt()],
            [dsc["x1ago"].ap().opt()], waits=["c_wr"], inc="cc")


def lpbq_all(em, din, dsc, sb, fa, bfa, p1rel):
    for wnm, qnm, ntile, icols in (("wg1", "wqg", 24, C), ("wu1", "wqu", 24, C),
                                   ("wd1", "wqd", 8, HID)):
        for ot in range(ntile):
            nb = icols // 32
            w = fa(P2_LW, icols)
            wv = w.rearrange("p (nb b) -> p nb b", b=32)
            smax = fa(P2_LS, nb)
            em.dma("sync", w, din[wnm].ap()[ot * 128:(ot + 1) * 128, :],
                   waits=[("q2_wr", em.val("q2_wr"))] + p1rel,
                   inc="q2_ld")
            em.op("vector", lambda e, wv=wv, smax=smax: e.tensor_reduce(
                smax, wv, AX.X, ALU.max, apply_absolute_value=True),
                waits=["q2_ld"], inc="q2_v")
            vp(em, sb, smax, "q2_v", 128, nb)
            em.op("vector", lambda e, smax=smax: e.tensor_scalar(
                smax, smax, 1.0 / 7.0, 1e-12, ALU.mult, ALU.max), inc="q2_v")
            vp(em, sb, smax, "q2_v", 128, nb)
            srec = fa(P2_LS + 128, nb)
            em.op("vector", lambda e, srec=srec, smax=smax: e.reciprocal(srec, smax),
                  inc="q2_v")
            vp(em, sb, srec, "q2_v", 128, nb)
            sbb = smax.unsqueeze(2).to_broadcast([128, nb, 32])
            srb = srec.unsqueeze(2).to_broadcast([128, nb, 32])
            em.op("vector", lambda e, wv=wv, srb=srb: e.tensor_tensor(
                wv, wv, srb, ALU.mult), inc="q2_v")
            for cchunk in range(icols // 512):
                c0 = cchunk * 512
                em.op("vector", lambda e, c0=c0, w=w: e.tensor_scalar(
                    sb["i0"][:, :], w[:, c0:c0 + 512], RC, -RC, ALU.add, ALU.add),
                    inc="q2_v")
                em.op("vector", lambda e, c0=c0, w=w: e.tensor_copy(
                    w[:, c0:c0 + 512], sb["i0"][:, :]), inc="q2_v")
            em.op("vector", lambda e, wv=wv, sbb=sbb: e.tensor_tensor(
                bfa(B2_HF + 12288, icols).rearrange("p (nb b) -> p nb b", b=32),
                wv, sbb, ALU.mult), inc="q2_v")
            em.dma("gpsimd", dsc[qnm].ap()[ot * 128:(ot + 1) * 128, :],
                   bfa(B2_HF + 12288, icols), waits=["q2_v"], inc="q2_wr")
            yield


def phase3(em, nc, din, dout, dsc, sb, ps, fa, bfa):
    p2rel = [("g_t", em.val("g_t")), ("c_v", em.val("c_v")),
             ("q2_v", em.val("q2_v")), ("q2_wr", em.val("q2_wr")),
             ("g_s", em.val("g_s")), ("c_wr", em.val("c_wr"))]
    for nm, dst in [("wkT", B3_WK), ("wvT", B3_WV), ("wrT", B3_WR)]:
        for d in range(8):
            em.dma("sync", fa(P3_X1 + (d % 2) * 512, 512),
                   din[nm].ap()[d * 128:(d + 1) * 128, :],
                   waits=[("p3_s", em.val("p3_s"))] + p2rel, inc="p3_ld")
            em.op("scalar", lambda e, dst=dst, d=d: e.activation(
                bfa(dst + d * 512, 512), fa(P3_X1 + (d % 2) * 512, 512), AF.Copy),
                waits=["p3_ld"], inc="p3_s")
    for ct in range(4):
        em.dma("sync", fa(P3_X1 + (ct % 2) * 1024, 1024),
               din["woT"].ap()[ct * 128:(ct + 1) * 128, :],
               waits=[("p3_s", em.val("p3_s"))], inc="p3_ld")
        em.op("scalar", lambda e, ct=ct: e.activation(
            bfa(B3_WO + ct * 1024, 1024), fa(P3_X1 + (ct % 2) * 1024, 1024), AF.Copy),
            waits=["p3_ld"], inc="p3_s")
    em.op("vector", lambda e: e.memset(sb["pcarry"][:, :], 0.0),
          waits=[("p3_s", em.val("p3_s"))], inc="p3_v")
    vp(em, sb, sb["pcarry"][:, :], "p3_v", 128, 12)
    em.op("vector", lambda e: e.tensor_scalar(sb["pcarry"][:, 0:4],
                                              sb["pcarry"][:, 0:4], 0.0, -1e30,
                                              ALU.mult, ALU.add), inc="p3_v")
    vp(em, sb, sb["pcarry"][:, 0:4], "p3_v", 128, 4)

    def x1body(e, core, then):
        b = core % 4
        for half in range(2):
            rk = b + 4 * half
            ins = e.dma_start(
                out=dsc["x1loc"].ap()[:, half * THALF:(half + 1) * THALF],
                in_=dsc["x1ago"].ap()[rk * 1024:(rk + 1) * 1024, :])
            then(ins)

    em.switch("gpsimd", x1body, waits=["cc"], inc="loc_wr", nincs=2)
    x1loc_done = em.val("loc_wr")

    for ck in range(8):
        t0 = ck * 512
        for d in range(8):
            em.dma("sync", fa(P3_X1 + d * 513 + 1, 512),
                   dsc["x1loc"].ap()[d * 128:(d + 1) * 128, t0:t0 + 512],
                   waits=[("loc_wr", x1loc_done), ("p3_v", em.val("p3_v")),
                          ("p3_t", em.val("p3_t")), ("p3_s", em.val("p3_s"))],
                   inc="p3_ld")
        sqt = {}
        for d in range(8):
            xs = fa(P3_X1 + d * 513 + 1, 512)
            sq = fa(P3_XN + (d % 2) * 512, 512)
            em.op("tensor", lambda e, xs=xs, d=d: e.matmul(
                ps["pa"][0:1, :], sb["onesf"][:, :], xs, start=(d == 0), stop=(d == 7)),
                waits=["p3_ld"], inc="p3_t")
            em.op("scalar", lambda e, xs=xs, sq=sq: e.activation(sq, xs, AF.Square),
                  waits=["p3_ld", ("p3_t", sqt.get(d - 2, 0))], inc="p3_s")
            em.op("tensor", lambda e, sq=sq, d=d: e.matmul(
                ps["pa"][32:33, :], sb["onesf"][:, :], sq, start=(d == 0), stop=(d == 7)),
                waits=["p3_s"], inc="p3_t")
            sqt[d] = em.val("p3_t")
        ln_rows(em, sb, fa, ps["pa"], P3_RW, "p3_s", "p3_v", "p3_t",
                extraw=[("row3", em.val("row3"))])
        em.dma("gpsimd", dsc["lnrow3"].ap()[0:1, t0:t0 + 512], fa(P3_RW, 512)[0:1, :],
               waits=["p3_v"], inc="row3")
        em.dma("gpsimd", dsc["lnrow3"].ap()[1:2, t0:t0 + 512],
               fa(P3_RW + 512, 512)[0:1, :], inc="row3")
        em.dma("gpsimd", fa(P3_LR, 512),
               dsc["lnrow3"].ap()[0:1, t0:t0 + 512].to_broadcast([128, 512]),
               waits=["row3", ("p3_v", em.val("p3_v"))], inc="rep0")
        em.dma("gpsimd", fa(P3_LR + 512, 512),
               dsc["lnrow3"].ap()[1:2, t0:t0 + 512].to_broadcast([128, 512]),
               inc="rep0")
        for d in range(8):
            xs = fa(P3_X1 + d * 513 + 1, 512)
            xn = fa(P3_XN + d * 513 + 1, 512)
            em.op("vector", lambda e, xs=xs, xn=xn: e.tensor_tensor(
                xn, xs, fa(P3_LR, 512), ALU.mult), waits=["rep0"], inc="p3_v")
            em.op("vector", lambda e, xn=xn: e.tensor_tensor(
                xn, xn, fa(P3_LR + 512, 512), ALU.add), inc="p3_v")
            em.op("vector", lambda e, xn=xn: e.tensor_reduce(
                sb["mxc"][:, 1:2], xn, AX.X, ALU.max, apply_absolute_value=True),
                inc="p3_v")
            vp(em, sb, sb["mxc"][:, 1:2], "p3_v", 128, 1)
            em.op("vector", lambda e: e.tensor_tensor(
                sb["mxc"][:, 0:1], sb["mxc"][:, 0:1], sb["mxc"][:, 1:2], ALU.max),
                inc="p3_v")
            vp(em, sb, sb["mxc"][:, 0:1], "p3_v", 128, 1)
            em.op("vector", lambda e, d=d: e.tensor_copy(
                fa(P3_XN + d * 513, 1), sb["xhalo"][:, d:d + 1]), inc="p3_v")
            vp(em, sb, fa(P3_XN + d * 513, 1), "p3_v", 128, 1)
            for mixn, mixdst in (("tmk", B3_XK), ("tmv", B3_XV), ("tmr", B3_XR)):
                col = sb[mixn][:, d:d + 1]
                em.op("vector", lambda e, d=d, xn=xn: e.tensor_tensor(
                    fa(P3_Y, 512), xn, fa(P3_XN + d * 513, 512), ALU.subtract),
                    inc="p3_v")
                em.op("vector", lambda e, d=d, col=col, mixdst=mixdst: e.scalar_tensor_tensor(
                    bfa(mixdst + d * 512, 512), fa(P3_Y, 512), col,
                    fa(P3_XN + d * 513, 512), ALU.mult, ALU.add), inc="p3_v")
            em.op("vector", lambda e, d=d, xn=xn: e.tensor_copy(
                sb["xhalo"][:, d:d + 1], xn[:, 511:512]), inc="p3_v")
            vp(em, sb, sb["xhalo"][:, d:d + 1], "p3_v", 128, 1)
        for ct in range(4):
            for nm, wof, src, pst in (("k", B3_WK, B3_XK, "pb"),
                                      ("v", B3_WV, B3_XV, "pc"),
                                      ("r", B3_WR, B3_XR, "pd")):
                for d in range(8):
                    em.op("tensor", lambda e, wof=wof, ct=ct, d=d, src=src, pst=pst:
                          e.matmul(ps[pst][:, :],
                                   bfa(wof + d * 512 + ct * 128, 128),
                                   bfa(src + d * 512, 512),
                                   start=(d == 0), stop=(d == 7)),
                          waits=["p3_v", ("p3_s", em.val("p3_s"))], inc="p3_t")
                if nm == "k":
                    em.op("scalar", lambda e, ct=ct, pst=pst: e.activation(
                        fa(P3_K + ct * 512, 512), ps[pst][:, :], AF.Copy),
                        waits=["p3_t"], inc="p3_s")
                elif nm == "v":
                    em.op("scalar", lambda e, ct=ct, pst=pst: e.activation(
                        fa(P3_V + ct * 512, 512), ps[pst][:, :], AF.Copy),
                        waits=["p3_t"], inc="p3_s")
                else:
                    em.op("scalar", lambda e, ct=ct, pst=pst: e.activation(
                        fa(P3_SR + ct * 512, 512), ps[pst][:, :], AF.Sigmoid),
                        waits=["p3_t"], inc="p3_s")
        for ct in range(4):
            wkv_tile(em, sb, fa, bfa, ct)
        for od in range(8):
            for ct in range(4):
                em.op("tensor", lambda e, od=od, ct=ct: e.matmul(
                    ps["pa"][:, :],
                    bfa(B3_WO + ct * 1024 + od * 128, 128),
                    bfa(B3_Y1 + ct * 512, 512), start=(ct == 0), stop=(ct == 3)),
                    waits=["p3_v", ("p3_s", em.val("p3_s"))], inc="p3_t")
            em.op("scalar", lambda e, od=od: e.activation(
                bfa(B3_ST, 512), ps["pa"][:, :], AF.Copy),
                waits=["p3_t", ("p3_wr", em.val("p3_wr"))], inc="p3_s")
            em.dma("gpsimd", dsc["s1agi"].ap()[od * 128:(od + 1) * 128, t0:t0 + 512],
                   bfa(B3_ST, 512), waits=["p3_s"], inc="p3_wr")
    em.dma("gpsimd", dsc["pmax"].ap()[0:1, :], sb["mxc"][:, 0:1], waits=["p3_v"],
           inc="row3")
    em.dma("gpsimd", sb["rowz"][0:1, :],
           dsc["pmax"].ap()[0:1, :], waits=["row3"],
           inc="rep0")
    em.op("vector", lambda e: e.tensor_reduce(sb["colz"][0:1, 2:3], sb["rowz"][0:1, :],
                                              AX.X, ALU.max), waits=["rep0"], inc="p3_v")
    vp(em, sb, sb["colz"][0:1, 2:3], "p3_v", 1, 1)
    em.dma("gpsimd", dsc["ar1i"].ap()[0:1, 0:1], sb["colz"][0:1, 2:3],
           waits=["p3_v"], inc="row3")
    em.dma("gpsimd", dsc["ar1i"].ap()[0:1, 1:2], sb["colz"][0:1, 2:3], inc="row3")
    em.coll("AllReduce", ALU.max, RG8, [dsc["ar1i"].ap().opt()],
            [dsc["ar1o"].ap().opt()], waits=["row3"], inc="cc")
    em.coll("AllGather", ALU.bypass, RG8, [dsc["s1agi"].ap().opt()],
            [dsc["s1ago"].ap().opt()], waits=["p3_wr"], inc="cc")


def wkv_tile(em, sb, fa, bfa, ct):
    kk = fa(P3_K + ct * 512, 512)
    vv = fa(P3_V + ct * 512, 512)
    sr = fa(P3_SR + ct * 512, 512)
    ab = fa(P3_AB + ct * 513, 513)
    bb = fa(P3_BB + ct * 513, 513)
    yy = fa(P3_Y, 512)
    E = fa(P3_XN, 512)
    E2 = fa(P3_XN + 513, 512)
    Ev = fa(P3_XN + 2 * 513, 512)
    ucol = sb["wkvp"][:, ct:ct + 1]
    ewcol = sb["wkvp"][:, 4 + ct:5 + ct]
    pcol = sb["pcarry"][:, ct:ct + 1]
    m = sb["colz"][:, 4:5]
    t1 = sb["colz"][:, 5:6]
    t2 = sb["colz"][:, 6:7]
    nm = sb["colz"][:, 9:10]
    um = sb["colz"][:, 10:11]
    ewb = yy
    em.op("vector", lambda e: e.tensor_reduce(t1, kk, AX.X, ALU.max),
          waits=[("p3_s", em.val("p3_s"))], inc="p3_v")
    vp(em, sb, t1, "p3_v", 128, 1)
    em.op("vector", lambda e: e.tensor_scalar(t2, ucol, 0.0, None, ALU.max), inc="p3_v")
    vp(em, sb, t2, "p3_v", 128, 1)
    em.op("vector", lambda e: e.tensor_tensor(t2, t2, t1, ALU.add), inc="p3_v")
    vp(em, sb, t2, "p3_v", 128, 1)
    em.op("vector", lambda e: e.tensor_tensor(m, t2, pcol, ALU.max), inc="p3_v")
    vp(em, sb, m, "p3_v", 128, 1)
    em.op("vector", lambda e: e.tensor_scalar(nm, m, -1.0, None, ALU.mult), inc="p3_v")
    vp(em, sb, nm, "p3_v", 128, 1)
    em.op("vector", lambda e: e.tensor_tensor(um, ucol, nm, ALU.add), inc="p3_v")
    vp(em, sb, um, "p3_v", 128, 1)
    em.op("scalar", lambda e: e.activation(E, kk, AF.Exp, bias=nm),
          waits=["p3_v"], inc="p3_s")
    em.op("scalar", lambda e: e.activation(E2, kk, AF.Exp, bias=um), inc="p3_s")
    em.op("vector", lambda e: e.tensor_tensor(t1, pcol, nm, ALU.add),
          waits=["p3_s"], inc="p3_v")
    vp(em, sb, t1, "p3_v", 128, 1)
    em.op("scalar", lambda e: e.activation(t1, t1, AF.Exp), waits=["p3_v"], inc="p3_s")
    em.op("vector", lambda e: e.tensor_tensor(ab[:, 0:1], sb["pcarry"][:, 4 + ct:5 + ct],
                                              t1, ALU.mult), waits=["p3_s"], inc="p3_v")
    vp(em, sb, ab[:, 0:1], "p3_v", 128, 1)
    em.op("vector", lambda e: e.tensor_tensor(bb[:, 0:1], sb["pcarry"][:, 8 + ct:9 + ct],
                                              t1, ALU.mult), inc="p3_v")
    vp(em, sb, bb[:, 0:1], "p3_v", 128, 1)
    em.op("vector", lambda e, ewb=ewb: e.tensor_scalar(
        ewb, kk, 0.0, ewcol, ALU.mult, ALU.add), inc="p3_v")
    em.op("vector", lambda e: e.tensor_tensor(Ev, E, vv, ALU.mult), inc="p3_v")
    em.op("vector", lambda e, ewb=ewb: e.tensor_tensor_scan(
        ab[:, 1:513], ewb, Ev, ab[:, 0:1],
        ALU.mult, ALU.add), inc="p3_v")
    em.op("vector", lambda e, ewb=ewb: e.tensor_tensor_scan(
        bb[:, 1:513], ewb, E, bb[:, 0:1],
        ALU.mult, ALU.add), inc="p3_v")
    em.op("vector", lambda e: e.tensor_tensor(yy, E2, vv, ALU.mult), inc="p3_v")
    em.op("vector", lambda e: e.tensor_tensor(yy, yy, ab[:, 0:512], ALU.add), inc="p3_v")
    em.op("vector", lambda e: e.tensor_tensor(E2, E2, bb[:, 0:512], ALU.add), inc="p3_v")
    em.op("vector", lambda e: e.reciprocal(E2, E2), inc="p3_v")
    em.op("vector", lambda e: e.tensor_tensor(yy, yy, E2, ALU.mult), inc="p3_v")
    em.op("vector", lambda e, ct=ct: e.tensor_tensor(
        bfa(B3_Y1 + ct * 512, 512), yy, sr, ALU.mult), inc="p3_v")
    em.op("vector", lambda e: e.tensor_copy(pcol, m), inc="p3_v")
    vp(em, sb, pcol, "p3_v", 128, 1)
    em.op("vector", lambda e, ct=ct: e.tensor_copy(sb["pcarry"][:, 4 + ct:5 + ct],
                                                   ab[:, 512:513]), inc="p3_v")
    vp(em, sb, sb["pcarry"][:, 4 + ct:5 + ct], "p3_v", 128, 1)
    em.op("vector", lambda e, ct=ct: e.tensor_copy(sb["pcarry"][:, 8 + ct:9 + ct],
                                                   bb[:, 512:513]), inc="p3_v")
    vp(em, sb, sb["pcarry"][:, 8 + ct:9 + ct], "p3_v", 128, 1)


def colmax_to_scalar(em, dsc, sb, srccol, dstcol, pcol_idx, chV, chDMA):
    em.dma("gpsimd", dsc["pmax"].ap()[pcol_idx:pcol_idx + 1, :], srccol,
           waits=[chV], inc=chDMA)
    em.dma("gpsimd", sb["rowz"][0:1, :],
           dsc["pmax"].ap()[pcol_idx:pcol_idx + 1, :],
           waits=[chDMA], inc=chDMA)
    em.op("vector", lambda e: e.tensor_reduce(
        sb["colz"][0:1, dstcol:dstcol + 1], sb["rowz"][0:1, :], AX.X, ALU.max),
        waits=[chDMA], inc=chV)
    vp(em, sb, sb["colz"][0:1, dstcol:dstcol + 1], chV, 1, 1)


def phase4(em, nc, din, dout, dsc, sb, ps, fa, bfa):
    em.dma("gpsimd", sb["colz"][:, 0:1],
           dsc["ar1o"].ap()[0:1, 0:1].to_broadcast([128, 1]), waits=["cc"],
           inc="rep0")
    em.op("vector", lambda e: e.tensor_scalar(sb["colz"][:, 0:1], sb["colz"][:, 0:1],
                                              1.0 / QM15, 1e-12, ALU.mult, ALU.max),
          waits=["rep0"], inc="q_v")
    vp(em, sb, sb["colz"][:, 0:1], "q_v", 128, 1)
    em.op("vector", lambda e: e.reciprocal(sb["colz"][:, 1:2], sb["colz"][:, 0:1]),
          inc="q_v")
    vp(em, sb, sb["colz"][:, 1:2], "q_v", 128, 1)
    em.op("vector", lambda e: e.memset(sb["mxc"][:, 2:4], 0.0), inc="q_v")
    vp(em, sb, sb["mxc"][:, 2:4], "q_v", 128, 2)

    p3rel = [("p3_v", em.val("p3_v")), ("p3_t", em.val("p3_t")),
             ("p3_s", em.val("p3_s")), ("p3_wr", em.val("p3_wr"))]
    for tc in range(4):
        for d in range(8):
            em.dma("sync", fa(P4_X1 + d * 512, 512),
                   dsc["x1agi"].ap()[d * 128:(d + 1) * 128, tc * 512:(tc + 1) * 512],
                   waits=[("q_v", em.val("q_v")), ("q_s", em.val("q_s"))] + p3rel,
                   inc="q_ld")
        sqt = {}
        for d in range(8):
            xs = fa(P4_X1 + d * 512, 512)
            sq = fa(P4_SG, 512)
            em.op("tensor", lambda e, xs=xs, d=d: e.matmul(
                ps["pa"][0:1, :], sb["onesf"][:, :], xs, start=(d == 0), stop=(d == 7)),
                waits=["q_ld"], inc="q_t")
            em.op("scalar", lambda e, xs=xs, sq=sq: e.activation(sq, xs, AF.Square),
                  waits=["q_ld", ("q_t", sqt.get(d - 1, 0))], inc="q_s")
            em.op("tensor", lambda e, sq=sq, d=d: e.matmul(
                ps["pa"][32:33, :], sb["onesf"][:, :], sq, start=(d == 0), stop=(d == 7)),
                waits=["q_s"], inc="q_t")
            sqt[d] = em.val("q_t")
        ln_rows(em, sb, fa, ps["pa"], P4_RW, "q_s", "q_v", "q_t",
                extraw=[("row4", em.val("row4"))])
        em.dma("gpsimd", dsc["lnrow4"].ap()[0:1, tc * 512:(tc + 1) * 512],
               fa(P4_RW, 512)[0:1, :], waits=["q_v"], inc="row4")
        em.dma("gpsimd", dsc["lnrow4"].ap()[1:2, tc * 512:(tc + 1) * 512],
               fa(P4_RW + 512, 512)[0:1, :], inc="row4")
        em.dma("gpsimd", fa(P4_LR, 512),
               dsc["lnrow4"].ap()[0:1, tc * 512:(tc + 1) * 512].to_broadcast([128, 512]),
               waits=["row4", ("q_v", em.val("q_v"))], inc="rep0")
        em.dma("gpsimd", fa(P4_LR + 512, 512),
               dsc["lnrow4"].ap()[1:2, tc * 512:(tc + 1) * 512].to_broadcast([128, 512]),
               inc="rep0")
        for d in range(8):
            xn = fa(P4_XN + d * 512, 512)
            em.op("vector", lambda e, d=d, xn=xn: e.tensor_tensor(
                xn, fa(P4_X1 + d * 512, 512), fa(P4_LR, 512), ALU.mult),
                waits=["rep0"], inc="q_v")
            em.op("vector", lambda e, xn=xn: e.tensor_tensor(
                xn, xn, fa(P4_LR + 512, 512), ALU.add), inc="q_v")
            vst = fa(P4_CF, 512)
            em.op("vector", lambda e, xn=xn, vst=vst: e.tensor_scalar(
                vst, xn, sb["colz"][:, 1:2], RC, ALU.mult, ALU.add), inc="q_v")
            em.op("vector", lambda e, vst=vst: e.tensor_scalar(
                sb["i0"][:, :], vst, -RC, None, ALU.add), inc="q_v")
            em.op("scalar", lambda e, d=d, tc=tc: e.activation(
                bfa(B4_XQ + tc * 4096 + d * 512, 512), sb["i0"][:, :], AF.Copy,
                scale=sb["colz"][:, 0:1]), waits=["q_v"], inc="q_s")

    for wi, (qnm, bdram) in enumerate((("wqg", "gtd"), ("wqu", "upd"))):
        for ic in range(8):
            em.dma("sync", bfa(B4_WQ + ic * 3072, 3072),
                   dsc[qnm].ap()[0:HID, ic * 128:(ic + 1) * 128],
                   waits=["q2_wr", ("q_t", em.val("q_t"))] + p3rel, inc="q_ld",
                   transpose=True)
        for hid in range(24):
            for tc in range(4):
                pst = ps["pc"] if (hid * 4 + tc) % 2 == 0 else ps["pd"]
                for d in range(8):
                    em.op("tensor", lambda e, hid=hid, tc=tc, d=d, pst=pst: e.matmul(
                        pst[:, :], bfa(B4_WQ + d * 3072 + hid * 128, 128),
                        bfa(B4_XQ + tc * 4096 + d * 512, 512),
                        start=(d == 0), stop=(d == 7)),
                        waits=["q_ld", "q_s", ("q_v", em.val("q_v"))], inc="q_t")
                em.op("vector", lambda e, pst=pst: e.tensor_reduce(
                    sb["mxc"][:, 4:5], pst[:, :], AX.X, ALU.max,
                    apply_absolute_value=True), waits=["q_t"], inc="q_v")
                vp(em, sb, sb["mxc"][:, 4:5], "q_v", 128, 1)
                em.op("vector", lambda e, wi=wi: e.tensor_tensor(
                    sb["mxc"][:, 2 + wi:3 + wi], sb["mxc"][:, 2 + wi:3 + wi],
                    sb["mxc"][:, 4:5], ALU.max), inc="q_v")
                vp(em, sb, sb["mxc"][:, 2 + wi:3 + wi], "q_v", 128, 1)
                em.op("scalar", lambda e, pst=pst: e.activation(
                    bfa(B4_ST, 512), pst[:, :], AF.Copy),
                    waits=["q_v", ("q_wr", em.val("q_wr"))], inc="q_s")
                em.dma("gpsimd", dsc[bdram].ap()[hid, :, tc * 512:(tc + 1) * 512],
                       bfa(B4_ST, 512), waits=["q_s"], inc="q_wr")
    colmax_to_scalar(em, dsc, sb, sb["mxc"][:, 2:3], 3, 1, "q_v", "row4")
    colmax_to_scalar(em, dsc, sb, sb["mxc"][:, 3:4], 4, 2, "q_v", "row4")
    em.dma("gpsimd", dsc["ar2i"].ap()[0:1, 0:2], sb["colz"][0:1, 3:5],
           waits=["q_v"], inc="row4")
    em.coll("AllReduce", ALU.max, RG8, [dsc["ar2i"].ap().opt()],
            [dsc["ar2o"].ap().opt()], waits=["row4"], inc="cc")
    em.dma("gpsimd", sb["colz"][:, 2:3],
           dsc["ar2o"].ap()[0:1, 0:1].to_broadcast([128, 1]), waits=["cc"], inc="rep0")
    em.dma("gpsimd", sb["colz"][:, 4:5],
           dsc["ar2o"].ap()[0:1, 1:2].to_broadcast([128, 1]), inc="rep0")
    for c0 in (2, 4):
        em.op("vector", lambda e, c0=c0: e.tensor_scalar(
            sb["colz"][:, c0:c0 + 1], sb["colz"][:, c0:c0 + 1], 1.0 / QM15, 1e-12,
            ALU.mult, ALU.max), waits=["rep0"], inc="q_v")
        vp(em, sb, sb["colz"][:, c0:c0 + 1], "q_v", 128, 1)
        em.op("vector", lambda e, c0=c0: e.reciprocal(
            sb["colz"][:, c0 + 1:c0 + 2], sb["colz"][:, c0:c0 + 1]), inc="q_v")
        vp(em, sb, sb["colz"][:, c0 + 1:c0 + 2], "q_v", 128, 1)
    em.op("vector", lambda e: e.memset(sb["mxi"][:, :], 0), inc="q_v")
    vp(em, sb, sb["mxi"][:, :], "q_v", 128, 2)
    qwr_all = em.val("q_wr")
    rb_guard = {0: 0, 1: 0}

    def rebuild_P(tc, hid, with_up):
        par = hid % 2
        qch = "q_gt0" if par == 0 else "q_gt1"
        gt = bfa(B4_GU + par * 1024, 512)
        ut = bfa(B4_GU + par * 1024 + 512, 512)
        em.dma("sync", gt, dsc["gtd"].ap()[hid, :, tc * 512:(tc + 1) * 512],
               waits=[("q_v", rb_guard[par]), ("q_wr", qwr_all)], inc=qch)
        if with_up:
            em.dma("sync", ut, dsc["upd"].ap()[hid, :, tc * 512:(tc + 1) * 512],
                   inc=qch)
        vst = fa(P4_CF, 512)
        em.op("vector", lambda e, gt=gt, vst=vst: e.tensor_scalar(
            vst, gt, sb["colz"][:, 3:4], RC, ALU.mult, ALU.add),
            waits=[qch, ("q_s", em.val("q_s"))], inc="q_v")
        em.op("vector", lambda e, vst=vst: e.tensor_scalar(
            sb["i0"][:, :], vst, -RC, None, ALU.add), inc="q_v")
        em.op("scalar", lambda e: e.activation(fa(P4_SG, 512), sb["i0"][:, :],
                                               AF.Sigmoid, scale=sb["colz"][:, 2:3]),
              waits=["q_v"], inc="q_s")
        em.op("vector", lambda e, vst=vst: e.tensor_scalar(
            vst, fa(P4_SG, 512), 65536.0, RC, ALU.mult, ALU.add),
            waits=["q_s"], inc="q_v")
        em.op("vector", lambda e, vst=vst: e.tensor_scalar(
            sb["i1"][:, :], vst, -RC, None, ALU.add), inc="q_v")
        em.op("vector", lambda e: e.tensor_scalar(sb["i1"][:, :], sb["i1"][:, :],
                                                  65535, None, ALU.min), inc="q_v")
        em.op("vector", lambda e: e.tensor_tensor(sb["i2"][:, :], sb["i0"][:, :],
                                                  sb["i1"][:, :], ALU.mult), inc="q_v")
        return ut

    for tc in range(4):
        for hid in range(24):
            rebuild_P(tc, hid, False)
            em.op("vector", lambda e: e.tensor_reduce(
                sb["mxi"][:, 1:2], sb["i2"][:, :], AX.X, ALU.max,
                apply_absolute_value=True), inc="q_v")
            vp(em, sb, sb["mxi"][:, 1:2], "q_v", 128, 1)
            em.op("vector", lambda e: e.tensor_tensor(
                sb["mxi"][:, 0:1], sb["mxi"][:, 0:1], sb["mxi"][:, 1:2], ALU.max),
                inc="q_v")
            vp(em, sb, sb["mxi"][:, 0:1], "q_v", 128, 1)
            rb_guard[hid % 2] = em.val("q_v")
    em.op("vector", lambda e: e.tensor_copy(sb["mxc"][:, 5:6], sb["mxi"][:, 0:1]),
          inc="q_v")
    vp(em, sb, sb["mxc"][:, 5:6], "q_v", 128, 1)
    colmax_to_scalar(em, dsc, sb, sb["mxc"][:, 5:6], 6, 3, "q_v", "row4")
    em.dma("gpsimd", dsc["ar3i"].ap()[0:1, 0:1], sb["colz"][0:1, 6:7],
           waits=["q_v"], inc="row4")
    em.dma("gpsimd", dsc["ar3i"].ap()[0:1, 1:2], sb["colz"][0:1, 6:7], inc="row4")
    em.coll("AllReduce", ALU.max, RG8, [dsc["ar3i"].ap().opt()],
            [dsc["ar3o"].ap().opt()], waits=["row4"], inc="cc")
    em.dma("gpsimd", sb["colz"][:, 6:7],
           dsc["ar3o"].ap()[0:1, 0:1].to_broadcast([128, 1]), waits=["cc"], inc="rep0")
    em.op("vector", lambda e: e.tensor_scalar(sb["colz"][:, 7:8], sb["colz"][:, 2:3],
                                              1.0 / 65536.0, None, ALU.mult),
          waits=["rep0"], inc="q_v")
    vp(em, sb, sb["colz"][:, 7:8], "q_v", 128, 1)
    em.op("vector", lambda e: e.tensor_tensor(sb["colz"][:, 8:9], sb["colz"][:, 6:7],
                                              sb["colz"][:, 7:8], ALU.mult), inc="q_v")
    vp(em, sb, sb["colz"][:, 8:9], "q_v", 128, 1)
    em.op("vector", lambda e: e.tensor_scalar(sb["colz"][:, 8:9], sb["colz"][:, 8:9],
                                              1.0 / QM15, 1e-12, ALU.mult, ALU.max),
          inc="q_v")
    vp(em, sb, sb["colz"][:, 8:9], "q_v", 128, 1)
    em.op("vector", lambda e: e.reciprocal(sb["colz"][:, 12:13], sb["colz"][:, 8:9]),
          inc="q_v")
    vp(em, sb, sb["colz"][:, 12:13], "q_v", 128, 1)
    em.op("vector", lambda e: e.tensor_tensor(sb["colz"][:, 6:7], sb["colz"][:, 7:8],
                                              sb["colz"][:, 12:13], ALU.mult),
          inc="q_v")
    vp(em, sb, sb["colz"][:, 6:7], "q_v", 128, 1)

    em.op("vector", lambda e: e.memset(sb["mxi"][:, 0:1], 0), inc="q_v")
    vp(em, sb, sb["mxi"][:, 0:1], "q_v", 128, 1)
    for tc in range(4):
        for hid in range(24):
            ut = rebuild_P(tc, hid, True)
            vst = fa(P4_CF, 512)
            em.op("vector", lambda e, vst=vst: e.tensor_scalar(
                vst, sb["i2"][:, :], sb["colz"][:, 6:7], RC, ALU.mult, ALU.add),
                inc="q_v")
            em.op("vector", lambda e, vst=vst: e.tensor_scalar(
                sb["i2"][:, :], vst, -RC, None, ALU.add), inc="q_v")
            em.op("vector", lambda e, ut=ut, vst=vst: e.tensor_scalar(
                vst, ut, sb["colz"][:, 5:6], RC, ALU.mult, ALU.add),
                waits=[("q_s", em.val("q_s"))], inc="q_v")
            em.op("vector", lambda e, vst=vst: e.tensor_scalar(
                sb["i0"][:, :], vst, -RC, None, ALU.add), inc="q_v")
            em.op("vector", lambda e: e.tensor_tensor(
                sb["i1"][:, :], sb["i2"][:, :], sb["i0"][:, :], ALU.mult), inc="q_v")
            em.op("vector", lambda e: e.tensor_reduce(
                sb["mxi"][:, 1:2], sb["i1"][:, :], AX.X, ALU.max,
                apply_absolute_value=True), inc="q_v")
            vp(em, sb, sb["mxi"][:, 1:2], "q_v", 128, 1)
            em.op("vector", lambda e: e.tensor_tensor(
                sb["mxi"][:, 0:1], sb["mxi"][:, 0:1], sb["mxi"][:, 1:2], ALU.max),
                inc="q_v")
            vp(em, sb, sb["mxi"][:, 0:1], "q_v", 128, 1)
            rb_guard[hid % 2] = em.val("q_v")
    em.op("vector", lambda e: e.tensor_copy(sb["mxc"][:, 6:7], sb["mxi"][:, 0:1]),
          inc="q_v")
    vp(em, sb, sb["mxc"][:, 6:7], "q_v", 128, 1)
    colmax_to_scalar(em, dsc, sb, sb["mxc"][:, 6:7], 9, 3, "q_v", "row4")
    em.dma("gpsimd", dsc["ar4i"].ap()[0:1, 0:1], sb["colz"][0:1, 9:10],
           waits=["q_v"], inc="row4")
    em.dma("gpsimd", dsc["ar4i"].ap()[0:1, 1:2], sb["colz"][0:1, 9:10], inc="row4")
    em.coll("AllReduce", ALU.max, RG8, [dsc["ar4i"].ap().opt()],
            [dsc["ar4o"].ap().opt()], waits=["row4"], inc="cc")
    em.dma("gpsimd", sb["colz"][:, 9:10],
           dsc["ar4o"].ap()[0:1, 0:1].to_broadcast([128, 1]), waits=["cc"], inc="rep0")
    em.op("vector", lambda e: e.tensor_tensor(sb["colz"][:, 10:11], sb["colz"][:, 8:9],
                                              sb["colz"][:, 4:5], ALU.mult),
          waits=["rep0"], inc="q_v")
    vp(em, sb, sb["colz"][:, 10:11], "q_v", 128, 1)
    em.op("vector", lambda e: e.tensor_tensor(sb["colz"][:, 11:12], sb["colz"][:, 9:10],
                                              sb["colz"][:, 10:11], ALU.mult), inc="q_v")
    vp(em, sb, sb["colz"][:, 11:12], "q_v", 128, 1)
    em.op("vector", lambda e: e.tensor_scalar(sb["colz"][:, 11:12], sb["colz"][:, 11:12],
                                              1.0 / QM15, 1e-12, ALU.mult, ALU.max),
          inc="q_v")
    vp(em, sb, sb["colz"][:, 11:12], "q_v", 128, 1)
    em.op("vector", lambda e: e.reciprocal(sb["colz"][:, 12:13], sb["colz"][:, 11:12]),
          inc="q_v")
    vp(em, sb, sb["colz"][:, 12:13], "q_v", 128, 1)
    em.op("vector", lambda e: e.tensor_tensor(sb["colz"][:, 9:10], sb["colz"][:, 10:11],
                                              sb["colz"][:, 12:13], ALU.mult),
          inc="q_v")
    vp(em, sb, sb["colz"][:, 9:10], "q_v", 128, 1)

    for hc in range(24):
        em.dma("sync", bfa(B4_WQ + hc * 1024, 1024),
               dsc["wqd"].ap()[0:C, hc * 128:(hc + 1) * 128],
               waits=["q2_wr", ("q_t", em.val("q_t"))], inc="q_ld", transpose=True)

    def s1body(e, core, then):
        g = core // 4
        b = core % 4
        for rep, rk in ((0, b), (1, b + 4)):
            ins = e.dma_start(
                out=dsc["s1loc"].ap()[rep * C:(rep + 1) * C, :],
                in_=dsc["s1ago"].ap()[rk * C:(rk + 1) * C,
                                      g * THALF:(g + 1) * THALF])
            then(ins)

    em.switch("gpsimd", s1body, waits=["cc"], inc="loc_wr", nincs=2)
    s1loc_done = em.val("loc_wr")

    for tc in range(4):
        for hid in range(24):
            ut = rebuild_P(tc, hid, True)
            vst = fa(P4_CF, 512)
            em.op("vector", lambda e, vst=vst: e.tensor_scalar(
                vst, sb["i2"][:, :], sb["colz"][:, 6:7], RC, ALU.mult, ALU.add),
                inc="q_v")
            em.op("vector", lambda e, vst=vst: e.tensor_scalar(
                sb["i2"][:, :], vst, -RC, None, ALU.add), inc="q_v")
            em.op("vector", lambda e, ut=ut, vst=vst: e.tensor_scalar(
                vst, ut, sb["colz"][:, 5:6], RC, ALU.mult, ALU.add),
                waits=[("q_s", em.val("q_s"))], inc="q_v")
            em.op("vector", lambda e, vst=vst: e.tensor_scalar(
                sb["i0"][:, :], vst, -RC, None, ALU.add), inc="q_v")
            em.op("vector", lambda e: e.tensor_tensor(
                sb["i1"][:, :], sb["i2"][:, :], sb["i0"][:, :], ALU.mult), inc="q_v")
            em.op("vector", lambda e, vst=vst: e.tensor_scalar(
                vst, sb["i1"][:, :], sb["colz"][:, 9:10], RC, ALU.mult, ALU.add),
                inc="q_v")
            em.op("vector", lambda e, vst=vst: e.tensor_scalar(
                sb["i1"][:, :], vst, -RC, None, ALU.add), inc="q_v")
            em.op("scalar", lambda e, hid=hid: e.activation(
                bfa(B4_OQ + hid * 512, 512), sb["i1"][:, :], AF.Copy,
                scale=sb["colz"][:, 11:12]),
                waits=["q_v", ("q_t", em.val("q_t"))], inc="q_s")
            rb_guard[hid % 2] = em.val("q_v")
        for d in range(8):
            pst = ps["pc"] if d % 2 == 0 else ps["pd"]
            for hid in range(24):
                em.op("tensor", lambda e, d=d, hid=hid, pst=pst: e.matmul(
                    pst[:, :], bfa(B4_WQ + hid * 1024 + d * 128, 128),
                    bfa(B4_OQ + hid * 512, 512), start=(hid == 0), stop=(hid == 23)),
                    waits=["q_s", "q_ld", ("q_v", em.val("q_v"))], inc="q_t")
            em.op("scalar", lambda e, d=d, pst=pst: e.activation(
                fa(P4_G1 + d * 512, 512), pst[:, :], AF.Silu), waits=["q_t"],
                inc="q_s")
        for rep in range(2):
            base = 36864 if rep == 0 else 43008
            for d in range(8):
                em.dma("sync", bfa(base + d * 512, 512),
                       dsc["s1loc"].ap()[rep * C + d * 128:rep * C + (d + 1) * 128,
                                         tc * 512:(tc + 1) * 512],
                       waits=[("loc_wr", s1loc_done), ("q_v", em.val("q_v"))],
                       inc="q_ld")
        for d in range(8):
            em.dma("sync", fa(P4_X1 + d * 512, 512),
                   dsc["x1agi"].ap()[d * 128:(d + 1) * 128, tc * 512:(tc + 1) * 512],
                   waits=[("q_v", em.val("q_v"))], inc="q_ld")
        for d in range(8):
            em.op("vector", lambda e, d=d: e.tensor_tensor(
                fa(P4_SC + d * 512, 512), bfa(36864 + d * 512, 512),
                bfa(43008 + d * 512, 512), ALU.add),
                waits=["q_ld"], inc="q_v")
            em.op("vector", lambda e, d=d: e.tensor_tensor(
                fa(P4_SC + d * 512, 512), fa(P4_SC + d * 512, 512),
                fa(P4_G1 + d * 512, 512), ALU.mult), waits=["q_s"], inc="q_v")
            em.op("vector", lambda e, d=d: e.tensor_tensor(
                fa(P4_XN + d * 512, 512), fa(P4_SC + d * 512, 512),
                fa(P4_X1 + d * 512, 512), ALU.add), inc="q_v")
            em.dma("gpsimd", dout.ap()[d * 128:(d + 1) * 128, tc * 512:(tc + 1) * 512],
                   fa(P4_XN + d * 512, 512), waits=["q_v"], inc="out_wr")
    em.op("gpsimd", lambda e: e.nop(), waits=["out_wr"], inc=None)


def kernel(**inputs):
    in_maps = make_in_maps(inputs)
    nc = build_program()
    res = run_bass_kernel_spmd(nc, in_maps, core_ids=list(range(8)))
    out = np.zeros((B, L, C), dtype=np.float32)
    for c in range(8):
        b, g = c % 4, c // 4
        out[b, g * THALF:(g + 1) * THALF, :] = res.results[c]["out"].T
    return out


# version 06 (final)
# speedup vs baseline: 1.0888x; 1.0122x over previous
import numpy as np
import ml_dtypes
import concourse.bass as bass
import concourse.mybir as mybir
from concourse.bass_utils import run_bass_kernel_spmd

F32 = mybir.dt.float32
BF16 = mybir.dt.bfloat16
I32 = mybir.dt.int32
AF = mybir.ActivationFunctionType
ALU = mybir.AluOpType
AX = mybir.AxisListType

B, L, C = 4, 4096, 1024
DI, NST, KC, RNK = 256, 16, 4, 8
NHPC = 4
HID = 3072
TC1 = 1024
NCH = L // TC1
THALF = L // 2
EPS = 1e-5
QM15 = 32767.0
RC = 12582912.0

RG8 = [[0, 1, 2, 3, 4, 5, 6, 7]]
DEBUG = False
CUT = None
P1HCK = 16
P1U = 96
P1SCAN = True
P1PART = 9
P1VAR = 0
P1SKIP = False
P0PART = 3
P0MASK = 15

FA_COLS = 21016
BF_COLS = 47616

P0_XC = 0
P0_SQ = 8192
P0_RW = 9216
P0_LR = 10240
P0_WST = 12352
P0_HH = 15616

P1_U = 0
P1_UC = 2054
P1_SZ = 4102
P1_DL = 6150
P1_DB = 8198
P1_T0 = 10246
P1_T1 = 10758
P1_SS = 11270
P1_DA = 12294
P1_EP = 14342
P1_F0 = 15366

P2_X = 0
P2_S = 4096
P2_X1 = 8192
P2_G = 12288
P2_LW = 12800
P2_LS = 15872

P3_X1 = 0
P3_XN = 4104
P3_K = 8208
P3_V = 10256
P3_SR = 12304
P3_AB = 14352
P3_BB = 16404
P3_Y = 18456
P3_LR = 18968
P3_RW = 19992

P4_X1 = 0
P4_XN = 4096
P4_LR = 8192
P4_RW = 9216
P4_SG = 10240
P4_S1 = 10752
P4_G1 = 11776
P4_SC = 15872
P4_CF = 19968

B1_XNH = 0
B1_XNT = 16384
B1_DU = 32768
B1_X = 34816
B1_H = 35840
B1_PR = 36864
B1_BR = 40960
B1_CR = 43008
B1_YG = 45056
B1_HF = 47104
B2_WD = 0
B2_HF = 24576
B3_XK = 0
B3_XV = 4096
B3_XR = 8192
B3_WK = 12288
B3_WV = 16384
B3_WR = 20480
B3_WO = 24576
B3_Y1 = 28672
B3_ST = 30720
B4_WQ = 0
B4_XQ = 24576
B4_OQ = 24576
B4_GU = 40960
B4_ST = 43008


def _f(x):
    return np.ascontiguousarray(np.asarray(x), dtype=np.float32)


def make_in_maps(inp):
    eye = np.ascontiguousarray(np.eye(128, dtype=ml_dtypes.bfloat16))
    ones = np.ones((128, 1), np.float32)
    maps = []
    for c in range(8):
        b, g = c % 4, c // 4
        mh = slice(4 * g, 4 * g + 4)
        ch = slice(512 * g, 512 * g + 512)
        m = {"identb": eye, "onesf": ones}
        m["xbT"] = _f(np.asarray(inp["L0_attn_x"])[b].T)
        w = np.asarray(inp["L0_attn_in_proj_w"])[mh]
        m["ipwT"] = _f(np.transpose(w, (2, 0, 1)).reshape(128, 2048))
        cw = np.asarray(inp["L0_attn_conv_w"])[mh].reshape(4, 2, 128, KC)
        m["convw"] = _f(np.transpose(cw, (2, 0, 1, 3)).reshape(128, 32))
        cb = np.asarray(inp["L0_attn_conv_b"])[mh].reshape(4, 2, 128)
        m["convb"] = _f(np.transpose(cb, (2, 0, 1)).reshape(128, 8))
        xp = np.asarray(inp["L0_attn_x_proj_w"])[mh]
        xpT = np.transpose(xp, (2, 0, 1)).reshape(2, 128, 4, 40)
        m["xpwT"] = _f(np.transpose(xpT, (1, 2, 0, 3)).reshape(128, 320))
        dw = np.asarray(inp["L0_attn_dt_proj_w"])[mh].reshape(4, 2, 128, RNK)
        m["dtwT"] = _f(np.transpose(dw, (3, 0, 1, 2)).reshape(RNK, 1024))
        db = np.asarray(inp["L0_attn_dt_proj_b"])[mh].reshape(4, 2, 128)
        m["dtb"] = _f(np.transpose(db, (2, 0, 1)).reshape(128, 8))
        al = np.asarray(inp["L0_attn_A_log"])[mh].reshape(4, 2, 128, NST)
        m["alog"] = _f(np.transpose(al, (2, 0, 1, 3)).reshape(128, 128))
        dd = np.asarray(inp["L0_attn_D"])[mh].reshape(4, 2, 128)
        m["ddd"] = _f(np.transpose(dd, (2, 0, 1)).reshape(128, 8))
        ow = np.asarray(inp["L0_attn_out_proj_w"])[mh]
        owT = np.transpose(ow, (2, 0, 1)).reshape(2, 128, 4, 128)
        m["opwT"] = _f(np.transpose(owT, (1, 2, 0, 3)).reshape(128, 1024))
        wg = np.asarray(inp["L0_ffn_w_gate"]).reshape(24, 128, 8, 128)
        m["wg0"] = np.ascontiguousarray(
            np.transpose(wg, (0, 3, 2, 1)), dtype=np.int32).reshape(24, 128, 1024)
        wu = np.asarray(inp["L0_ffn_w_up"]).reshape(24, 128, 8, 128)
        m["wu0"] = np.ascontiguousarray(
            np.transpose(wu, (0, 3, 2, 1)), dtype=np.int32).reshape(24, 128, 1024)
        wd = np.asarray(inp["L0_ffn_w_down"]).reshape(8, 128, 24, 128)
        m["wd0"] = np.ascontiguousarray(
            np.transpose(wd, (2, 3, 0, 1)), dtype=np.int32).reshape(24, 128, 1024)
        m["wkT"] = _f(np.asarray(inp["L1_attn_Wk"]).T[:, ch])
        m["wvT"] = _f(np.asarray(inp["L1_attn_Wv"]).T[:, ch])
        m["wrT"] = _f(np.asarray(inp["L1_attn_Wr"]).T[:, ch])
        m["woT"] = _f(np.asarray(inp["L1_attn_Wo"]).T[ch, :])
        for nm, key in [("tdec", "L1_attn_time_decay"), ("tfir", "L1_attn_time_first"),
                        ("tmk", "L1_attn_time_mix_k"), ("tmv", "L1_attn_time_mix_v"),
                        ("tmr", "L1_attn_time_mix_r")]:
            m[nm] = _f(np.asarray(inp[key]).reshape(8, 128).T)
        m["wg1"] = _f(inp["L1_ffn_w_gate"])
        m["wu1"] = _f(inp["L1_ffn_w_up"])
        m["wd1"] = _f(inp["L1_ffn_w_down"])
        maps.append(m)
    return maps



class Ch:
    __slots__ = ("name", "n")

    def __init__(self, name):
        self.name = name
        self.n = 0


class Em:
    def __init__(self, nc, ename, eng, chans, sems):
        self.nc = nc
        self.ename = ename
        self.eng = eng
        self.chans = chans
        self.sems = sems
        self.lastw = {}
        self._pid = None

    def pid(self):
        if self._pid is None:
            self._pid = self.eng.partition_id()
        return self._pid

    def _wait(self, waits):
        for w in waits:
            if isinstance(w, tuple):
                name, val = w
            else:
                name, val = w, self.chans[w].n
            if val > 0 and self.lastw.get(name, -1) < val:
                self.eng.wait_ge(self.sems[name], val)
                self.lastw[name] = val

    def val(self, chname):
        return self.chans[chname].n

    def op(self, ename, fn, waits=(), inc=None, k=1):
        if ename == self.ename:
            self._wait(waits)
            ins = fn(self.eng)
            if inc is not None and ins is not None:
                ins.then_inc(self.sems[inc], k)
        if inc is not None:
            self.chans[inc].n += k

    def dma(self, ename, out, in_, waits=(), inc=None, transpose=False):
        if transpose:
            self.op(ename, lambda e: e.dma_start(out=out, in_=in_, transpose=True),
                    waits, inc, 16)
        else:
            self.op(ename, lambda e: e.dma_start(out=out, in_=in_), waits, inc, 16)

    def switch(self, ename, body, waits=(), inc=None, nincs=1, kper=16):
        if ename == self.ename:
            self._wait(waits)
            idx = self.pid()
            sem = self.sems[inc] if inc is not None else None
            for i in self.nc.Switch(engines=[self.eng], index=[idx], n=8):
                def then(instr, sem=sem):
                    if sem is not None:
                        instr.then_inc(sem, kper)
                body(self.eng, i, then)
        if inc is not None:
            self.chans[inc].n += kper * nincs

    def coll(self, kind, op, groups, ins, outs, waits=(), inc=None):
        self.op("gpsimd",
                lambda e: e.collective_compute(kind, op, replica_groups=groups,
                                               ins=ins, outs=outs),
                waits, inc, 1)


CHANNELS = [
    "w_ld", "x_ld", "x_ldb", "loc_wr", "pm_s", "pm_v",
    "st_t", "st_v", "st_s", "row_wr", "rep0", "rep0b",
    "pre_t", "pre_s", "pre_v", "brc_wr",
    "sc_s", "sc_v", "sc_t", "rep_ld", "rep_ldb",
    "epi_v", "epi_t", "epi_s", "s_wr",
    "f0_ld", "f0_t", "f0_s", "f0_v", "hf_wr",
    "cc", "g_t", "g_s", "g_v", "hf_ld",
    "c_ld", "c_v", "c_wr", "q2_v", "q2_s", "q2_ld", "q2_wr",
    "p3_ld", "p3_t", "p3_v", "p3_s", "p3_wr", "row3",
    "q_ld", "q_gt0", "q_gt1", "q_t", "q_v", "q_s", "q_wr", "row4",
    "out_wr", "dbg_wr",
]


def build_program():
    nc = bass.Bass("TRN2", target_bir_lowering=False, detect_race_conditions=False)

    din = {}
    for name, shape, dt in [
        ("xbT", [C, L], F32), ("ipwT", [128, 2048], F32), ("convw", [128, 32], F32),
        ("convb", [128, 8], F32), ("xpwT", [128, 320], F32), ("dtwT", [RNK, 1024], F32),
        ("dtb", [128, 8], F32), ("alog", [128, 128], F32), ("ddd", [128, 8], F32),
        ("opwT", [128, 1024], F32),
        ("wg0", [24, 128, 1024], I32), ("wu0", [24, 128, 1024], I32),
        ("wd0", [24, 128, 1024], I32),
        ("wkT", [C, 512], F32), ("wvT", [C, 512], F32), ("wrT", [C, 512], F32),
        ("woT", [512, C], F32),
        ("tdec", [128, 8], F32), ("tfir", [128, 8], F32), ("tmk", [128, 8], F32),
        ("tmv", [128, 8], F32), ("tmr", [128, 8], F32),
        ("wg1", [HID, C], F32), ("wu1", [HID, C], F32), ("wd1", [C, HID], F32),
        ("identb", [128, 128], BF16), ("onesf", [128, 1], F32),
    ]:
        din[name] = nc.dram_tensor(name, shape, dt, kind="ExternalInput")
    dout = nc.dram_tensor("out", [C, THALF], F32, kind="ExternalOutput")

    dsc = {}
    for name, shape, dt in [
        ("brc", [NHPC, 2 * NST, L], BF16),
        ("lnrow", [2, L], F32), ("lnrow3", [2, L], F32), ("lnrow4", [2, THALF], F32),
        ("hfd", [24, 128, THALF], BF16),
        ("gd", [8, 128, THALF], F32),
        ("wqg", [HID, C], BF16), ("wqu", [HID, C], BF16), ("wqd", [C, HID], BF16),
        ("upd", [24, 128, THALF], BF16), ("gtd", [24, 128, THALF], BF16),
        ("pmax", [4, 128], F32),
        ("sagi", [512, L], F32),
        ("x1agi", [C, THALF], F32),
        ("s1agi", [C, L], BF16),
        ("ar1i", [1, 2], F32), ("ar1o", [1, 2], F32),
        ("ar2i", [1, 2], F32), ("ar2o", [1, 2], F32),
        ("ar3i", [1, 2], F32), ("ar3o", [1, 2], F32),
        ("ar4i", [1, 2], F32), ("ar4o", [1, 2], F32),
        ("dbg", [8, 128, 2048], F32) if DEBUG else ("dbg", [1, 1, 1], F32),
        ("dbgb", [8, 128, 2048], BF16) if DEBUG else ("dbgb", [1, 1, 1], BF16),
        ("xhh", [512, L], F32), ("xth", [C, THALF], F32),
        ("lnrowth", [2, THALF], F32),
        ("sloc", [C, THALF], F32), ("x1loc", [C, L], F32),
        ("s1loc", [2 * C, THALF], BF16),
    ]:
        dbg_fetch = {"dbg", "dbgb", "lnrow", "brc", "hfd", "gd",
                     "wqg", "upd", "lnrow3"}
        kind = "ExternalOutput" if (DEBUG and name in dbg_fetch) else "Internal"
        dsc[name] = nc.dram_tensor(name, shape, dt, kind=kind)
    for name, shape, dt in [
        ("sago", [8 * 512, L], F32), ("x1ago", [8 * C, THALF], F32),
        ("s1ago", [8 * C, L], BF16),
    ]:
        dsc[name] = nc.dram_tensor(name, shape, dt, addr_space="Shared")
    if DEBUG:
        for name, src in [("dsagi", [512, L]), ("dx1", [C, THALF]),
                          ("dar", [1, 8])]:
            dsc[name] = nc.dram_tensor(name, src, F32, kind="ExternalOutput")
        dsc["ds1"] = nc.dram_tensor("ds1", [C, L], BF16, kind="ExternalOutput")

    small_specs = [
        ("ipw", [128, 2048], BF16), ("opw", [128, 1024], BF16),
        ("xpw", [128, 320], F32), ("dtw", [RNK, 1024], F32),
        ("convw", [128, 32], F32), ("convb", [128, 8], F32),
        ("dtb", [128, 8], F32), ("acoef", [128, 128], F32), ("ddd", [128, 8], F32),
        ("tdec", [128, 8], F32), ("tfir", [128, 8], F32), ("tmk", [128, 8], F32),
        ("tmv", [128, 8], F32), ("tmr", [128, 8], F32),
        ("wdec", [128, 8], F32), ("ewc", [128, 8], F32),
        ("identb", [128, 128], BF16), ("onesf", [128, 1], F32),
        ("carry", [128, 32], F32),
        ("pcarry", [128, 12], F32),
        ("wkvp", [128, 8], F32),
        ("xhalo", [128, 16], F32),
        ("w0slab", [128, 1024], I32), ("w0bf", [128, 2048], BF16),
        ("i0", [128, 512], I32), ("i1", [128, 512], I32), ("i2", [128, 512], I32),
        ("mxi", [128, 2], I32),
        ("colz", [128, 16], F32),
        ("mxc", [128, 8], F32),
        ("rowz", [1, 128], F32),
        ("bcst", [128, 1024], BF16),
        ("cvals", [128, 4], F32),
        ("jnk", [128, 96], F32),
    ]

    with nc.cleanup_on_exit():
        ctxs = []
        sb = {}
        for name, shape, dt in small_specs:
            t = nc.sbuf_tensor("sb_" + name, shape, dt)
            sb[name] = t.__enter__()
            ctxs.append(t)
        t = nc.sbuf_tensor("FA", [128, FA_COLS], F32)
        sb["FA"] = t.__enter__()
        ctxs.append(t)
        t = nc.sbuf_tensor("BFA", [128, BF_COLS], BF16)
        sb["BFA"] = t.__enter__()
        ctxs.append(t)

        nc.const_aps.register(F32, EPS, sb["cvals"][:, 0:1])
        nc.const_aps.register(F32, -0.5, sb["cvals"][:, 1:2])
        nc.const_aps.register(F32, 1.0 / 127.0, sb["cvals"][:, 2:3])

        ps = {}
        for name, shape in [("pa", [128, 512]), ("pb", [128, 512]),
                            ("pc", [128, 512]), ("pd", [128, 512]),
                            ("py0", [128, 1024]), ("py1", [128, 1024])]:
            t = nc.psum_tensor(name, shape, F32)
            ps[name] = t.__enter__()
            ctxs.append(t)

        sems = {}
        for name in CHANNELS:
            t = nc.semaphore(name)
            sems[name] = t.__enter__()
            ctxs.append(t)

        blk = nc.Block()
        blk.__enter__()
        chans = {}

        def mk(ename, eng):
            for cn in CHANNELS:
                chans[cn] = Ch(cn)
            em = Em(nc, ename, eng, chans, sems)
            program(em, nc, din, dout, dsc, sb, ps)

        @blk.sync
        def _(e):
            mk("sync", e)

        @blk.tensor
        def _(e):
            mk("tensor", e)

        @blk.scalar
        def _(e):
            mk("scalar", e)

        @blk.vector
        def _(e):
            mk("vector", e)

        @blk.gpsimd
        def _(e):
            mk("gpsimd", e)

        blk.__exit__(None, None, None)
        for t in reversed(ctxs):
            t.__exit__(None, None, None)
    return nc



def program(em, nc, din, dout, dsc, sb, ps):
    FA = sb["FA"]
    BFA = sb["BFA"]

    def fa(c0, n):
        return FA[:, c0:c0 + n]

    def bfa(c0, n):
        return BFA[:, c0:c0 + n]

    ctx = dict(em=em, nc=nc, din=din, dout=dout, dsc=dsc, sb=sb, ps=ps, fa=fa, bfa=bfa)
    phases = [phase0, phase1, phase2, phase3, phase4]
    upto = len(phases) if CUT is None else CUT + 1
    for ph in phases[:upto]:
        ph(**ctx)
    if DEBUG and upto == len(phases):
        em.dma("gpsimd", dsc["dsagi"].ap(), dsc["sagi"].ap(),
               waits=["s_wr"], inc="dbg_wr")
        em.dma("gpsimd", dsc["dx1"].ap(), dsc["x1agi"].ap(),
               waits=["c_wr"], inc="dbg_wr")
        em.dma("gpsimd", dsc["ds1"].ap(), dsc["s1agi"].ap(),
               waits=["p3_wr"], inc="dbg_wr")
        em.dma("gpsimd", dsc["dar"].ap()[0:1, 0:2], dsc["ar1o"].ap(),
               waits=["cc"], inc="dbg_wr")
        em.dma("gpsimd", dsc["dar"].ap()[0:1, 2:4], dsc["ar2o"].ap(),
               inc="dbg_wr")
        em.op("gpsimd", lambda e: e.nop(), waits=["dbg_wr"], inc=None)


def vp(em, sb, ap, ch, p, c):
    em.op("vector", lambda e, ap=ap, p=p, c=c: e.tensor_copy(
        sb["jnk"][0:p, 0:c], ap), inc=ch)


def ln_rows(em, sb, fa, psrow, rw0, chS, chV, chT, extraw=()):
    rrow = fa(rw0, 512)[0:1, :]
    mrow = fa(rw0 + 512, 512)[0:1, :]
    em.op("vector", lambda e: e.tensor_scalar(mrow, psrow[0:1, :], 1.0 / C, None,
                                              ALU.mult),
          waits=[chT] + list(extraw), inc=chV)
    em.op("vector", lambda e: e.scalar_tensor_tensor(rrow, mrow, -1.0, mrow,
                                                     ALU.mult, ALU.mult), inc=chV)
    em.op("vector", lambda e: e.scalar_tensor_tensor(rrow, psrow[32:33, :], 1.0 / C,
                                                     rrow, ALU.mult, ALU.add),
          inc=chV)
    em.op("scalar", lambda e: e.activation(rrow, rrow, AF.Ln, bias=EPS),
          waits=[chV], inc=chS)
    em.op("scalar", lambda e: e.activation(rrow, rrow, AF.Exp, scale=-0.5), inc=chS)
    em.op("vector", lambda e: e.scalar_tensor_tensor(mrow, mrow, -1.0, rrow,
                                                     ALU.mult, ALU.mult),
          waits=[chS], inc=chV)


def phase0(em, nc, din, dout, dsc, sb, ps, fa, bfa):
    em.op("vector", lambda e: e.memset(sb["cvals"][:, 0:1], EPS), inc="pm_v")
    em.op("vector", lambda e: e.memset(sb["cvals"][:, 1:2], -0.5), inc="pm_v")
    em.op("vector", lambda e: e.memset(sb["cvals"][:, 2:3], 1.0 / 127.0), inc="pm_v")
    if P0MASK & 1:
        for nm in ["convw", "convb", "dtb", "ddd", "tdec", "tfir", "tmk", "tmv",
                   "tmr", "identb", "onesf"]:
            em.dma("sync", sb[nm][:, :], din[nm].ap(), inc="w_ld")
        em.dma("sync", sb["xpw"][:, :], din["xpwT"].ap(), inc="w_ld")
        em.dma("sync", sb["dtw"][:, :], din["dtwT"].ap(), inc="w_ld")
        em.dma("sync", fa(P0_WST, 2048), din["ipwT"].ap(), inc="w_ld")
        em.dma("sync", fa(P0_WST + 2048, 1024), din["opwT"].ap(), inc="w_ld")
        em.dma("sync", fa(P0_WST + 3072, 128), din["alog"].ap(), inc="w_ld")
    if P0MASK & 2:
        em.op("scalar", lambda e: e.activation(sb["ipw"][:, :], fa(P0_WST, 2048),
                                               AF.Copy),
              waits=["w_ld"], inc="pm_s")
        em.op("scalar", lambda e: e.activation(sb["opw"][:, :], fa(P0_WST + 2048, 1024),
                                               AF.Copy), inc="pm_s")
    if P0MASK & 4:
        em.op("scalar", lambda e: e.activation(sb["acoef"][:, :], fa(P0_WST + 3072, 128),
                                               AF.Exp),
              waits=["w_ld"], inc="pm_s")
        em.op("vector", lambda e: e.tensor_scalar(sb["acoef"][:, :], sb["acoef"][:, :],
                                                  -1.0, None, ALU.mult),
              waits=["pm_s"], inc="pm_v")
        em.op("scalar", lambda e: e.activation(sb["wdec"][:, :], sb["tdec"][:, :],
                                               AF.Exp),
              inc="pm_s")
        em.op("vector", lambda e: e.tensor_scalar(sb["wdec"][:, :], sb["wdec"][:, :],
                                                  -1.0, None, ALU.mult),
              waits=["pm_s"], inc="pm_v")
        em.op("scalar", lambda e: e.activation(sb["ewc"][:, :], sb["wdec"][:, :],
                                               AF.Exp),
              waits=["pm_v"], inc="pm_s")
    if P0MASK & 8:
        em.op("vector", lambda e: e.memset(sb["carry"][:, :], 0.0), inc="pm_v")
        em.op("vector", lambda e: e.memset(sb["mxc"][:, :], 0.0), inc="pm_v")
        em.op("vector", lambda e: e.memset(sb["colz"][:, :], 0.0), inc="pm_v")
        em.op("vector", lambda e: e.memset(sb["xhalo"][:, :], 0.0), inc="pm_v")
    if P0PART < 1:
        return

    def pick(e, core, then):
        g = core // 4
        for ct in range(4):
            e.tensor_copy(sb["wkvp"][:, ct:ct + 1], sb["tfir"][:, 4 * g + ct:4 * g + ct + 1])
        for ct in range(4):
            ins = e.tensor_copy(sb["wkvp"][:, 4 + ct:5 + ct],
                                sb["ewc"][:, 4 * g + ct:4 * g + ct + 1])
        then(ins)

    em.switch("vector", pick, waits=["pm_s"], inc="pm_v", nincs=1, kper=1)
    vp(em, sb, sb["wkvp"][:, 0:8], "pm_v", 128, 8)

    def locbody(e, core, then):
        g = core // 4
        then(e.dma_start(out=dsc["xhh"].ap(),
                         in_=din["xbT"].ap()[g * 512:(g + 1) * 512, :]))
        then(e.dma_start(out=dsc["xth"].ap(),
                         in_=din["xbT"].ap()[:, g * THALF:(g + 1) * THALF]))

    em.switch("gpsimd", locbody, waits=(), inc="loc_wr", nincs=2)

    if P0PART < 2:
        return
    hist_v = {}
    for tc in range(8):
        xch = "x_ld" if tc % 2 == 0 else "x_ldb"
        xoff = P0_XC + (tc % 2) * 4096
        for i in range(8):
            dst = fa(xoff + i * 512, 512)
            em.dma("sync", dst,
                   din["xbT"].ap()[i * 128:(i + 1) * 128, tc * 512:(tc + 1) * 512],
                   waits=[("st_v", hist_v.get(tc - 2, 0))], inc=xch)
        hh0 = P0_HH + (tc % 2) * 2048
        for j in range(4):
            em.dma("sync", fa(hh0 + j * 512, 512),
                   dsc["xhh"].ap()[j * 128:(j + 1) * 128, tc * 512:(tc + 1) * 512],
                   waits=[("loc_wr", 32), ("st_v", hist_v.get(tc - 2, 0))], inc=xch)
        sqt = {}
        for i in range(8):
            dst = fa(xoff + i * 512, 512)
            sq = fa(P0_SQ + (i % 2) * 512, 512)
            em.op("scalar", lambda e, dst=dst, sq=sq: e.activation(sq, dst, AF.Square),
                  waits=[xch, ("st_t", sqt.get(i - 2, 0))], inc="st_s")
            em.op("tensor", lambda e, dst=dst, i=i: e.matmul(
                ps["pa"][0:1, :], sb["onesf"][:, :], dst, start=(i == 0), stop=(i == 7)),
                waits=[xch, "w_ld", ("st_v", em.val("st_v"))], inc="st_t")
            em.op("tensor", lambda e, sq=sq, i=i: e.matmul(
                ps["pa"][32:33, :], sb["onesf"][:, :], sq, start=(i == 0), stop=(i == 7)),
                waits=["st_s"], inc="st_t")
            sqt[i] = em.val("st_t")
        ln_rows(em, sb, fa, ps["pa"], P0_RW, "st_s", "st_v", "st_t",
                extraw=[("row_wr", em.val("row_wr"))])
        em.dma("gpsimd", dsc["lnrow"].ap()[0:1, tc * 512:(tc + 1) * 512],
               fa(P0_RW, 512)[0:1, :], waits=["st_v"], inc="row_wr")
        em.dma("gpsimd", dsc["lnrow"].ap()[1:2, tc * 512:(tc + 1) * 512],
               fa(P0_RW + 512, 512)[0:1, :], inc="row_wr")
        lr = P0_LR + (tc % 2) * 1024
        rch = "rep0" if tc % 2 == 0 else "rep0b"
        em.dma("gpsimd", fa(lr, 512),
               dsc["lnrow"].ap()[0:1, tc * 512:(tc + 1) * 512].to_broadcast([128, 512]),
               waits=["row_wr", ("st_v", hist_v.get(tc - 2, 0))], inc=rch)
        em.dma("gpsimd", fa(lr + 512, 512),
               dsc["lnrow"].ap()[1:2, tc * 512:(tc + 1) * 512].to_broadcast([128, 512]),
               inc=rch)
        for j in range(4):
            src = fa(hh0 + j * 512, 512)
            dst = bfa(B1_XNH + j * 4096 + tc * 512, 512)
            em.op("vector", lambda e, src=src, dst=dst, lr=lr: e.tensor_tensor(
                dst, src, fa(lr, 512), ALU.mult),
                waits=[rch, xch], inc="st_v")
            em.op("vector", lambda e, dst=dst, lr=lr: e.tensor_tensor(
                dst, dst, fa(lr + 512, 512), ALU.add), inc="st_v")
        hist_v[tc] = em.val("st_v")

    if P0PART < 3:
        return
    def rowbody(e, core, then):
        g = core // 4
        then(e.dma_start(out=dsc["lnrowth"].ap(),
                         in_=dsc["lnrow"].ap()[:, g * THALF:(g + 1) * THALF]))

    em.switch("gpsimd", rowbody, waits=["row_wr"], inc="row_wr", nincs=1)
    for tc in range(4):
        xch = "x_ld" if tc % 2 == 0 else "x_ldb"
        rch = "rep0" if tc % 2 == 0 else "rep0b"
        xoff = P0_XC + (tc % 2) * 4096
        for i in range(8):
            em.dma("sync", fa(xoff + i * 512, 512),
                   dsc["xth"].ap()[i * 128:(i + 1) * 128, tc * 512:(tc + 1) * 512],
                   waits=[("loc_wr", 32), ("st_v", hist_v.get(8 + tc - 2, hist_v[7]))],
                   inc=xch)
        lr = P0_LR + (tc % 2) * 1024
        em.dma("gpsimd", fa(lr, 512),
               dsc["lnrowth"].ap()[0:1, tc * 512:(tc + 1) * 512].to_broadcast([128, 512]),
               waits=["row_wr", ("st_v", hist_v.get(8 + tc - 2, hist_v[7]))],
               inc=rch)
        em.dma("gpsimd", fa(lr + 512, 512),
               dsc["lnrowth"].ap()[1:2, tc * 512:(tc + 1) * 512].to_broadcast([128, 512]),
               inc=rch)
        for i in range(8):
            src = fa(xoff + i * 512, 512)
            th = bfa(B1_XNT + i * 2048 + tc * 512, 512)
            em.op("vector", lambda e, src=src, th=th, lr=lr: e.tensor_tensor(
                th, src, fa(lr, 512), ALU.mult), waits=[rch, xch], inc="st_v")
            em.op("vector", lambda e, th=th, lr=lr: e.tensor_tensor(
                th, th, fa(lr + 512, 512), ALU.add), inc="st_v")
        hist_v[8 + tc] = em.val("st_v")
    if DEBUG:
        em.dma("gpsimd", dsc["dbgb"].ap()[5], bfa(B1_XNT, 2048), waits=["st_v"],
               inc="dbg_wr")
        em.dma("gpsimd", dsc["dbgb"].ap()[6], bfa(B1_XNH, 2048), waits=["st_v"],
               inc="dbg_wr")


def phase1(em, nc, din, dout, dsc, sb, ps, fa, bfa):
    unit = [0]
    it = 0
    for h in range(NHPC):
        for ck in range(NCH):
            if it < P1HCK:
                l0_pre(em, din, dsc, sb, ps, fa, bfa, h, ck)
                if P1SCAN:
                    l0_scan(em, dsc, sb, ps, fa, bfa, h, ck)
                it += 1
            for _ in range(6):
                if unit[0] < min(96, P1U):
                    ffn0_unit(em, din, dsc, sb, ps, fa, bfa, unit[0])
                    unit[0] += 1
    while unit[0] < min(96, P1U):
        ffn0_unit(em, din, dsc, sb, ps, fa, bfa, unit[0])
        unit[0] += 1


def l0_pre(em, din, dsc, sb, ps, fa, bfa, h, ck):
    t0 = ck * TC1
    war = [("sc_v", em.val("sc_v")), ("epi_v", em.val("epi_v")),
           ("sc_s", em.val("sc_s"))]
    if P1SKIP:
        if P1VAR >= 6:
            dtbcol = sb["dtb"][:, 0:1]
            dsl = fa(P1_DL, 512)
            if P1VAR == 6:
                em.op("vector", lambda e, dsl=dsl, dtbcol=dtbcol: e.tensor_scalar(
                    dsl, ps["pa"][:, :], dtbcol, 0.0, ALU.add, ALU.max),
                    waits=[("pm_v", em.val("pm_v"))], inc="pre_v")
            elif P1VAR == 7:
                em.op("vector", lambda e, dsl=dsl, dtbcol=dtbcol: e.tensor_scalar(
                    dsl, fa(P1_DB, 512), dtbcol, 0.0, ALU.add, ALU.max),
                    waits=[("pm_v", em.val("pm_v"))], inc="pre_v")
            elif P1VAR == 8:
                em.op("vector", lambda e, dsl=dsl: e.tensor_copy(
                    dsl, ps["pa"][:, :]),
                    waits=[("pm_v", em.val("pm_v"))], inc="pre_v")
            return
        nit = 1 if P1VAR >= 3 else 2
        for dh in range(nit):
            for tc in range(nit):
                if P1VAR != 5:
                    em.op("tensor", lambda e, dh=dh, tc=tc: e.matmul(
                        ps["pa"][:, :],
                        sb["dtw"][:, (h * 2 + dh) * 128:(h * 2 + dh) * 128 + 128],
                        fa(P1_DB + tc * 512, 512)[0:RNK, :], start=True, stop=True),
                        waits=["pre_s", ("pre_v", em.val("pre_v"))], inc="pre_t")
                dtbcol = sb["dtb"][:, h * 2 + dh:h * 2 + dh + 1]
                dsl = fa(P1_DL + dh * 1024 + tc * 512, 512)
                if P1VAR != 4:
                    em.op("vector", lambda e, dsl=dsl, dtbcol=dtbcol: e.tensor_scalar(
                        dsl, ps["pa"][:, :], dtbcol, 0.0, ALU.add, ALU.max),
                        waits=["pre_t", ("sc_s", em.val("sc_s"))], inc="pre_v")
                if P1PART >= 6:
                    em.op("scalar", lambda e, dtbcol=dtbcol: e.activation(
                        fa(P1_T0, 512), ps["pa"][:, :], AF.Abs, bias=dtbcol),
                        waits=["pre_t"], inc="pre_s")
        return
    for e4 in range(4):
        for nh in range(2):
            pst = ps["pa"] if (e4 * 2 + nh) % 2 == 0 else ps["pb"]
            em.op("tensor", lambda e, e4=e4, nh=nh, pst=pst: e.matmul(
                pst[:, :], sb["ipw"][:, h * 512 + e4 * 128:h * 512 + (e4 + 1) * 128],
                bfa(B1_XNH + h * 4096 + t0 + nh * 512, 512), start=True, stop=True),
                waits=["pm_v", ("st_v", em.val("st_v")), ("pre_s", em.val("pre_s")),
                       ("pre_v", em.val("pre_v"))],
                inc="pre_t")
            if e4 < 2:
                dst = fa(P1_U + e4 * 1027 + 3 + nh * 512, 512)
                em.op("scalar", lambda e, dst=dst, pst=pst: e.activation(
                    dst, pst[:, :], AF.Copy), waits=["pre_t"] + war, inc="pre_s")
            else:
                dst = fa(P1_SZ + (e4 - 2) * 1024 + nh * 512, 512)
                em.op("scalar", lambda e, dst=dst, pst=pst: e.activation(
                    dst, pst[:, :], AF.Silu), waits=["pre_t"] + war, inc="pre_s")
    if P1PART < 2:
        return
    for dh in range(2):
        if ck == 0:
            em.op("vector", lambda e, dh=dh: e.memset(fa(P1_U + dh * 1027, 3), 0.0),
                  waits=[("pre_s", em.val("pre_s"))], inc="pre_v")
        else:
            em.op("vector", lambda e, dh=dh: e.tensor_copy(
                fa(P1_U + dh * 1027, 3), sb["colz"][:, 8 + dh * 3:11 + dh * 3]),
                waits=[("pre_s", em.val("pre_s"))], inc="pre_v")
    for dh in range(2):
        ub = P1_U + dh * 1027
        ucd = fa(P1_UC + dh * 1024, 1024)
        cwcols = (h * 2 + dh) * 4
        cbcol = sb["convb"][:, h * 2 + dh:h * 2 + dh + 1]
        em.op("vector", lambda e, ub=ub, ucd=ucd, cwcols=cwcols, cbcol=cbcol:
              e.tensor_scalar(ucd, fa(ub, 1024), sb["convw"][:, cwcols:cwcols + 1],
                              cbcol, ALU.mult, ALU.add),
              waits=["pre_s"], inc="pre_v")
        for k in range(1, KC):
            em.op("vector", lambda e, k=k, ub=ub, ucd=ucd, cwcols=cwcols:
                  e.scalar_tensor_tensor(ucd, fa(ub + k, 1024),
                                         sb["convw"][:, cwcols + k:cwcols + k + 1],
                                         ucd, ALU.mult, ALU.add), inc="pre_v")
        if ck < NCH - 1:
            em.op("vector", lambda e, ub=ub, dh=dh: e.tensor_copy(
                sb["colz"][:, 8 + dh * 3:11 + dh * 3], fa(ub + 1024, 3)), inc="pre_v")
        em.op("scalar", lambda e, ucd=ucd: e.activation(ucd, ucd, AF.Silu),
              waits=["pre_v"], inc="pre_s")
    if P1PART < 3:
        return
    for tc in range(2):
        for dh in range(2):
            em.op("tensor", lambda e, tc=tc, dh=dh: e.matmul(
                ps["pb"][0:40, :], sb["xpw"][:, (h * 2 + dh) * 40:(h * 2 + dh) * 40 + 40],
                fa(P1_UC + dh * 1024 + tc * 512, 512), start=(dh == 0), stop=(dh == 1)),
                waits=["pre_s"], inc="pre_t")
        em.op("scalar", lambda e, tc=tc: e.activation(
            fa(P1_DB + tc * 512, 512)[0:40, :], ps["pb"][0:40, :], AF.Copy),
            waits=["pre_t"], inc="pre_s")
    if P1PART < 4:
        return
    em.op("scalar", lambda e: e.activation(sb["bcst"][0:40, :],
                                           fa(P1_DB, 1024)[0:40, :], AF.Copy),
          waits=["pre_s", ("brc_wr", em.val("brc_wr"))], inc="pre_s")
    em.dma("gpsimd", dsc["brc"].ap()[h, :, t0:t0 + TC1], sb["bcst"][8:40, :],
           waits=["pre_s"], inc="brc_wr")
    if P1PART < 5:
        return
    for dh in range(2):
        for tc in range(2):
            em.op("tensor", lambda e, dh=dh, tc=tc: e.matmul(
                ps["pa"][:, :], sb["dtw"][:, (h * 2 + dh) * 128:(h * 2 + dh) * 128 + 128],
                fa(P1_DB + tc * 512, 512)[0:RNK, :], start=True, stop=True),
                waits=["pre_s", ("pre_v", em.val("pre_v"))], inc="pre_t")
            dtbcol = sb["dtb"][:, h * 2 + dh:h * 2 + dh + 1]
            dsl = fa(P1_DL + dh * 1024 + tc * 512, 512)
            em.op("vector", lambda e, dtbcol=dtbcol: e.tensor_scalar(
                fa(P1_T1, 512), ps["pa"][:, :], dtbcol, None, ALU.add),
                waits=["pre_t", ("pre_s", em.val("pre_s")),
                       ("sc_s", em.val("sc_s"))], inc="pre_v")
            em.op("vector", lambda e, dsl=dsl: e.tensor_scalar(
                dsl, fa(P1_T1, 512), 0.0, None, ALU.max), inc="pre_v")
            if P1PART < 6:
                continue
            em.op("scalar", lambda e: e.activation(
                fa(P1_T0, 512), fa(P1_T1, 512), AF.Abs),
                waits=[("pre_v", em.val("pre_v"))], inc="pre_s")
            if P1PART >= 7:
                em.op("scalar", lambda e: e.activation(fa(P1_T1, 512), fa(P1_T0, 512),
                                                       AF.Exp, scale=-1.0),
                      waits=[("pre_v", em.val("pre_v"))], inc="pre_s")
            if P1PART >= 8:
                em.op("scalar", lambda e: e.activation(fa(P1_T1, 512), fa(P1_T1, 512),
                                                       AF.Ln, bias=1.0), inc="pre_s")
            if P1PART < 9:
                continue
            em.op("vector", lambda e, dsl=dsl: e.tensor_tensor(
                dsl, dsl, fa(P1_T1, 512), ALU.add), waits=["pre_s"], inc="pre_v")
        if P1PART < 9:
            continue
        em.op("vector", lambda e, dh=dh: e.tensor_tensor(
            bfa(B1_DU + dh * 1024, 1024), fa(P1_DL + dh * 1024, 1024),
            fa(P1_UC + dh * 1024, 1024), ALU.mult), inc="pre_v")
    if DEBUG and h == 0 and ck == 0:
        em.dma("gpsimd", dsc["dbg"].ap()[5][:, 0:128], sb["acoef"][:, :],
               waits=["pre_v"], inc="dbg_wr")
        em.dma("gpsimd", dsc["dbg"].ap()[0], fa(P1_DL, 2048), waits=["pre_v"],
               inc="dbg_wr")
        em.dma("gpsimd", dsc["dbg"].ap()[1], fa(P1_UC, 2048), waits=["pre_v"],
               inc="dbg_wr")
        em.dma("gpsimd", dsc["dbg"].ap()[2], fa(P1_SZ, 2048), waits=["pre_v"],
               inc="dbg_wr")
        em.dma("gpsimd", dsc["dbgb"].ap()[0], bfa(B1_DU, 2048), waits=["pre_v"],
               inc="dbg_wr")


def l0_scan(em, dsc, sb, ps, fa, bfa, h, ck):
    t0 = ck * TC1
    tbase = em.val("sc_t")
    da_guard = {0: 0, 1: 0}
    rep_guard = {0: 0, 1: 0}
    for n in range(NST):
        par = n % 2
        rch = "rep_ld" if par == 0 else "rep_ldb"
        em.dma("gpsimd", bfa(B1_BR + par * 1024, 1024),
               dsc["brc"].ap()[h, n:n + 1, t0:t0 + TC1].to_broadcast([128, TC1]),
               waits=["brc_wr", ("sc_v", rep_guard[par])],
               inc=rch)
        em.dma("gpsimd", bfa(B1_CR + par * 1024, 1024),
               dsc["brc"].ap()[h, NST + n:NST + n + 1, t0:t0 + TC1]
               .to_broadcast([128, TC1]),
               inc=rch)
        for dh in range(2):
            yps = ps["py0"] if dh == 0 else ps["py1"]
            acol = sb["acoef"][:, (h * 2 + dh) * NST + n:(h * 2 + dh) * NST + n + 1]
            da = fa(P1_DA + dh * 1024, 1024)
            em.op("scalar", lambda e, da=da, acol=acol, dh=dh: e.activation(
                da, fa(P1_DL + dh * 1024, 1024), AF.Exp, scale=acol),
                waits=["pre_v", ("sc_v", da_guard[dh])], inc="sc_s")
            xt = bfa(B1_X, 1024)
            ht = bfa(B1_H, 1024)
            pr = bfa(B1_PR + (par * 2 + dh) * 1024, 1024)
            em.op("vector", lambda e, xt=xt, dh=dh, par=par: e.tensor_tensor(
                xt, bfa(B1_DU + dh * 1024, 1024), bfa(B1_BR + par * 1024, 1024),
                ALU.mult),
                waits=[rch, "sc_s", "pre_v"], inc="sc_v")
            init = 0.0 if ck == 0 else sb["carry"][:, dh * NST + n:dh * NST + n + 1]
            em.op("vector", lambda e, ht=ht, da=da, xt=xt, init=init:
                  e.tensor_tensor_scan(ht, da, xt, init, ALU.mult, ALU.add),
                  inc="sc_v")
            da_guard[dh] = em.val("sc_v")
            prw = [("sc_t", tbase + 4 * (n - 1))] if n >= 2 else []
            em.op("vector", lambda e, pr=pr, ht=ht, par=par: e.tensor_tensor(
                pr, ht, bfa(B1_CR + par * 1024, 1024), ALU.mult),
                waits=prw, inc="sc_v")
            if ck < NCH - 1:
                em.op("vector", lambda e, ht=ht, dh=dh, n=n: e.tensor_copy(
                    sb["carry"][:, dh * NST + n:dh * NST + n + 1], ht[:, TC1 - 1:TC1]),
                    inc="sc_v")
            if DEBUG and h == 0 and ck == 0 and dh == 0 and n == 0:
                em.dma("gpsimd", dsc["dbg"].ap()[3][:, 0:1024], da,
                       waits=["sc_v"], inc="dbg_wr")
                em.dma("gpsimd", dsc["dbgb"].ap()[1][:, 0:1024], xt,
                       waits=["sc_v"], inc="dbg_wr")
                em.dma("gpsimd", dsc["dbgb"].ap()[2][:, 0:1024], ht,
                       waits=["sc_v"], inc="dbg_wr")
                em.dma("gpsimd", dsc["dbgb"].ap()[3][:, 0:1024], pr,
                       waits=["sc_v"], inc="dbg_wr")
            if DEBUG and h == 0 and ck == 0 and n <= 2:
                em.op("vector", lambda e: e.engine_nop(), waits=["dbg_wr"],
                      inc="sc_v")
            for q in range(2):
                em.op("tensor", lambda e, yps=yps, pr=pr, q=q, n=n: e.matmul(
                    yps[:, q * 512:(q + 1) * 512], sb["identb"][:, :],
                    pr[:, q * 512:(q + 1) * 512], start=(n == 0), stop=(n == NST - 1)),
                    waits=["sc_v"], inc="sc_t")
        rep_guard[par] = em.val("sc_v")
    for dh in range(2):
        yps = ps["py0"] if dh == 0 else ps["py1"]
        dcol = sb["ddd"][:, h * 2 + dh:h * 2 + dh + 1]
        yg = bfa(B1_YG + dh * 1024, 1024)
        em.op("vector", lambda e, dh=dh, dcol=dcol, yps=yps: e.scalar_tensor_tensor(
            fa(P1_EP, 1024), fa(P1_UC + dh * 1024, 1024), dcol, yps[:, :],
            ALU.mult, ALU.add),
            waits=["sc_t", ("epi_t", em.val("epi_t"))], inc="epi_v")
        em.op("vector", lambda e, dh=dh, yg=yg: e.tensor_tensor(
            yg, fa(P1_EP, 1024), fa(P1_SZ + dh * 1024, 1024), ALU.mult),
            inc="epi_v")
        if DEBUG and h == 0 and ck == 0 and dh == 0:
            em.dma("gpsimd", dsc["dbg"].ap()[4][:, 0:1024], fa(P1_EP, 1024),
                   waits=["epi_v"], inc="dbg_wr")
            em.dma("gpsimd", dsc["dbgb"].ap()[4][:, 0:1024], yg,
                   waits=["epi_v"], inc="dbg_wr")
            em.op("vector", lambda e: e.engine_nop(), waits=["dbg_wr"], inc="epi_v")
        for q in range(2):
            em.op("tensor", lambda e, dh=dh, q=q, yg=yg: e.matmul(
                ps["pc" if q == 0 else "pd"][:, :],
                sb["opw"][:, (h * 2 + dh) * 128:(h * 2 + dh) * 128 + 128],
                yg[:, q * 512:q * 512 + 512], start=(dh == 0), stop=(dh == 1)),
                waits=["epi_v", ("f0_v", em.val("f0_v")), ("epi_s", em.val("epi_s"))],
                inc="epi_t")
    for q in range(2):
        em.op("scalar", lambda e, q=q: e.activation(
            fa(P1_SS + q * 512, 512), ps["pc" if q == 0 else "pd"][:, :], AF.Copy),
            waits=["epi_t", ("s_wr", em.val("s_wr"))], inc="epi_s")
    em.dma("gpsimd", dsc["sagi"].ap()[h * 128:(h + 1) * 128, t0:t0 + TC1],
           fa(P1_SS, 1024), waits=["epi_s"], inc="s_wr")


def ffn0_unit(em, din, dsc, sb, ps, fa, bfa, unit):
    hidc, tc = unit // 4, unit % 4
    if tc == 0:
        em.dma("sync", sb["w0slab"][:, 0:1024], din["wg0"].ap()[hidc],
               waits=[("f0_s", em.val("f0_s"))], inc="f0_ld")
        em.op("scalar", lambda e: e.activation(sb["w0bf"][:, 0:1024],
                                               sb["w0slab"][:, 0:1024],
                                               AF.Copy, scale=1.0 / 127.0),
              waits=["f0_ld", ("pm_v", 3), ("f0_t", em.val("f0_t"))], inc="f0_s")
        em.dma("sync", sb["w0slab"][:, 0:1024], din["wu0"].ap()[hidc],
               waits=[("f0_s", em.val("f0_s"))], inc="f0_ld")
        em.op("scalar", lambda e: e.activation(sb["w0bf"][:, 1024:2048],
                                               sb["w0slab"][:, 0:1024],
                                               AF.Copy, scale=1.0 / 127.0),
              waits=["f0_ld", ("f0_t", em.val("f0_t"))], inc="f0_s")
    for d in range(8):
        mv = bfa(B1_XNT + d * 2048 + tc * 512, 512)
        em.op("tensor", lambda e, d=d, mv=mv: e.matmul(
            ps["pc"][:, :], sb["w0bf"][:, d * 128:(d + 1) * 128], mv,
            start=(d == 0), stop=(d == 7)),
            waits=["f0_s", ("f0_v", em.val("f0_v")), ("epi_s", em.val("epi_s")),
                   ("st_v", em.val("st_v"))],
            inc="f0_t")
    for d in range(8):
        mv = bfa(B1_XNT + d * 2048 + tc * 512, 512)
        em.op("tensor", lambda e, d=d, mv=mv: e.matmul(
            ps["pd"][:, :], sb["w0bf"][:, 1024 + d * 128:1024 + (d + 1) * 128], mv,
            start=(d == 0), stop=(d == 7)), inc="f0_t")
    em.op("scalar", lambda e: e.activation(fa(P1_F0, 512), ps["pc"][:, :], AF.Silu),
          waits=["f0_t"], inc="f0_s")
    em.op("vector", lambda e: e.tensor_tensor(bfa(B1_HF, 512), fa(P1_F0, 512),
                                              ps["pd"][:, :], ALU.mult),
          waits=["f0_s", ("hf_wr", em.val("hf_wr"))], inc="f0_v")
    em.dma("gpsimd", dsc["hfd"].ap()[hidc, :, tc * 512:(tc + 1) * 512], bfa(B1_HF, 512),
           waits=["f0_v"], inc="hf_wr")


def phase2(em, nc, din, dout, dsc, sb, ps, fa, bfa):
    p1rel = [("pre_t", em.val("pre_t")), ("f0_t", em.val("f0_t")),
             ("sc_t", em.val("sc_t")), ("sc_v", em.val("sc_v")),
             ("epi_t", em.val("epi_t")), ("f0_v", em.val("f0_v")),
             ("sc_s", em.val("sc_s")), ("epi_v", em.val("epi_v"))]
    em.coll("AllGather", ALU.bypass, RG8, [dsc["sagi"].ap().opt()],
            [dsc["sago"].ap().opt()], waits=["s_wr"], inc="cc")
    for hc in range(24):
        em.dma("sync", sb["w0slab"][:, 0:1024], din["wd0"].ap()[hc],
               waits=[("g_s", em.val("g_s")), ("f0_t", em.val("f0_t"))],
               inc="f0_ld")
        em.op("scalar", lambda e, hc=hc: e.activation(
            bfa(B2_WD + hc * 1024, 1024), sb["w0slab"][:, 0:1024], AF.Copy,
            scale=1.0 / 127.0), waits=["f0_ld", ("pm_v", 3)] + p1rel, inc="g_s")
    if DEBUG:
        em.dma("gpsimd", dsc["dbgb"].ap()[7][:, 0:1024], bfa(B2_WD, 1024),
               waits=["g_s"], inc="dbg_wr")
    lp = lpbq_all(em, din, dsc, sb, fa, bfa, p1rel)
    for tc in range(4):
        for hc in range(24):
            em.dma("sync", bfa(B2_HF + hc * 512, 512),
                   dsc["hfd"].ap()[hc, :, tc * 512:(tc + 1) * 512],
                   waits=["hf_wr", ("g_t", tc * 192)] + p1rel, inc="hf_ld")
        if DEBUG and tc == 0:
            em.dma("gpsimd", dsc["dbgb"].ap()[7][:, 1024:1536], bfa(B2_HF, 512),
                   waits=["hf_ld"], inc="dbg_wr")
        for d in range(8):
            pst = ps["pc"] if d % 2 == 0 else ps["pd"]
            for hc in range(24):
                dbw = ["dbg_wr"] if DEBUG else []
                em.op("tensor", lambda e, d=d, hc=hc, pst=pst: e.matmul(
                    pst[:, :], bfa(B2_WD + hc * 1024 + d * 128, 128),
                    bfa(B2_HF + hc * 512, 512), start=(hc == 0), stop=(hc == 23)),
                    waits=["g_s", "hf_ld", ("g_v", em.val("g_v"))] + dbw, inc="g_t")
            em.op("scalar", lambda e, d=d, pst=pst: e.activation(
                fa(P2_G, 512), pst[:, :], AF.Silu),
                waits=["g_t", ("c_wr", em.val("c_wr"))], inc="g_s")
            em.dma("gpsimd", dsc["gd"].ap()[d, :, tc * 512:(tc + 1) * 512],
                   fa(P2_G, 512), waits=["g_s"], inc="c_wr")
        next(lp, None)
        next(lp, None)
    for _ in lp:
        pass
    def sbody(e, core, then):
        g = core // 4
        b = core % 4
        for rk, half in ((b, 0), (b + 4, 1)):
            ins = e.dma_start(
                out=dsc["sloc"].ap()[half * 512:(half + 1) * 512, :],
                in_=dsc["sago"].ap()[rk * 512:(rk + 1) * 512,
                                     g * THALF:(g + 1) * THALF])
            then(ins)

    em.switch("gpsimd", sbody, waits=["cc"], inc="loc_wr", nincs=2)
    sw_done = em.val("loc_wr")
    for tc in range(4):
        for d in range(8):
            em.dma("sync", fa(P2_X + d * 512, 512),
                   dsc["xth"].ap()[d * 128:(d + 1) * 128, tc * 512:(tc + 1) * 512],
                   waits=[("c_v", em.val("c_v")), ("loc_wr", sw_done)] + p1rel,
                   inc="c_ld")
        for d in range(8):
            em.dma("sync", fa(P2_S + d * 512, 512),
                   dsc["sloc"].ap()[d * 128:(d + 1) * 128, tc * 512:(tc + 1) * 512],
                   waits=[("loc_wr", sw_done), ("c_v", em.val("c_v"))], inc="c_ld")
        for d in range(8):
            em.dma("sync", fa(P2_X1 + d * 512, 512),
                   dsc["gd"].ap()[d, :, tc * 512:(tc + 1) * 512],
                   waits=["c_wr", ("c_v", em.val("c_v"))], inc="c_ld")
        for d in range(8):
            em.op("vector", lambda e, d=d: e.scalar_tensor_tensor(
                fa(P2_X1 + d * 512, 512), fa(P2_S + d * 512, 512), 1.0,
                fa(P2_X1 + d * 512, 512), ALU.mult, ALU.mult),
                waits=["c_ld"], inc="c_v")
            em.op("vector", lambda e, d=d: e.tensor_tensor(
                fa(P2_X1 + d * 512, 512), fa(P2_X1 + d * 512, 512),
                fa(P2_X + d * 512, 512), ALU.add), inc="c_v")
            em.dma("gpsimd", dsc["x1agi"].ap()[d * 128:(d + 1) * 128,
                                               tc * 512:(tc + 1) * 512],
                   fa(P2_X1 + d * 512, 512), waits=["c_v"], inc="c_wr")
    em.coll("AllGather", ALU.bypass, RG8, [dsc["x1agi"].ap().opt()],
            [dsc["x1ago"].ap().opt()], waits=["c_wr"], inc="cc")


def lpbq_all(em, din, dsc, sb, fa, bfa, p1rel):
    for wnm, qnm, ntile, icols in (("wg1", "wqg", 24, C), ("wu1", "wqu", 24, C),
                                   ("wd1", "wqd", 8, HID)):
        for ot in range(ntile):
            nb = icols // 32
            w = fa(P2_LW, icols)
            wv = w.rearrange("p (nb b) -> p nb b", b=32)
            smax = fa(P2_LS, nb)
            em.dma("sync", w, din[wnm].ap()[ot * 128:(ot + 1) * 128, :],
                   waits=[("q2_wr", em.val("q2_wr"))] + p1rel,
                   inc="q2_ld")
            em.op("vector", lambda e, wv=wv, smax=smax: e.tensor_reduce(
                smax, wv, AX.X, ALU.max, apply_absolute_value=True),
                waits=["q2_ld"], inc="q2_v")
            vp(em, sb, smax, "q2_v", 128, nb)
            em.op("vector", lambda e, smax=smax: e.tensor_scalar(
                smax, smax, 1.0 / 7.0, 1e-12, ALU.mult, ALU.max), inc="q2_v")
            vp(em, sb, smax, "q2_v", 128, nb)
            srec = fa(P2_LS + 128, nb)
            em.op("vector", lambda e, srec=srec, smax=smax: e.reciprocal(srec, smax),
                  inc="q2_v")
            vp(em, sb, srec, "q2_v", 128, nb)
            sbb = smax.unsqueeze(2).to_broadcast([128, nb, 32])
            srb = srec.unsqueeze(2).to_broadcast([128, nb, 32])
            em.op("vector", lambda e, wv=wv, srb=srb: e.tensor_tensor(
                wv, wv, srb, ALU.mult), inc="q2_v")
            for cchunk in range(icols // 512):
                c0 = cchunk * 512
                em.op("vector", lambda e, c0=c0, w=w: e.tensor_scalar(
                    sb["i0"][:, :], w[:, c0:c0 + 512], RC, -RC, ALU.add, ALU.add),
                    inc="q2_v")
                em.op("vector", lambda e, c0=c0, w=w: e.tensor_copy(
                    w[:, c0:c0 + 512], sb["i0"][:, :]), inc="q2_v")
            em.op("vector", lambda e, wv=wv, sbb=sbb: e.tensor_tensor(
                bfa(B2_HF + 12288, icols).rearrange("p (nb b) -> p nb b", b=32),
                wv, sbb, ALU.mult), inc="q2_v")
            em.dma("gpsimd", dsc[qnm].ap()[ot * 128:(ot + 1) * 128, :],
                   bfa(B2_HF + 12288, icols), waits=["q2_v"], inc="q2_wr")
            yield


def phase3(em, nc, din, dout, dsc, sb, ps, fa, bfa):
    p2rel = [("g_t", em.val("g_t")), ("c_v", em.val("c_v")),
             ("q2_v", em.val("q2_v")), ("q2_wr", em.val("q2_wr")),
             ("g_s", em.val("g_s")), ("c_wr", em.val("c_wr"))]
    for nm, dst in [("wkT", B3_WK), ("wvT", B3_WV), ("wrT", B3_WR)]:
        for d in range(8):
            em.dma("sync", fa(P3_X1 + (d % 2) * 512, 512),
                   din[nm].ap()[d * 128:(d + 1) * 128, :],
                   waits=[("p3_s", em.val("p3_s"))] + p2rel, inc="p3_ld")
            em.op("scalar", lambda e, dst=dst, d=d: e.activation(
                bfa(dst + d * 512, 512), fa(P3_X1 + (d % 2) * 512, 512), AF.Copy),
                waits=["p3_ld"], inc="p3_s")
    for ct in range(4):
        em.dma("sync", fa(P3_X1 + (ct % 2) * 1024, 1024),
               din["woT"].ap()[ct * 128:(ct + 1) * 128, :],
               waits=[("p3_s", em.val("p3_s"))], inc="p3_ld")
        em.op("scalar", lambda e, ct=ct: e.activation(
            bfa(B3_WO + ct * 1024, 1024), fa(P3_X1 + (ct % 2) * 1024, 1024), AF.Copy),
            waits=["p3_ld"], inc="p3_s")
    em.op("vector", lambda e: e.memset(sb["pcarry"][:, :], 0.0),
          waits=[("p3_s", em.val("p3_s"))], inc="p3_v")
    vp(em, sb, sb["pcarry"][:, :], "p3_v", 128, 12)
    em.op("vector", lambda e: e.tensor_scalar(sb["pcarry"][:, 0:4],
                                              sb["pcarry"][:, 0:4], 0.0, -1e30,
                                              ALU.mult, ALU.add), inc="p3_v")
    vp(em, sb, sb["pcarry"][:, 0:4], "p3_v", 128, 4)

    def x1body(e, core, then):
        b = core % 4
        for half in range(2):
            rk = b + 4 * half
            ins = e.dma_start(
                out=dsc["x1loc"].ap()[:, half * THALF:(half + 1) * THALF],
                in_=dsc["x1ago"].ap()[rk * 1024:(rk + 1) * 1024, :])
            then(ins)

    em.switch("gpsimd", x1body, waits=["cc"], inc="loc_wr", nincs=2)
    x1loc_done = em.val("loc_wr")

    for ck in range(8):
        t0 = ck * 512
        for d in range(8):
            em.dma("sync", fa(P3_X1 + d * 513 + 1, 512),
                   dsc["x1loc"].ap()[d * 128:(d + 1) * 128, t0:t0 + 512],
                   waits=[("loc_wr", x1loc_done), ("p3_v", em.val("p3_v")),
                          ("p3_t", em.val("p3_t")), ("p3_s", em.val("p3_s"))],
                   inc="p3_ld")
        sqt = {}
        for d in range(8):
            xs = fa(P3_X1 + d * 513 + 1, 512)
            sq = fa(P3_XN + (d % 2) * 512, 512)
            em.op("tensor", lambda e, xs=xs, d=d: e.matmul(
                ps["pa"][0:1, :], sb["onesf"][:, :], xs, start=(d == 0), stop=(d == 7)),
                waits=["p3_ld"], inc="p3_t")
            em.op("scalar", lambda e, xs=xs, sq=sq: e.activation(sq, xs, AF.Square),
                  waits=["p3_ld", ("p3_t", sqt.get(d - 2, 0))], inc="p3_s")
            em.op("tensor", lambda e, sq=sq, d=d: e.matmul(
                ps["pa"][32:33, :], sb["onesf"][:, :], sq, start=(d == 0), stop=(d == 7)),
                waits=["p3_s"], inc="p3_t")
            sqt[d] = em.val("p3_t")
        ln_rows(em, sb, fa, ps["pa"], P3_RW, "p3_s", "p3_v", "p3_t",
                extraw=[("row3", em.val("row3"))])
        em.dma("gpsimd", dsc["lnrow3"].ap()[0:1, t0:t0 + 512], fa(P3_RW, 512)[0:1, :],
               waits=["p3_v"], inc="row3")
        em.dma("gpsimd", dsc["lnrow3"].ap()[1:2, t0:t0 + 512],
               fa(P3_RW + 512, 512)[0:1, :], inc="row3")
        em.dma("gpsimd", fa(P3_LR, 512),
               dsc["lnrow3"].ap()[0:1, t0:t0 + 512].to_broadcast([128, 512]),
               waits=["row3", ("p3_v", em.val("p3_v"))], inc="rep0")
        em.dma("gpsimd", fa(P3_LR + 512, 512),
               dsc["lnrow3"].ap()[1:2, t0:t0 + 512].to_broadcast([128, 512]),
               inc="rep0")
        for d in range(8):
            xs = fa(P3_X1 + d * 513 + 1, 512)
            xn = fa(P3_XN + d * 513 + 1, 512)
            em.op("vector", lambda e, xs=xs, xn=xn: e.tensor_tensor(
                xn, xs, fa(P3_LR, 512), ALU.mult), waits=["rep0"], inc="p3_v")
            em.op("vector", lambda e, xn=xn: e.tensor_tensor(
                xn, xn, fa(P3_LR + 512, 512), ALU.add), inc="p3_v")
            em.op("vector", lambda e, xn=xn: e.tensor_reduce(
                sb["mxc"][:, 1:2], xn, AX.X, ALU.max, apply_absolute_value=True),
                inc="p3_v")
            vp(em, sb, sb["mxc"][:, 1:2], "p3_v", 128, 1)
            em.op("vector", lambda e: e.tensor_tensor(
                sb["mxc"][:, 0:1], sb["mxc"][:, 0:1], sb["mxc"][:, 1:2], ALU.max),
                inc="p3_v")
            vp(em, sb, sb["mxc"][:, 0:1], "p3_v", 128, 1)
            em.op("vector", lambda e, d=d: e.tensor_copy(
                fa(P3_XN + d * 513, 1), sb["xhalo"][:, d:d + 1]), inc="p3_v")
            vp(em, sb, fa(P3_XN + d * 513, 1), "p3_v", 128, 1)
            for mixn, mixdst in (("tmk", B3_XK), ("tmv", B3_XV), ("tmr", B3_XR)):
                col = sb[mixn][:, d:d + 1]
                em.op("vector", lambda e, d=d, xn=xn: e.tensor_tensor(
                    fa(P3_Y, 512), xn, fa(P3_XN + d * 513, 512), ALU.subtract),
                    inc="p3_v")
                em.op("vector", lambda e, d=d, col=col, mixdst=mixdst: e.scalar_tensor_tensor(
                    bfa(mixdst + d * 512, 512), fa(P3_Y, 512), col,
                    fa(P3_XN + d * 513, 512), ALU.mult, ALU.add), inc="p3_v")
            em.op("vector", lambda e, d=d, xn=xn: e.tensor_copy(
                sb["xhalo"][:, d:d + 1], xn[:, 511:512]), inc="p3_v")
            vp(em, sb, sb["xhalo"][:, d:d + 1], "p3_v", 128, 1)
        if ck == 7:
            em.dma("gpsimd", dsc["pmax"].ap()[0:1, :], sb["mxc"][:, 0:1],
                   waits=[("p3_v", em.val("p3_v"))], inc="row3")
            em.dma("gpsimd", sb["rowz"][0:1, :], dsc["pmax"].ap()[0:1, :],
                   waits=["row3"], inc="rep0")
            em.op("vector", lambda e: e.tensor_reduce(
                sb["colz"][0:1, 2:3], sb["rowz"][0:1, :], AX.X, ALU.max),
                waits=["rep0"], inc="p3_v")
            vp(em, sb, sb["colz"][0:1, 2:3], "p3_v", 1, 1)
            em.dma("gpsimd", dsc["ar1i"].ap()[0:1, 0:1], sb["colz"][0:1, 2:3],
                   waits=["p3_v"], inc="row3")
            em.dma("gpsimd", dsc["ar1i"].ap()[0:1, 1:2], sb["colz"][0:1, 2:3],
                   inc="row3")
            em.coll("AllReduce", ALU.max, RG8, [dsc["ar1i"].ap().opt()],
                    [dsc["ar1o"].ap().opt()], waits=["row3"], inc="cc")
            em.cc_ar1 = em.val("cc")
        for ct in range(4):
            for nm, wof, src, pst in (("k", B3_WK, B3_XK, "pb"),
                                      ("v", B3_WV, B3_XV, "pc"),
                                      ("r", B3_WR, B3_XR, "pd")):
                for d in range(8):
                    em.op("tensor", lambda e, wof=wof, ct=ct, d=d, src=src, pst=pst:
                          e.matmul(ps[pst][:, :],
                                   bfa(wof + d * 512 + ct * 128, 128),
                                   bfa(src + d * 512, 512),
                                   start=(d == 0), stop=(d == 7)),
                          waits=["p3_v", ("p3_s", em.val("p3_s"))], inc="p3_t")
                if nm == "k":
                    em.op("scalar", lambda e, ct=ct, pst=pst: e.activation(
                        fa(P3_K + ct * 512, 512), ps[pst][:, :], AF.Copy),
                        waits=["p3_t"], inc="p3_s")
                elif nm == "v":
                    em.op("scalar", lambda e, ct=ct, pst=pst: e.activation(
                        fa(P3_V + ct * 512, 512), ps[pst][:, :], AF.Copy),
                        waits=["p3_t"], inc="p3_s")
                else:
                    em.op("scalar", lambda e, ct=ct, pst=pst: e.activation(
                        fa(P3_SR + ct * 512, 512), ps[pst][:, :], AF.Sigmoid),
                        waits=["p3_t"], inc="p3_s")
        for ct in range(4):
            wkv_tile(em, sb, fa, bfa, ct)
        for od in range(8):
            for ct in range(4):
                em.op("tensor", lambda e, od=od, ct=ct: e.matmul(
                    ps["pa"][:, :],
                    bfa(B3_WO + ct * 1024 + od * 128, 128),
                    bfa(B3_Y1 + ct * 512, 512), start=(ct == 0), stop=(ct == 3)),
                    waits=["p3_v", ("p3_s", em.val("p3_s"))], inc="p3_t")
            em.op("scalar", lambda e, od=od: e.activation(
                bfa(B3_ST, 512), ps["pa"][:, :], AF.Copy),
                waits=["p3_t", ("p3_wr", em.val("p3_wr"))], inc="p3_s")
            em.dma("gpsimd", dsc["s1agi"].ap()[od * 128:(od + 1) * 128, t0:t0 + 512],
                   bfa(B3_ST, 512), waits=["p3_s"], inc="p3_wr")
    em.coll("AllGather", ALU.bypass, RG8, [dsc["s1agi"].ap().opt()],
            [dsc["s1ago"].ap().opt()], waits=["p3_wr"], inc="cc")


def wkv_tile(em, sb, fa, bfa, ct):
    kk = fa(P3_K + ct * 512, 512)
    vv = fa(P3_V + ct * 512, 512)
    sr = fa(P3_SR + ct * 512, 512)
    ab = fa(P3_AB + ct * 513, 513)
    bb = fa(P3_BB + ct * 513, 513)
    yy = fa(P3_Y, 512)
    E = fa(P3_XN, 512)
    E2 = fa(P3_XN + 513, 512)
    Ev = fa(P3_XN + 2 * 513, 512)
    ucol = sb["wkvp"][:, ct:ct + 1]
    ewcol = sb["wkvp"][:, 4 + ct:5 + ct]
    pcol = sb["pcarry"][:, ct:ct + 1]
    m = sb["colz"][:, 4:5]
    t1 = sb["colz"][:, 5:6]
    t2 = sb["colz"][:, 6:7]
    nm = sb["colz"][:, 9:10]
    um = sb["colz"][:, 10:11]
    ewb = yy
    em.op("vector", lambda e: e.tensor_reduce(t1, kk, AX.X, ALU.max),
          waits=[("p3_s", em.val("p3_s"))], inc="p3_v")
    vp(em, sb, t1, "p3_v", 128, 1)
    em.op("vector", lambda e: e.tensor_scalar(t2, ucol, 0.0, None, ALU.max), inc="p3_v")
    vp(em, sb, t2, "p3_v", 128, 1)
    em.op("vector", lambda e: e.tensor_tensor(t2, t2, t1, ALU.add), inc="p3_v")
    vp(em, sb, t2, "p3_v", 128, 1)
    em.op("vector", lambda e: e.tensor_tensor(m, t2, pcol, ALU.max), inc="p3_v")
    vp(em, sb, m, "p3_v", 128, 1)
    em.op("vector", lambda e: e.tensor_scalar(nm, m, -1.0, None, ALU.mult), inc="p3_v")
    vp(em, sb, nm, "p3_v", 128, 1)
    em.op("vector", lambda e: e.tensor_tensor(um, ucol, nm, ALU.add), inc="p3_v")
    vp(em, sb, um, "p3_v", 128, 1)
    em.op("scalar", lambda e: e.activation(E, kk, AF.Exp, bias=nm),
          waits=["p3_v"], inc="p3_s")
    em.op("scalar", lambda e: e.activation(E2, kk, AF.Exp, bias=um), inc="p3_s")
    em.op("vector", lambda e: e.tensor_tensor(t1, pcol, nm, ALU.add),
          waits=["p3_s"], inc="p3_v")
    vp(em, sb, t1, "p3_v", 128, 1)
    em.op("scalar", lambda e: e.activation(t1, t1, AF.Exp), waits=["p3_v"], inc="p3_s")
    em.op("vector", lambda e: e.tensor_tensor(ab[:, 0:1], sb["pcarry"][:, 4 + ct:5 + ct],
                                              t1, ALU.mult), waits=["p3_s"], inc="p3_v")
    vp(em, sb, ab[:, 0:1], "p3_v", 128, 1)
    em.op("vector", lambda e: e.tensor_tensor(bb[:, 0:1], sb["pcarry"][:, 8 + ct:9 + ct],
                                              t1, ALU.mult), inc="p3_v")
    vp(em, sb, bb[:, 0:1], "p3_v", 128, 1)
    em.op("vector", lambda e, ewb=ewb: e.tensor_scalar(
        ewb, kk, 0.0, ewcol, ALU.mult, ALU.add), inc="p3_v")
    em.op("vector", lambda e: e.tensor_tensor(Ev, E, vv, ALU.mult), inc="p3_v")
    em.op("vector", lambda e, ewb=ewb: e.tensor_tensor_scan(
        ab[:, 1:513], ewb, Ev, ab[:, 0:1],
        ALU.mult, ALU.add), inc="p3_v")
    em.op("vector", lambda e, ewb=ewb: e.tensor_tensor_scan(
        bb[:, 1:513], ewb, E, bb[:, 0:1],
        ALU.mult, ALU.add), inc="p3_v")
    em.op("vector", lambda e: e.tensor_tensor(yy, E2, vv, ALU.mult), inc="p3_v")
    em.op("vector", lambda e: e.tensor_tensor(yy, yy, ab[:, 0:512], ALU.add), inc="p3_v")
    em.op("vector", lambda e: e.tensor_tensor(E2, E2, bb[:, 0:512], ALU.add), inc="p3_v")
    em.op("vector", lambda e: e.reciprocal(E2, E2), inc="p3_v")
    em.op("vector", lambda e: e.tensor_tensor(yy, yy, E2, ALU.mult), inc="p3_v")
    em.op("vector", lambda e, ct=ct: e.tensor_tensor(
        bfa(B3_Y1 + ct * 512, 512), yy, sr, ALU.mult), inc="p3_v")
    em.op("vector", lambda e: e.tensor_copy(pcol, m), inc="p3_v")
    vp(em, sb, pcol, "p3_v", 128, 1)
    em.op("vector", lambda e, ct=ct: e.tensor_copy(sb["pcarry"][:, 4 + ct:5 + ct],
                                                   ab[:, 512:513]), inc="p3_v")
    vp(em, sb, sb["pcarry"][:, 4 + ct:5 + ct], "p3_v", 128, 1)
    em.op("vector", lambda e, ct=ct: e.tensor_copy(sb["pcarry"][:, 8 + ct:9 + ct],
                                                   bb[:, 512:513]), inc="p3_v")
    vp(em, sb, sb["pcarry"][:, 8 + ct:9 + ct], "p3_v", 128, 1)


def colmax_to_scalar(em, dsc, sb, srccol, dstcol, pcol_idx, chV, chDMA):
    em.dma("gpsimd", dsc["pmax"].ap()[pcol_idx:pcol_idx + 1, :], srccol,
           waits=[chV], inc=chDMA)
    em.dma("gpsimd", sb["rowz"][0:1, :],
           dsc["pmax"].ap()[pcol_idx:pcol_idx + 1, :],
           waits=[chDMA], inc=chDMA)
    em.op("vector", lambda e: e.tensor_reduce(
        sb["colz"][0:1, dstcol:dstcol + 1], sb["rowz"][0:1, :], AX.X, ALU.max),
        waits=[chDMA], inc=chV)
    vp(em, sb, sb["colz"][0:1, dstcol:dstcol + 1], chV, 1, 1)


def phase4(em, nc, din, dout, dsc, sb, ps, fa, bfa):
    em.dma("gpsimd", sb["colz"][:, 0:1],
           dsc["ar1o"].ap()[0:1, 0:1].to_broadcast([128, 1]),
           waits=[("cc", em.cc_ar1)], inc="rep0")
    em.op("vector", lambda e: e.tensor_scalar(sb["colz"][:, 0:1], sb["colz"][:, 0:1],
                                              1.0 / QM15, 1e-12, ALU.mult, ALU.max),
          waits=["rep0"], inc="q_v")
    vp(em, sb, sb["colz"][:, 0:1], "q_v", 128, 1)
    em.op("vector", lambda e: e.reciprocal(sb["colz"][:, 1:2], sb["colz"][:, 0:1]),
          inc="q_v")
    vp(em, sb, sb["colz"][:, 1:2], "q_v", 128, 1)
    em.op("vector", lambda e: e.memset(sb["mxc"][:, 2:4], 0.0), inc="q_v")
    vp(em, sb, sb["mxc"][:, 2:4], "q_v", 128, 2)

    p3rel = [("p3_v", em.val("p3_v")), ("p3_t", em.val("p3_t")),
             ("p3_s", em.val("p3_s")), ("p3_wr", em.val("p3_wr"))]
    for tc in range(4):
        for d in range(8):
            em.dma("sync", fa(P4_X1 + d * 512, 512),
                   dsc["x1agi"].ap()[d * 128:(d + 1) * 128, tc * 512:(tc + 1) * 512],
                   waits=[("q_v", em.val("q_v")), ("q_s", em.val("q_s"))] + p3rel,
                   inc="q_ld")
        sqt = {}
        for d in range(8):
            xs = fa(P4_X1 + d * 512, 512)
            sq = fa(P4_SG, 512)
            em.op("tensor", lambda e, xs=xs, d=d: e.matmul(
                ps["pa"][0:1, :], sb["onesf"][:, :], xs, start=(d == 0), stop=(d == 7)),
                waits=["q_ld"], inc="q_t")
            em.op("scalar", lambda e, xs=xs, sq=sq: e.activation(sq, xs, AF.Square),
                  waits=["q_ld", ("q_t", sqt.get(d - 1, 0))], inc="q_s")
            em.op("tensor", lambda e, sq=sq, d=d: e.matmul(
                ps["pa"][32:33, :], sb["onesf"][:, :], sq, start=(d == 0), stop=(d == 7)),
                waits=["q_s"], inc="q_t")
            sqt[d] = em.val("q_t")
        ln_rows(em, sb, fa, ps["pa"], P4_RW, "q_s", "q_v", "q_t",
                extraw=[("row4", em.val("row4"))])
        em.dma("gpsimd", dsc["lnrow4"].ap()[0:1, tc * 512:(tc + 1) * 512],
               fa(P4_RW, 512)[0:1, :], waits=["q_v"], inc="row4")
        em.dma("gpsimd", dsc["lnrow4"].ap()[1:2, tc * 512:(tc + 1) * 512],
               fa(P4_RW + 512, 512)[0:1, :], inc="row4")
        em.dma("gpsimd", fa(P4_LR, 512),
               dsc["lnrow4"].ap()[0:1, tc * 512:(tc + 1) * 512].to_broadcast([128, 512]),
               waits=["row4", ("q_v", em.val("q_v"))], inc="rep0")
        em.dma("gpsimd", fa(P4_LR + 512, 512),
               dsc["lnrow4"].ap()[1:2, tc * 512:(tc + 1) * 512].to_broadcast([128, 512]),
               inc="rep0")
        for d in range(8):
            xn = fa(P4_XN + d * 512, 512)
            em.op("vector", lambda e, d=d, xn=xn: e.tensor_tensor(
                xn, fa(P4_X1 + d * 512, 512), fa(P4_LR, 512), ALU.mult),
                waits=["rep0"], inc="q_v")
            em.op("vector", lambda e, xn=xn: e.tensor_tensor(
                xn, xn, fa(P4_LR + 512, 512), ALU.add), inc="q_v")
            vst = fa(P4_CF, 512)
            em.op("vector", lambda e, xn=xn, vst=vst: e.tensor_scalar(
                vst, xn, sb["colz"][:, 1:2], RC, ALU.mult, ALU.add), inc="q_v")
            em.op("vector", lambda e, vst=vst: e.tensor_scalar(
                sb["i0"][:, :], vst, -RC, None, ALU.add), inc="q_v")
            em.op("scalar", lambda e, d=d, tc=tc: e.activation(
                bfa(B4_XQ + tc * 4096 + d * 512, 512), sb["i0"][:, :], AF.Copy,
                scale=sb["colz"][:, 0:1]), waits=["q_v"], inc="q_s")

    for wi, (qnm, bdram) in enumerate((("wqg", "gtd"), ("wqu", "upd"))):
        for ic in range(8):
            em.dma("sync", bfa(B4_WQ + ic * 3072, 3072),
                   dsc[qnm].ap()[0:HID, ic * 128:(ic + 1) * 128],
                   waits=["q2_wr", ("q_t", em.val("q_t"))] + p3rel, inc="q_ld",
                   transpose=True)
        for hid in range(24):
            for tc in range(4):
                pst = ps["pc"] if (hid * 4 + tc) % 2 == 0 else ps["pd"]
                for d in range(8):
                    em.op("tensor", lambda e, hid=hid, tc=tc, d=d, pst=pst: e.matmul(
                        pst[:, :], bfa(B4_WQ + d * 3072 + hid * 128, 128),
                        bfa(B4_XQ + tc * 4096 + d * 512, 512),
                        start=(d == 0), stop=(d == 7)),
                        waits=["q_ld", "q_s", ("q_v", em.val("q_v"))], inc="q_t")
                em.op("vector", lambda e, pst=pst: e.tensor_reduce(
                    sb["mxc"][:, 4:5], pst[:, :], AX.X, ALU.max,
                    apply_absolute_value=True), waits=["q_t"], inc="q_v")
                vp(em, sb, sb["mxc"][:, 4:5], "q_v", 128, 1)
                em.op("vector", lambda e, wi=wi: e.tensor_tensor(
                    sb["mxc"][:, 2 + wi:3 + wi], sb["mxc"][:, 2 + wi:3 + wi],
                    sb["mxc"][:, 4:5], ALU.max), inc="q_v")
                vp(em, sb, sb["mxc"][:, 2 + wi:3 + wi], "q_v", 128, 1)
                em.op("scalar", lambda e, pst=pst: e.activation(
                    bfa(B4_ST, 512), pst[:, :], AF.Copy),
                    waits=["q_v", ("q_wr", em.val("q_wr"))], inc="q_s")
                em.dma("gpsimd", dsc[bdram].ap()[hid, :, tc * 512:(tc + 1) * 512],
                       bfa(B4_ST, 512), waits=["q_s"], inc="q_wr")
    colmax_to_scalar(em, dsc, sb, sb["mxc"][:, 2:3], 3, 1, "q_v", "row4")
    colmax_to_scalar(em, dsc, sb, sb["mxc"][:, 3:4], 4, 2, "q_v", "row4")
    em.dma("gpsimd", dsc["ar2i"].ap()[0:1, 0:2], sb["colz"][0:1, 3:5],
           waits=["q_v"], inc="row4")
    em.coll("AllReduce", ALU.max, RG8, [dsc["ar2i"].ap().opt()],
            [dsc["ar2o"].ap().opt()], waits=["row4"], inc="cc")
    em.dma("gpsimd", sb["colz"][:, 2:3],
           dsc["ar2o"].ap()[0:1, 0:1].to_broadcast([128, 1]), waits=["cc"], inc="rep0")
    em.dma("gpsimd", sb["colz"][:, 4:5],
           dsc["ar2o"].ap()[0:1, 1:2].to_broadcast([128, 1]), inc="rep0")
    for c0 in (2, 4):
        em.op("vector", lambda e, c0=c0: e.tensor_scalar(
            sb["colz"][:, c0:c0 + 1], sb["colz"][:, c0:c0 + 1], 1.0 / QM15, 1e-12,
            ALU.mult, ALU.max), waits=["rep0"], inc="q_v")
        vp(em, sb, sb["colz"][:, c0:c0 + 1], "q_v", 128, 1)
        em.op("vector", lambda e, c0=c0: e.reciprocal(
            sb["colz"][:, c0 + 1:c0 + 2], sb["colz"][:, c0:c0 + 1]), inc="q_v")
        vp(em, sb, sb["colz"][:, c0 + 1:c0 + 2], "q_v", 128, 1)
    em.op("vector", lambda e: e.memset(sb["mxi"][:, :], 0), inc="q_v")
    vp(em, sb, sb["mxi"][:, :], "q_v", 128, 2)
    qwr_all = em.val("q_wr")
    rb_guard = {0: 0, 1: 0}

    def rebuild_P(tc, hid, with_up):
        par = hid % 2
        qch = "q_gt0" if par == 0 else "q_gt1"
        gt = bfa(B4_GU + par * 1024, 512)
        ut = bfa(B4_GU + par * 1024 + 512, 512)
        em.dma("sync", gt, dsc["gtd"].ap()[hid, :, tc * 512:(tc + 1) * 512],
               waits=[("q_v", rb_guard[par]), ("q_wr", qwr_all)], inc=qch)
        if with_up:
            em.dma("sync", ut, dsc["upd"].ap()[hid, :, tc * 512:(tc + 1) * 512],
                   inc=qch)
        vst = fa(P4_CF, 512)
        em.op("vector", lambda e, gt=gt, vst=vst: e.tensor_scalar(
            vst, gt, sb["colz"][:, 3:4], RC, ALU.mult, ALU.add),
            waits=[qch, ("q_s", em.val("q_s"))], inc="q_v")
        em.op("vector", lambda e, vst=vst: e.tensor_scalar(
            sb["i0"][:, :], vst, -RC, None, ALU.add), inc="q_v")
        em.op("scalar", lambda e: e.activation(fa(P4_SG, 512), sb["i0"][:, :],
                                               AF.Sigmoid, scale=sb["colz"][:, 2:3]),
              waits=["q_v"], inc="q_s")
        em.op("vector", lambda e, vst=vst: e.tensor_scalar(
            vst, fa(P4_SG, 512), 65536.0, RC, ALU.mult, ALU.add),
            waits=["q_s"], inc="q_v")
        em.op("vector", lambda e, vst=vst: e.tensor_scalar(
            sb["i1"][:, :], vst, -RC, None, ALU.add), inc="q_v")
        em.op("vector", lambda e: e.tensor_scalar(sb["i1"][:, :], sb["i1"][:, :],
                                                  65535, None, ALU.min), inc="q_v")
        em.op("vector", lambda e: e.tensor_tensor(sb["i2"][:, :], sb["i0"][:, :],
                                                  sb["i1"][:, :], ALU.mult), inc="q_v")
        return ut

    for tc in range(4):
        for hid in range(24):
            rebuild_P(tc, hid, False)
            em.op("vector", lambda e: e.tensor_reduce(
                sb["mxi"][:, 1:2], sb["i2"][:, :], AX.X, ALU.max,
                apply_absolute_value=True), inc="q_v")
            vp(em, sb, sb["mxi"][:, 1:2], "q_v", 128, 1)
            em.op("vector", lambda e: e.tensor_tensor(
                sb["mxi"][:, 0:1], sb["mxi"][:, 0:1], sb["mxi"][:, 1:2], ALU.max),
                inc="q_v")
            vp(em, sb, sb["mxi"][:, 0:1], "q_v", 128, 1)
            rb_guard[hid % 2] = em.val("q_v")
    em.op("vector", lambda e: e.tensor_copy(sb["mxc"][:, 5:6], sb["mxi"][:, 0:1]),
          inc="q_v")
    vp(em, sb, sb["mxc"][:, 5:6], "q_v", 128, 1)
    colmax_to_scalar(em, dsc, sb, sb["mxc"][:, 5:6], 6, 3, "q_v", "row4")
    em.dma("gpsimd", dsc["ar3i"].ap()[0:1, 0:1], sb["colz"][0:1, 6:7],
           waits=["q_v"], inc="row4")
    em.dma("gpsimd", dsc["ar3i"].ap()[0:1, 1:2], sb["colz"][0:1, 6:7], inc="row4")
    em.coll("AllReduce", ALU.max, RG8, [dsc["ar3i"].ap().opt()],
            [dsc["ar3o"].ap().opt()], waits=["row4"], inc="cc")
    em.dma("gpsimd", sb["colz"][:, 6:7],
           dsc["ar3o"].ap()[0:1, 0:1].to_broadcast([128, 1]), waits=["cc"], inc="rep0")
    em.op("vector", lambda e: e.tensor_scalar(sb["colz"][:, 7:8], sb["colz"][:, 2:3],
                                              1.0 / 65536.0, None, ALU.mult),
          waits=["rep0"], inc="q_v")
    vp(em, sb, sb["colz"][:, 7:8], "q_v", 128, 1)
    em.op("vector", lambda e: e.tensor_tensor(sb["colz"][:, 8:9], sb["colz"][:, 6:7],
                                              sb["colz"][:, 7:8], ALU.mult), inc="q_v")
    vp(em, sb, sb["colz"][:, 8:9], "q_v", 128, 1)
    em.op("vector", lambda e: e.tensor_scalar(sb["colz"][:, 8:9], sb["colz"][:, 8:9],
                                              1.0 / QM15, 1e-12, ALU.mult, ALU.max),
          inc="q_v")
    vp(em, sb, sb["colz"][:, 8:9], "q_v", 128, 1)
    em.op("vector", lambda e: e.reciprocal(sb["colz"][:, 12:13], sb["colz"][:, 8:9]),
          inc="q_v")
    vp(em, sb, sb["colz"][:, 12:13], "q_v", 128, 1)
    em.op("vector", lambda e: e.tensor_tensor(sb["colz"][:, 6:7], sb["colz"][:, 7:8],
                                              sb["colz"][:, 12:13], ALU.mult),
          inc="q_v")
    vp(em, sb, sb["colz"][:, 6:7], "q_v", 128, 1)

    em.op("vector", lambda e: e.memset(sb["mxi"][:, 0:1], 0), inc="q_v")
    vp(em, sb, sb["mxi"][:, 0:1], "q_v", 128, 1)
    for tc in range(4):
        for hid in range(24):
            ut = rebuild_P(tc, hid, True)
            vst = fa(P4_CF, 512)
            em.op("vector", lambda e, vst=vst: e.tensor_scalar(
                vst, sb["i2"][:, :], sb["colz"][:, 6:7], RC, ALU.mult, ALU.add),
                inc="q_v")
            em.op("vector", lambda e, vst=vst: e.tensor_scalar(
                sb["i2"][:, :], vst, -RC, None, ALU.add), inc="q_v")
            em.op("vector", lambda e, ut=ut, vst=vst: e.tensor_scalar(
                vst, ut, sb["colz"][:, 5:6], RC, ALU.mult, ALU.add),
                waits=[("q_s", em.val("q_s"))], inc="q_v")
            em.op("vector", lambda e, vst=vst: e.tensor_scalar(
                sb["i0"][:, :], vst, -RC, None, ALU.add), inc="q_v")
            em.op("vector", lambda e: e.tensor_tensor(
                sb["i1"][:, :], sb["i2"][:, :], sb["i0"][:, :], ALU.mult), inc="q_v")
            em.op("vector", lambda e: e.tensor_reduce(
                sb["mxi"][:, 1:2], sb["i1"][:, :], AX.X, ALU.max,
                apply_absolute_value=True), inc="q_v")
            vp(em, sb, sb["mxi"][:, 1:2], "q_v", 128, 1)
            em.op("vector", lambda e: e.tensor_tensor(
                sb["mxi"][:, 0:1], sb["mxi"][:, 0:1], sb["mxi"][:, 1:2], ALU.max),
                inc="q_v")
            vp(em, sb, sb["mxi"][:, 0:1], "q_v", 128, 1)
            rb_guard[hid % 2] = em.val("q_v")
    em.op("vector", lambda e: e.tensor_copy(sb["mxc"][:, 6:7], sb["mxi"][:, 0:1]),
          inc="q_v")
    vp(em, sb, sb["mxc"][:, 6:7], "q_v", 128, 1)
    colmax_to_scalar(em, dsc, sb, sb["mxc"][:, 6:7], 9, 3, "q_v", "row4")
    em.dma("gpsimd", dsc["ar4i"].ap()[0:1, 0:1], sb["colz"][0:1, 9:10],
           waits=["q_v"], inc="row4")
    em.dma("gpsimd", dsc["ar4i"].ap()[0:1, 1:2], sb["colz"][0:1, 9:10], inc="row4")
    em.coll("AllReduce", ALU.max, RG8, [dsc["ar4i"].ap().opt()],
            [dsc["ar4o"].ap().opt()], waits=["row4"], inc="cc")
    em.dma("gpsimd", sb["colz"][:, 9:10],
           dsc["ar4o"].ap()[0:1, 0:1].to_broadcast([128, 1]), waits=["cc"], inc="rep0")
    em.op("vector", lambda e: e.tensor_tensor(sb["colz"][:, 10:11], sb["colz"][:, 8:9],
                                              sb["colz"][:, 4:5], ALU.mult),
          waits=["rep0"], inc="q_v")
    vp(em, sb, sb["colz"][:, 10:11], "q_v", 128, 1)
    em.op("vector", lambda e: e.tensor_tensor(sb["colz"][:, 11:12], sb["colz"][:, 9:10],
                                              sb["colz"][:, 10:11], ALU.mult), inc="q_v")
    vp(em, sb, sb["colz"][:, 11:12], "q_v", 128, 1)
    em.op("vector", lambda e: e.tensor_scalar(sb["colz"][:, 11:12], sb["colz"][:, 11:12],
                                              1.0 / QM15, 1e-12, ALU.mult, ALU.max),
          inc="q_v")
    vp(em, sb, sb["colz"][:, 11:12], "q_v", 128, 1)
    em.op("vector", lambda e: e.reciprocal(sb["colz"][:, 12:13], sb["colz"][:, 11:12]),
          inc="q_v")
    vp(em, sb, sb["colz"][:, 12:13], "q_v", 128, 1)
    em.op("vector", lambda e: e.tensor_tensor(sb["colz"][:, 9:10], sb["colz"][:, 10:11],
                                              sb["colz"][:, 12:13], ALU.mult),
          inc="q_v")
    vp(em, sb, sb["colz"][:, 9:10], "q_v", 128, 1)

    for hc in range(24):
        em.dma("sync", bfa(B4_WQ + hc * 1024, 1024),
               dsc["wqd"].ap()[0:C, hc * 128:(hc + 1) * 128],
               waits=["q2_wr", ("q_t", em.val("q_t"))], inc="q_ld", transpose=True)

    def s1body(e, core, then):
        g = core // 4
        b = core % 4
        for rep, rk in ((0, b), (1, b + 4)):
            ins = e.dma_start(
                out=dsc["s1loc"].ap()[rep * C:(rep + 1) * C, :],
                in_=dsc["s1ago"].ap()[rk * C:(rk + 1) * C,
                                      g * THALF:(g + 1) * THALF])
            then(ins)

    em.switch("gpsimd", s1body, waits=["cc"], inc="loc_wr", nincs=2)
    s1loc_done = em.val("loc_wr")

    for tc in range(4):
        for hid in range(24):
            ut = rebuild_P(tc, hid, True)
            vst = fa(P4_CF, 512)
            em.op("vector", lambda e, vst=vst: e.tensor_scalar(
                vst, sb["i2"][:, :], sb["colz"][:, 6:7], RC, ALU.mult, ALU.add),
                inc="q_v")
            em.op("vector", lambda e, vst=vst: e.tensor_scalar(
                sb["i2"][:, :], vst, -RC, None, ALU.add), inc="q_v")
            em.op("vector", lambda e, ut=ut, vst=vst: e.tensor_scalar(
                vst, ut, sb["colz"][:, 5:6], RC, ALU.mult, ALU.add),
                waits=[("q_s", em.val("q_s"))], inc="q_v")
            em.op("vector", lambda e, vst=vst: e.tensor_scalar(
                sb["i0"][:, :], vst, -RC, None, ALU.add), inc="q_v")
            em.op("vector", lambda e: e.tensor_tensor(
                sb["i1"][:, :], sb["i2"][:, :], sb["i0"][:, :], ALU.mult), inc="q_v")
            em.op("vector", lambda e, vst=vst: e.tensor_scalar(
                vst, sb["i1"][:, :], sb["colz"][:, 9:10], RC, ALU.mult, ALU.add),
                inc="q_v")
            em.op("vector", lambda e, vst=vst: e.tensor_scalar(
                sb["i1"][:, :], vst, -RC, None, ALU.add), inc="q_v")
            em.op("scalar", lambda e, hid=hid: e.activation(
                bfa(B4_OQ + hid * 512, 512), sb["i1"][:, :], AF.Copy,
                scale=sb["colz"][:, 11:12]),
                waits=["q_v", ("q_t", em.val("q_t"))], inc="q_s")
            rb_guard[hid % 2] = em.val("q_v")
        for d in range(8):
            pst = ps["pc"] if d % 2 == 0 else ps["pd"]
            for hid in range(24):
                em.op("tensor", lambda e, d=d, hid=hid, pst=pst: e.matmul(
                    pst[:, :], bfa(B4_WQ + hid * 1024 + d * 128, 128),
                    bfa(B4_OQ + hid * 512, 512), start=(hid == 0), stop=(hid == 23)),
                    waits=["q_s", "q_ld", ("q_v", em.val("q_v"))], inc="q_t")
            em.op("scalar", lambda e, d=d, pst=pst: e.activation(
                fa(P4_G1 + d * 512, 512), pst[:, :], AF.Silu), waits=["q_t"],
                inc="q_s")
        for rep in range(2):
            base = 36864 if rep == 0 else 43008
            for d in range(8):
                em.dma("sync", bfa(base + d * 512, 512),
                       dsc["s1loc"].ap()[rep * C + d * 128:rep * C + (d + 1) * 128,
                                         tc * 512:(tc + 1) * 512],
                       waits=[("loc_wr", s1loc_done), ("q_v", em.val("q_v"))],
                       inc="q_ld")
        for d in range(8):
            em.dma("sync", fa(P4_X1 + d * 512, 512),
                   dsc["x1agi"].ap()[d * 128:(d + 1) * 128, tc * 512:(tc + 1) * 512],
                   waits=[("q_v", em.val("q_v"))], inc="q_ld")
        for d in range(8):
            em.op("vector", lambda e, d=d: e.tensor_tensor(
                fa(P4_SC + d * 512, 512), bfa(36864 + d * 512, 512),
                bfa(43008 + d * 512, 512), ALU.add),
                waits=["q_ld"], inc="q_v")
            em.op("vector", lambda e, d=d: e.tensor_tensor(
                fa(P4_SC + d * 512, 512), fa(P4_SC + d * 512, 512),
                fa(P4_G1 + d * 512, 512), ALU.mult), waits=["q_s"], inc="q_v")
            em.op("vector", lambda e, d=d: e.tensor_tensor(
                fa(P4_XN + d * 512, 512), fa(P4_SC + d * 512, 512),
                fa(P4_X1 + d * 512, 512), ALU.add), inc="q_v")
            em.dma("gpsimd", dout.ap()[d * 128:(d + 1) * 128, tc * 512:(tc + 1) * 512],
                   fa(P4_XN + d * 512, 512), waits=["q_v"], inc="out_wr")
    em.op("gpsimd", lambda e: e.nop(), waits=["out_wr"], inc=None)


def kernel(**inputs):
    in_maps = make_in_maps(inputs)
    nc = build_program()
    res = run_bass_kernel_spmd(nc, in_maps, core_ids=list(range(8)))
    out = np.zeros((B, L, C), dtype=np.float32)
    for c in range(8):
        b, g = c % 4, c // 4
        out[b, g * THALF:(g + 1) * THALF, :] = res.results[c]["out"].T
    return out
